# version 16
# baseline (speedup 1.0000x reference)
import numpy as np
import ml_dtypes
import concourse.bass as bass
import concourse.mybir as mybir
from concourse.bass_utils import run_bass_kernel_spmd
from contextlib import ExitStack

F32 = mybir.dt.float32
BF16 = mybir.dt.bfloat16
AF = mybir.ActivationFunctionType
ALU = mybir.AluOpType
AX = mybir.AxisListType

NCORES = 8
NB = 4
T = 2048
D = 1024
NTOK = NB * T
G = 1024
NGRP = NTOK // G
E = 8
I2 = 4096
II = 2048
H = 4
C = 64
WT = 128
NWIN = T // WT
LN16 = float(np.log(16.0))
DEV_NGRP = NGRP
DEV_NE = E
DEV_NCORES = NCORES
DEV_PHASE = 5
DEV_DUMP = False
DEV_NPASS = 2
DEV_NWIN = NWIN
EPS = 1e-5


class Sem:
    __slots__ = ("h", "n")
    def __init__(self, h):
        self.h = h
        self.n = 0


class Prog:
    def __init__(self):
        self.items = []
        self.last_wait = {}

    def op(self, eng, fn, inc=None, inc2=None):
        if inc is not None:
            inc[0].n += inc[1]
        if inc2 is not None:
            inc2[0].n += inc2[1]
        def run(h, fn=fn, inc=inc, inc2=inc2):
            ins = fn(h)
            if inc is not None:
                ins = ins.then_inc(inc[0].h, inc[1])
            if inc2 is not None:
                ins.then_inc(inc2[0].h, inc2[1])
        self.items.append((eng, run))

    def wait(self, eng, sem, val):
        if val <= 0:
            return
        key = (eng, id(sem))
        if self.last_wait.get(key, -1) >= val:
            return
        self.last_wait[key] = val
        self.items.append((eng, lambda h, s=sem, v=val: h.wait_ge(s.h, v)))

    def run(self, eng, handle):
        for e, fn in self.items:
            if e == eng:
                fn(handle)

    def flush(self, nc):
        with nc.Block() as blk:
            @blk.sync
            def _(h):
                self.run("sync", h)
            @blk.scalar
            def _(h):
                self.run("scalar", h)
            @blk.vector
            def _(h):
                self.run("vector", h)
            @blk.tensor
            def _(h):
                self.run("tensor", h)
            @blk.gpsimd
            def _(h):
                self.run("gpsimd", h)


def A(t, off, dims):
    return bass.AP(t, off, dims)


class DStream:
    def __init__(self, sem_fn, name, nbuf):
        self.sems = [sem_fn(name + str(j)) for j in range(nbuf)]
        self.nbuf = nbuf
        self.count = 0

    def issue(self, P, q, fn):
        i = self.count
        self.count += 1
        P.op(q, fn, inc=(self.sems[i % self.nbuf], 16))
        return i

    def wait_done(self, P, eng, i):
        if i < 0:
            return
        P.wait(eng, self.sems[i % self.nbuf], 16 * (i // self.nbuf + 1))

    def wait_all(self, P, eng):
        for j, s in enumerate(self.sems):
            P.wait(eng, s, s.n)


def build_program(do_f1=True, do_attn=True, do_f2=True):
    nc = bass.Bass("TRN2", target_bir_lowering=False, detect_race_conditions=False)

    x_d = nc.dram_tensor("x", [NTOK, D], F32, kind="ExternalInput")
    w13_d = nc.dram_tensor("w13", [E, D, I2], F32, kind="ExternalInput")
    w2_d = nc.dram_tensor("w2", [E, II, D], F32, kind="ExternalInput")
    rwT_d = nc.dram_tensor("rwT", [D, E], F32, kind="ExternalInput")
    wq_d = nc.dram_tensor("Wq", [D, D], F32, kind="ExternalInput")
    wk_d = nc.dram_tensor("Wk", [D, D], F32, kind="ExternalInput")
    wv_d = nc.dram_tensor("Wv", [D, 2048], F32, kind="ExternalInput")
    wg_d = nc.dram_tensor("Wg", [D, 2048], F32, kind="ExternalInput")
    wo_d = nc.dram_tensor("Wo", [2048, D], F32, kind="ExternalInput")
    gk1_d = nc.dram_tensor("gk1", [D, 16], F32, kind="ExternalInput")
    gk2_d = nc.dram_tensor("gk2", [16, D], F32, kind="ExternalInput")
    gkbT_d = nc.dram_tensor("gkbT", [128, 8], F32, kind="ExternalInput")
    rmswB_d = nc.dram_tensor("rmswB", [128, 512], F32, kind="ExternalInput")
    identb_d = nc.dram_tensor("identb", [128, 128], BF16, kind="ExternalInput")
    identf_d = nc.dram_tensor("identf", [128, 128], F32, kind="ExternalInput")
    mask_d = nc.dram_tensor("maskU", [64, 64], F32, kind="ExternalInput")
    oneh_d = nc.dram_tensor("oneh", [8, 1024], BF16, kind="ExternalInput")
    ones_d = nc.dram_tensor("ones64", [128, 64], F32, kind="ExternalInput")

    x1_d = nc.dram_tensor("x1s", [NTOK, D], F32, kind="Internal")
    x2_d = nc.dram_tensor("x2s", [NTOK, D], F32, kind="Internal")
    y_d = nc.dram_tensor("y", [NTOK, D], F32, kind="ExternalOutput")
    w13i_d = nc.dram_tensor("w13i", [E * 4 * 128, 8192], BF16, kind="Internal")
    w2i_d = nc.dram_tensor("w2i", [E * 4 * 128, 4096], BF16, kind="Internal")
    wqi_d = nc.dram_tensor("wqi", [2 * 128, 4096], BF16, kind="Internal")
    wki_d = nc.dram_tensor("wki", [2 * 128, 4096], BF16, kind="Internal")
    wvi_d = nc.dram_tensor("wvi", [4 * 128, 4096], BF16, kind="Internal")
    wgi_d = nc.dram_tensor("wgi", [4 * 128, 4096], BF16, kind="Internal")
    woi_d = nc.dram_tensor("woi", [4 * 128, 4096], BF16, kind="Internal")

    f1_dst = x1_d if (do_attn or do_f2) else y_d
    at_dst = x2_d if do_f2 else y_d
    at_src = x1_d if do_f1 else x_d
    f2_src = x2_d if do_attn else (x1_d if do_f1 else x_d)

    st = ExitStack()
    sems = {}
    def sem(name):
        if name not in sems:
            sems[name] = Sem(st.enter_context(nc.semaphore(name)))
        return sems[name]

    with st:
        identb = st.enter_context(nc.sbuf_tensor("c_identb", [128, 128], BF16))
        identf = st.enter_context(nc.sbuf_tensor("c_identf", [128, 128], F32))
        maskU = st.enter_context(nc.sbuf_tensor("c_maskU", [64, 64], F32))
        oneh = st.enter_context(nc.sbuf_tensor("c_oneh", [8, 1024], BF16))
        ones64 = st.enter_context(nc.sbuf_tensor("c_ones64", [128, 64], F32))
        rwT_sb = st.enter_context(nc.sbuf_tensor("rwT_sb", [128, 64], BF16))
        gk1_sb = st.enter_context(nc.sbuf_tensor("gk1_sb", [128, 128], BF16))
        gk2_sb = st.enter_context(nc.sbuf_tensor("gk2_sb", [16, 1024], BF16))
        gkbT_sb = st.enter_context(nc.sbuf_tensor("gkbT_sb", [128, 8], F32))
        rmsw_sb = st.enter_context(nc.sbuf_tensor("rmsw_sb", [128, 512], BF16))

        s_c = sem("s_c")

        P = Prog()
        for dst, src in ((identb, identb_d), (identf, identf_d), (maskU, mask_d),
                         (oneh, oneh_d), (ones64, ones_d), (gkbT_sb, gkbT_d)):
            P.op("sync", lambda h, dst=dst, src=src: h.dma_start(out=dst[:], in_=src.ap()),
                 inc=(s_c, 16))
        s_cg = sem("s_cg")
        P.op("gpsimd", lambda h: h.dma_start(
            out=rwT_sb[:], in_=A(rwT_d, 0, [[8, 128], [1024, 8], [1, 8]])), inc=(s_cg, 16))
        P.op("gpsimd", lambda h: h.dma_start(
            out=gk1_sb[:], in_=A(gk1_d, 0, [[16, 128], [2048, 8], [1, 16]])), inc=(s_cg, 16))
        P.op("gpsimd", lambda h: h.dma_start(out=gk2_sb[:], in_=gk2_d.ap()), inc=(s_cg, 16))
        P.op("gpsimd", lambda h: h.dma_start(out=rmsw_sb[:], in_=rmswB_d.ap()), inc=(s_cg, 16))
        NC_CONST = 6
        NC_CONSTG = 4

        ds_pi = DStream(sem, "s_pi", 2)
        ds_po = DStream(sem, "s_po", 2)
        with nc.sbuf_tensor("stg0", [128, 8192], BF16) as stg0, \
             nc.sbuf_tensor("stg1", [128, 8192], BF16) as stg1:
            stgs = (stg0, stg1)
            pieces = []
            for e in range(E):
                for ib in range(4):
                    for sect in range(2):
                        src = A(w13_d, e * D * I2 + sect * 2048 + ib * 512,
                                [[I2, 128], [128 * I2, 8], [1, 512]])
                        pieces.append((src, w13i_d, (e * 4 + ib) * 128, sect * 4096, 4096))
            for e in range(E):
                for q in range(4):
                    src = A(w2_d, e * II * D + q * 256,
                            [[D, 128], [128 * D, 16], [1, 256]])
                    pieces.append((src, w2i_d, (e * 4 + q) * 128, 0, 4096))
            for fp in range(2):
                pieces.append((A(wq_d, fp * 512, [[D, 128], [128 * D, 8], [1, 512]]),
                               wqi_d, fp * 128, 0, 4096))
            for fp in range(2):
                pieces.append((A(wk_d, fp * 512, [[D, 128], [128 * D, 8], [1, 512]]),
                               wki_d, fp * 128, 0, 4096))
            for fp in range(4):
                pieces.append((A(wv_d, fp * 512, [[2048, 128], [128 * 2048, 8], [1, 512]]),
                               wvi_d, fp * 128, 0, 4096))
            for fp in range(4):
                pieces.append((A(wg_d, fp * 512, [[2048, 128], [128 * 2048, 8], [1, 512]]),
                               wgi_d, fp * 128, 0, 4096))
            for fp in range(4):
                pieces.append((A(wo_d, fp * 4 * 128 * D, [[D, 128], [128 * D, 4], [1, 1024]]),
                               woi_d, fp * 128, 0, 4096))

            for i, (src, img, row, coloff, ncols) in enumerate(pieces):
                b = stgs[i % 2]
                ds_po.wait_done(P, "gpsimd", i - 2)
                counts = [d[1] for d in src.ap[1:]]
                strides = []
                acc = 1
                for cnt in reversed(counts):
                    strides.append(acc)
                    acc *= cnt
                strides = list(reversed(strides))
                dstap = A(b, 0, [[8192, 128]] + [[s, cn] for s, cn in zip(strides, counts)])
                ii = ds_pi.issue(P, "gpsimd", lambda h, dstap=dstap, src=src:
                                 h.dma_start(out=dstap, in_=src))
                ds_pi.wait_done(P, "sync", ii)
                rowlen = img.shape[1]
                dst = A(img, row * rowlen + coloff, [[rowlen, 128], [1, ncols]])
                ds_po.issue(P, "sync", lambda h, b=b, dst=dst, ncols=ncols:
                            h.dma_start(out=dst, in_=b[:, 0:ncols]))
            n_pieces = len(pieces)
            P.flush(nc)
        P = None

        def emit_ln_tile(P, ctx, src_d, row0, col0, do_router, tile_no):
            pfx = ctx["pfx"]
            VE = ctx.get("ln_eng", "vector")
            CE = ctx.get("lncp_eng", "vector")
            xb = ctx["xb"]; sqb = ctx["sqb"]; stats = ctx["stats"]
            xlnb = ctx["xlnb"]; xlnT = ctx["xlnT"]; ptr6 = ctx["ptr6"]
            ds_x = ctx["ds_x"]; s_sq = sem(pfx + "sq")
            s_va = sem(pfx + "va"); s_rs = sem(pfx + "rs")
            s_ln = sem(pfx + "ln"); s_ptr = sem(pfx + "ptr")
            s_trc = sem(pfx + "trc"); s_red = sem(pfx + "red")
            s_sw = sem(pfx + "sw")
            par = tile_no % 2
            xbp = xb[par]; xlp = xlnb[par]
            P.wait("sync", s_ln, sem(pfx + "ln").n - 1)
            ix = ds_x.issue(P, "sync", lambda h: h.dma_start(
                out=xbp[:], in_=A(src_d, row0 * D, [[D, 128], [1, D]])))
            s_vw = sem(pfx + VE + "w")
            def vw(fn):
                P.op(VE, fn, inc=(s_vw, 1))
                P.wait(VE, s_vw, s_vw.n)
            ds_x.wait_done(P, VE, ix)
            P.wait(VE, s_red, s_red.n)
            P.op(VE, lambda h: h.tensor_mul(sqb[:], xbp[:], xbp[:]), inc=(s_sq, 1))
            ds_x.wait_done(P, "vector", ix)
            P.wait("vector", s_sq, s_sq.n)
            P.op("vector", lambda h: h.reduce_sum(stats[:, 0:1], xbp[:], axis=AX.X))
            P.op("vector", lambda h: h.reduce_sum(stats[:, 1:2], sqb[:], axis=AX.X),
                 inc=(s_red, 1))
            P.wait(VE, s_red, s_red.n)
            vw(lambda h: h.tensor_scalar(stats[:, 2:3], stats[:, 0:1],
                                         1.0 / D, None, ALU.mult))
            vw(lambda h: h.tensor_mul(stats[:, 3:4], stats[:, 2:3], stats[:, 2:3]))
            vw(lambda h: h.tensor_scalar(stats[:, 13:14], stats[:, 1:2],
                                         1.0 / D, None, ALU.mult))
            vw(lambda h: h.tensor_sub(stats[:, 4:5], stats[:, 13:14],
                                      stats[:, 3:4]))
            P.op(VE, lambda h: h.tensor_scalar(stats[:, 5:6], stats[:, 4:5],
                                               EPS, None, ALU.add), inc=(s_va, 1))
            P.wait("scalar", s_va, s_va.n)
            P.op("scalar", lambda h: h.activation(stats[:, 6:7], stats[:, 5:6],
                                                  AF.Sqrt), inc=(s_rs, 1))
            P.wait(VE, s_rs, s_rs.n)
            P.op(VE, lambda h: h.reciprocal(out=stats[:, 7:8], in_=stats[:, 6:7]))
            P.op(VE, lambda h: h.tensor_scalar(
                xlp[:], xbp[:], stats[:, 2:3], stats[:, 7:8],
                ALU.subtract, ALU.mult), inc=(s_ln, 1))
            P.wait("tensor", s_ln, s_ln.n)
            P.wait("tensor", s_trc, s_trc.n)
            for g in ctx["ptr6_guards"]:
                P.wait("tensor", g, g.n)
            for dt in range(8):
                P.op("tensor", lambda h, dt=dt: h.transpose(
                    ptr6[:, dt * 128:(dt + 1) * 128], xlp[:, dt * 128:(dt + 1) * 128],
                    identb[:]), inc=(s_ptr, 1) if dt == 7 else None)
            P.wait(CE, s_ptr, s_ptr.n)
            xlw = ctx["xlw"]
            xw = xlnT.shape[1]
            if CE == "scalar":
                P.op(CE, lambda h: h.activation(
                    A(xlnT, col0, [[xw, 128], [xlw, 8], [1, 128]]),
                    A(ptr6, 0, [[1024, 128], [128, 8], [1, 128]]), AF.Copy),
                    inc=(s_trc, 1))
            else:
                P.op(CE, lambda h: h.tensor_copy(
                    A(xlnT, col0, [[xw, 128], [xlw, 8], [1, 128]]),
                    A(ptr6, 0, [[1024, 128], [128, 8], [1, 128]])), inc=(s_trc, 1))
            if do_router:
                emit_router_tile(P, ctx, col0, tile_no)

        def emit_router_tile(P, ctx, col0, tile_no):
            pfx = ctx["pfx"]
            VE = ctx.get("ln_eng", "vector")
            CE = ctx.get("lncp_eng", "vector")
            xlnT = ctx["xlnT"]; p_rt = ctx["p_rt"]; ptr6 = ctx["ptr6"]
            stats = ctx["stats"]; rpb = ctx["rpb"]; rT = ctx["rT"]
            qv = ctx["qv"]; qv2 = ctx["qv2"]; qv3 = ctx["qv3"]
            s_trc = sem(pfx + "trc"); s_prt = sem(pfx + "prt")
            s_t2 = sem(pfx + "t2"); s_rtc = sem(pfx + "rtc")
            s_ptr = sem(pfx + "ptr"); s_nm1 = sem(pfx + "nm1")
            s_qx = sem(pfx + "qx"); s_sw = sem(pfx + "sw")
            s_red = sem(pfx + "red"); s_gx = sem(pfx + "gx")
            s_vw = sem(pfx + VE + "w")
            def vw(fn):
                P.op(VE, fn, inc=(s_vw, 1))
                P.wait(VE, s_vw, s_vw.n)
            rto = (tile_no % 2) * 8
            P.wait("tensor", s_trc, s_trc.n)
            P.wait("tensor", s_t2, s_t2.n - 1)
            for dt in range(8):
                P.op("tensor", lambda h, dt=dt, col0=col0, rto=rto: h.matmul(
                    p_rt[:, rto:rto + 8],
                    A(xlnT, dt * 1024 + col0, [[xlnT.shape[1], 128], [1, 128]]),
                    rwT_sb[:, dt * 8:(dt + 1) * 8],
                    start=(dt == 0), stop=(dt == 7)),
                    inc=(s_prt, 1) if dt == 7 else None)
            P.wait("vector", s_prt, s_prt.n)
            P.op("vector", lambda h, rto=rto: h.reduce_max(
                stats[:, 8:9], p_rt[:, rto:rto + 8], axis=AX.X,
                negate=True), inc=(s_nm1, 1))
            P.wait("scalar", s_nm1, s_nm1.n)
            P.op("scalar", lambda h, rto=rto: h.activation(
                qv[:], p_rt[:, rto:rto + 8], AF.Exp,
                bias=stats[:, 8:9]), inc=(s_qx, 1))
            P.wait("vector", s_qx, s_qx.n)
            P.op("vector", lambda h: h.reduce_sum(stats[:, 9:10], qv[:], axis=AX.X),
                 inc=(s_red, 1))
            P.wait(VE, s_qx, s_qx.n)
            vw(lambda h: h.tensor_single_scalar(qv2[:], qv[:], 1.0, ALU.is_ge))
            P.op(VE, lambda h: h.tensor_sub(qv3[:], qv[:], qv2[:]), inc=(s_gx, 1))
            P.wait("vector", s_gx, s_gx.n)
            P.op("vector", lambda h: h.reduce_max(stats[:, 10:11], qv3[:], axis=AX.X),
                 inc=(s_red, 1))
            P.wait(VE, s_red, s_red.n)
            vw(lambda h: h.tensor_scalar(stats[:, 14:15], stats[:, 9:10],
                                         1e-6, None, ALU.mult))
            vw(lambda h: h.tensor_add(stats[:, 11:12], stats[:, 14:15],
                                      stats[:, 10:11]))
            vw(lambda h: h.tensor_scalar(stats[:, 11:12], stats[:, 11:12],
                                         2.0, 2.0, ALU.mult, ALU.add))
            vw(lambda h: h.reciprocal(out=stats[:, 12:13], in_=stats[:, 11:12]))
            vw(lambda h: h.tensor_scalar(qv2[:], qv[:], stats[:, 10:11],
                                         None, ALU.is_ge))
            vw(lambda h: h.tensor_mul(qv3[:], qv[:], qv2[:]))
            P.op(VE, lambda h: h.tensor_scalar(rpb[:], qv3[:], stats[:, 12:13],
                                               None, ALU.mult), inc=(s_t2, 1))
            P.wait("tensor", s_t2, s_t2.n)
            P.wait("tensor", s_rtc, s_rtc.n)
            P.op("tensor", lambda h: h.transpose(ptr6[0:8, 0:128], rpb[:], identb[:]),
                 inc=(s_ptr, 1))
            P.wait(CE, s_ptr, s_ptr.n)
            if CE == "scalar":
                P.op(CE, lambda h, col0=col0: h.activation(
                    rT[:, col0:col0 + 128], ptr6[0:8, 0:128], AF.Copy), inc=(s_rtc, 1))
            else:
                P.op(CE, lambda h, col0=col0: h.tensor_copy(
                    rT[:, col0:col0 + 128], ptr6[0:8, 0:128]), inc=(s_rtc, 1))

        fcnt = {"c_tile": 0, "c_w13": 0, "c_w2": 0, "c_h": 0, "c_y": 0, "c_res": 0}
        fds = {}

        def ffn_stage(name, tname, src_d, dst_d, gate_sems):
            P = Prog()
            with ExitStack() as ar:
                sb = lambda n, shp, dt: ar.enter_context(nc.sbuf_tensor(tname + n, shp, dt))
                ps = lambda n, shp, dt: ar.enter_context(nc.psum_tensor(tname + n, shp, dt))
                ctx = {"pfx": name}
                ctx["xb"] = [sb("xb0", [128, 1024], F32), sb("xb1", [128, 1024], F32)]
                ctx["sqb"] = sb("sqb", [128, 1024], F32)
                ctx["stats"] = sb("stats", [128, 16], F32)
                ctx["xlnb"] = [sb("xlnb0", [128, 1024], BF16), sb("xlnb1", [128, 1024], BF16)]
                xlnTs = [sb("xlnTa", [128, 8192], BF16), sb("xlnTb", [128, 8192], BF16)]
                ctx["xlw"] = 1024
                ctx["rpb"] = sb("rpb", [128, 8], BF16)
                rTs = [sb("rTa", [8, 1024], BF16), sb("rTb", [8, 1024], BF16)]
                rb = sb("rb", [128, 1024], BF16)
                ctx["ln_eng"] = "vector"
                ctx["lncp_eng"] = "vector"
                w13b = [sb("w13b0", [128, 8192], BF16), sb("w13b1", [128, 8192], BF16)]
                w2b = [sb("w2b0", [128, 4096], BF16), sb("w2b1", [128, 4096], BF16),
                       sb("w2b2", [128, 4096], BF16)]
                sgb = sb("sgb", [128, 1024], BF16)
                tbb = sb("tbb", [128, 1024], BF16)
                h_all = sb("h_all", [128, 16384], BF16)
                yac = sb("yac", [128, 8192], F32)
                xrb = [sb("xrb0", [128, 1024], F32), sb("xrb1", [128, 1024], F32)]
                xob = [sb("xob0", [128, 1024], F32), sb("xob1", [128, 1024], F32)]
                ctx["qv"] = sb("qv", [128, 8], F32)
                ctx["qv2"] = sb("qv2", [128, 8], F32)
                ctx["qv3"] = sb("qv3", [128, 8], F32)
                pgP = ps("pgP", [128, 1024], F32)
                puP = ps("puP", [128, 1024], F32)
                py = [ps("py0", [128, 512], F32), ps("py1", [128, 512], F32)]
                ctx["ptr6"] = ps("ptr6", [128, 1024], BF16)
                ctx["p_rt"] = ps("p_rt", [128, 16], F32)

                if not fds:
                    fds["x"] = DStream(sem, name + "x", 2)
                    fds["w13"] = DStream(sem, name + "w13", 2)
                    fds["w2"] = DStream(sem, name + "w2", 3)
                    fds["xr"] = DStream(sem, name + "xr", 2)
                    fds["st"] = DStream(sem, name + "st", 2)
                ds_x = fds["x"]; ds_w13 = fds["w13"]; ds_w2 = fds["w2"]
                ds_xr = fds["xr"]; ds_st = fds["st"]
                ctx["ds_x"] = ds_x
                s_pg = sem(name + "pg"); s_pu = sem(name + "pu")
                s_sg = sem(name + "sg"); s_puf = sem(name + "puf")
                s_h = sem(name + "h"); s_rbs = sem(name + "rbs"); s_prb = sem(name + "prb")
                s_py = sem(name + "py"); s_yev = sem(name + "yev")
                s_ptres = sem(name + "ptres")
                s_resc = sem(name + "resc")
                s_rtc = sem(name + "rtc"); s_t2 = sem(name + "t2")
                s_trc = sem(name + "trc")
                ctx["ptr6_guards"] = [s_rtc]

                ds_po.wait_all(P, "sync")
                for gs, gv in gate_sems:
                    for eng in ("sync", "tensor", "vector", "scalar"):
                        P.wait(eng, gs, gv)
                for eng in ("tensor", "vector", "scalar"):
                    P.wait(eng, s_c, 16 * NC_CONST)
                    P.wait(eng, sem("s_cg"), 16 * NC_CONSTG)

                h_marks = [0]
                rb_marks = [0]

                def emit_ln_group(g):
                    gm = max(0, g - 1)
                    P.wait("scalar", sem(name + "pu"), h_marks[min(gm, len(h_marks) - 1)])
                    P.wait("scalar", sem(name + "prb"), rb_marks[min(gm, len(rb_marks) - 1)])
                    ctx["xlnT"] = xlnTs[g % 2]
                    ctx["rT"] = rTs[g % 2]
                    for t in range(8):
                        emit_ln_tile(P, ctx, src_d, g * G + t * 128, t * 128,
                                     DEV_PHASE >= 2, fcnt["c_tile"])
                        fcnt["c_tile"] += 1

                emit_ln_group(0)
                for g in range(DEV_NGRP):
                    xlnT = xlnTs[g % 2]
                    rT = rTs[g % 2]
                    for e in (range(DEV_NE) if DEV_PHASE >= 3 else []):
                        P.wait("tensor", s_rtc, s_rtc.n)
                        P.wait("tensor", s_yev, s_yev.n)
                        P.wait("tensor", s_resc, fcnt["c_res"])
                        for cch in range(2):
                            P.op("tensor", lambda h, cch=cch, e=e, rT=rT: h.matmul(
                                py[cch][:, 0:512], oneh[:, e * 128:(e + 1) * 128],
                                rT[:, cch * 512:(cch + 1) * 512],
                                start=True, stop=True), inc=(s_prb, 1) if cch == 1 else None)
                        P.wait("vector", s_prb, s_prb.n)
                        P.op("vector", lambda h: h.tensor_copy(rb[:, 0:512], py[0][:, 0:512]))
                        P.op("vector", lambda h: h.tensor_copy(rb[:, 512:1024], py[1][:, 0:512]),
                             inc=(s_rbs, 1))
                        P.wait("vector", s_py, s_py.n)
                        for ib in range(4):
                            wb = fcnt['c_w13']; fcnt['c_w13'] += 1
                            wslot = w13b[wb % 2]
                            P.wait("sync", s_pu, 4 * (wb - 1))
                            ds_w13.issue(P, "sync", lambda h, wslot=wslot, e=e, ib=ib:
                                         h.dma_start(out=wslot[:],
                                                     in_=A(w13i_d, (e * 4 + ib) * 128 * 8192,
                                                           [[8192, 128], [1, 8192]])))
                            for ch in range(2):
                                for ipair in range(2):
                                    nh = fcnt['c_h']; fcnt['c_h'] += 1
                                    ita, itb = 2 * ipair, 2 * ipair + 1
                                    ds_w13.wait_done(P, "tensor", wb)
                                    P.wait("tensor", s_sg, nh - 1)
                                    P.wait("tensor", s_puf, nh - 1)
                                    if e == 0:
                                        P.wait("tensor", s_t2, s_t2.n)
                                    for half, it in ((0, ita), (1, itb)):
                                        for dt in range(8):
                                            P.op("tensor", lambda h, dt=dt, it=it, half=half, ch=ch, wslot=wslot, xlnT=xlnT:
                                                 h.matmul(pgP[:, half * 512:(half + 1) * 512],
                                                          wslot[:, dt * 512 + it * 128:
                                                                dt * 512 + (it + 1) * 128],
                                                          xlnT[:, dt * 1024 + ch * 512:
                                                               dt * 1024 + (ch + 1) * 512],
                                                          start=(dt == 0), stop=(dt == 7)),
                                                 inc=(s_pg, 1) if (dt == 7 and half == 1) else None)
                                    for half, it in ((0, ita), (1, itb)):
                                        for dt in range(8):
                                            P.op("tensor", lambda h, dt=dt, it=it, half=half, ch=ch, wslot=wslot, xlnT=xlnT:
                                                 h.matmul(puP[:, half * 512:(half + 1) * 512],
                                                          wslot[:, 4096 + dt * 512 + it * 128:
                                                                4096 + dt * 512 + (it + 1) * 128],
                                                          xlnT[:, dt * 1024 + ch * 512:
                                                               dt * 1024 + (ch + 1) * 512],
                                                          start=(dt == 0), stop=(dt == 7)),
                                                 inc=(s_pu, 1) if (dt == 7 and half == 1) else None)
                                    P.wait("scalar", s_pg, nh + 1)
                                    P.wait("scalar", s_puf, nh - 1)
                                    P.op("scalar", lambda h: h.activation(
                                        sgb[:], pgP[:], AF.Silu), inc=(s_sg, 1))
                                    P.wait("vector", s_pu, nh + 1)
                                    P.wait("vector", s_sg, nh + 1)
                                    P.op("vector", lambda h: h.tensor_mul(
                                        tbb[:], puP[:], sgb[:]), inc=(s_puf, 1))
                                    P.wait("vector", s_rbs, s_rbs.n)
                                    iga = (ib * 4 + ita) * 1024 + ch * 512
                                    igb = (ib * 4 + itb) * 1024 + ch * 512
                                    P.op("vector", lambda h, iga=iga, ch=ch: h.tensor_mul(
                                        A(h_all, iga, [[16384, 128], [1024, 2], [1, 512]]),
                                        tbb[:],
                                        A(rb, ch * 512, [[1024, 128], [0, 2], [1, 512]])),
                                        inc=(s_h, 1))
                            wq_i = fcnt['c_w2']; fcnt['c_w2'] += 1
                            if wq_i >= 3:
                                be = (wq_i - 3) // 4
                                qq = (wq_i - 3) % 4
                                P.wait("sync", s_py, be * 16 + 4 * qq + 4)
                            ds_w2.issue(P, "sync", lambda h, wq_i=wq_i, e=e, ib=ib:
                                        h.dma_start(out=w2b[wq_i % 3][:],
                                                    in_=A(w2i_d, (e * 4 + ib) * 128 * 4096,
                                                          [[4096, 128], [1, 4096]])))
                        if DEV_PHASE < 4:
                            continue
                        P.wait("tensor", s_h, s_h.n)
                        wq_first = fcnt["c_w2"] - 4
                        for ft in range(8):
                            for ch in range(2):
                                ny = fcnt['c_y']; fcnt['c_y'] += 1
                                bank = ny % 2
                                q = ft // 2
                                ds_w2.wait_done(P, "tensor", wq_first + q)
                                P.wait("tensor", s_yev, ny - 1)
                                for it in range(16):
                                    P.op("tensor", lambda h, it=it, ft=ft, ch=ch, bank=bank, q=q, wq_first=wq_first:
                                         h.matmul(py[bank][:],
                                                  w2b[(wq_first + q) % 3][:, it * 256 + (ft % 2) * 128:
                                                                          it * 256 + (ft % 2) * 128 + 128],
                                                  A(h_all, it * 1024 + ch * 512,
                                                    [[16384, 128], [1, 512]]),
                                                  start=(it == 0), stop=(it == 15)),
                                         inc=(s_py, 1) if it == 15 else None)
                                P.wait("vector", s_py, ny + 1)
                                dsl = A(yac, ft * 1024 + ch * 512, [[8192, 128], [1, 512]])
                                if e == 0:
                                    P.op("vector", lambda h, dsl=dsl, bank=bank:
                                         h.tensor_copy(dsl, py[bank][:]), inc=(s_yev, 1))
                                else:
                                    P.op("vector", lambda h, dsl=dsl, bank=bank:
                                         h.tensor_add(dsl, dsl, py[bank][:]), inc=(s_yev, 1))
                        if e == 5 and g + 1 < DEV_NGRP:
                            emit_ln_group(g + 1)
                    if DEV_PHASE < 3 and g + 1 < DEV_NGRP:
                        emit_ln_group(g + 1)
                    h_marks.append(sem(name + "pu").n)
                    rb_marks.append(sem(name + "prb").n)
                    P.wait("tensor", s_yev, s_yev.n)
                    for t in (range(8) if DEV_PHASE >= 5 else []):
                        nr = fcnt['c_res']; fcnt['c_res'] += 1
                        par = nr % 2
                        P.wait("sync", s_resc, nr - 1)
                        ixr = ds_xr.issue(P, "sync", lambda h, g=g, t=t, par=par:
                                          h.dma_start(out=xrb[par][:],
                                                      in_=A(src_d, (g * G + t * 128) * D,
                                                            [[D, 128], [1, D]])))
                        P.wait("tensor", s_resc, nr)
                        for ft in range(8):
                            P.op("tensor", lambda h, ft=ft, t=t: h.transpose(
                                py[ft // 4][:, (ft % 4) * 128:(ft % 4 + 1) * 128],
                                A(yac, ft * 1024 + t * 128, [[8192, 128], [1, 128]]),
                                identf[:]), inc=(s_ptres, 1) if ft == 7 else None)
                        P.wait("vector", s_ptres, s_ptres.n)
                        ds_xr.wait_done(P, "vector", ixr)
                        ds_st.wait_done(P, "vector", nr - 2)
                        P.op("vector", lambda h, par=par: h.tensor_add(
                            xob[par][:, 0:512], py[0][:, 0:512], xrb[par][:, 0:512]))
                        P.op("vector", lambda h, par=par: h.tensor_add(
                            xob[par][:, 512:1024], py[1][:, 0:512], xrb[par][:, 512:1024]),
                            inc=(s_resc, 1))
                        P.wait("sync", s_resc, nr + 1)
                        ds_st.issue(P, "sync", lambda h, g=g, t=t, par=par:
                                    h.dma_start(out=A(dst_d, (g * G + t * 128) * D,
                                                      [[D, 128], [1, D]]),
                                                in_=xob[par][:]))
                    P.wait("vector", s_ptres, s_ptres.n)
                if DEV_DUMP:
                    ds_dbg = DStream(sem, name + "dbg", 4)
                    P.wait("gpsimd", sem(name + "yev"), sem(name + "yev").n)
                    P.wait("gpsimd", sem(name + "trc"), sem(name + "trc").n)
                    P.wait("gpsimd", sem(name + "rtc"), sem(name + "rtc").n)
                    P.wait("gpsimd", sem(name + "h"), sem(name + "h").n)
                    ds_dbg.issue(P, "gpsimd", lambda h: h.dma_start(
                        out=A(dst_d, 0, [[D, 128], [1, 1024]]), in_=ctx["xlnT"][:, 0:1024]))
                    ds_dbg.issue(P, "gpsimd", lambda h: h.dma_start(
                        out=A(dst_d, 128 * D, [[D, 8], [1, 1024]]), in_=ctx["rT"][:, 0:1024]))
                    ds_dbg.issue(P, "gpsimd", lambda h: h.dma_start(
                        out=A(dst_d, 200 * D, [[D, 128], [1, 1024]]), in_=yac[:, 0:1024]))
                    ds_dbg.issue(P, "gpsimd", lambda h: h.dma_start(
                        out=A(dst_d, 400 * D, [[D, 128], [1, 1024]]), in_=h_all[:, 0:1024]))
                    ds_st.sems.extend(ds_dbg.sems)
                P.flush(nc)
            return ds_st

        def stream_gates(ds):
            return [(s, s.n) for s in ds.sems]

        gates = []
        if do_f1:
            f1_st = ffn_stage("f", "f1", x_d, f1_dst, gates)
            gates = stream_gates(f1_st)

        if do_attn:
            P = Prog()
            with ExitStack() as ar:
                sb = lambda n, shp, dt: ar.enter_context(nc.sbuf_tensor("a" + n, shp, dt))
                ps = lambda n, shp, dt: ar.enter_context(nc.psum_tensor("a" + n, shp, dt))
                ctx = {"pfx": "a"}
                ctx["xb"] = [sb("xb0", [128, 1024], F32), sb("xb1", [128, 1024], F32)]
                ctx["sqb"] = sb("sqb", [128, 1024], F32)
                ctx["stats"] = sb("stats", [128, 16], F32)
                ctx["xlnb"] = [sb("xlnb0", [128, 1024], BF16), sb("xlnb1", [128, 1024], BF16)]
                xlnTs_a = [sb("xlnTa", [128, 2048], BF16), sb("xlnTb", [128, 2048], BF16)]
                ctx["xlw"] = 256
                x1b = [sb("x1b0", [128, 1024], F32), sb("x1b1", [128, 1024], F32)]
                qT = sb("qT", [128, 2048], BF16)
                kT = sb("kT", [128, 2048], BF16)
                gT = sb("gT", [128, 2048], F32)
                v_sb = [sb("v0", [64, 4096], BF16), sb("v1", [64, 4096], BF16)]
                wgs = [sb("wg0", [128, 2048], BF16), sb("wg1", [128, 2048], BF16)]
                o_raw = [sb("or0", [128, 2048], BF16), sb("or1", [128, 2048], BF16)]
                wpb = [sb("wpb0", [128, 4096], BF16), sb("wpb1", [128, 4096], BF16),
                       sb("wpb2", [128, 4096], BF16)]
                S_all = sb("S", [128, 8192], F32)
                NPB = 8
                NPS = 8
                B_sb = [sb(f"B{i}", [128, 128], F32) for i in range(NPB)]
                scl = [sb(f"scl{i}", [128, 8], F32) for i in range(NPB)]
                Eq_sb = [sb(f"Eq{i}", [128, 128], BF16) for i in range(NPB)]
                Ek_sb = [sb(f"Ek{i}", [128, 128], BF16) for i in range(NPB)]
                qs_sb = [sb(f"qs{i}", [128, 128], BF16) for i in range(NPB)]
                ks_sb = [sb(f"ks{i}", [128, 128], BF16) for i in range(NPB)]
                kdT_sb = [sb(f"kdT{i}", [128, 128], BF16) for i in range(NPB)]
                kd_sb = [sb(f"kd{i}", [64, 256], BF16) for i in range(NPB)]
                ATm = [sb(f"ATm{i}", [64, 64], BF16) for i in range(NPB)]
                Sbf = [sb(f"Sbf{i}", [128, 1024], BF16) for i in range(NPS)]
                sq_scr = sb("sqscr", [128, 1024], F32)
                mh = sb("mh", [128, 16], F32)
                tmpb = sb("tmpb", [16, 256], BF16)
                sglb = [sb("sglb0", [128, 256], F32), sb("sglb1", [128, 256], F32)]
                sglb2 = [sb("sglb2a", [128, 256], F32), sb("sglb2b", [128, 256], F32)]
                oT_sb = sb("oT", [128, 2048], BF16)
                ofg = [sb("ofg0", [128, 2048], BF16), sb("ofg1", [128, 2048], BF16)]
                xob = [sb("xob0", [128, 1024], F32), sb("xob1", [128, 1024], F32)]
                ppj = [ps("ppj0", [128, 512], F32), ps("ppj1", [128, 512], F32)]
                pS = [ps("pS0", [128, 512], F32), ps("pS1", [128, 512], F32)]
                pO = [ps("pO0", [64, 512], F32), ps("pO1", [64, 512], F32)]
                ptr6 = ps("ptr6", [128, 1024], BF16)
                pout = ps("pout", [128, 512], F32)
                ctx["ptr6"] = ptr6
                ctx["pg0"] = None

                ds_x = DStream(sem, "a_x", 2)
                ctx["ds_x"] = ds_x
                ds_wp = DStream(sem, "a_wp", 3)
                ds_ax = DStream(sem, "a_axd", 2)
                ds_ast = DStream(sem, "a_std", 2)
                s_awo = sem("a_wo")
                s_pj = sem("a_pj"); s_pjf = sem("a_pjf")
                s_v1 = sem("a_v1"); s_s1 = sem("a_s1"); s_v2 = sem("a_v2")
                s_pkd = sem("a_pkd"); s_kdc = sem("a_kdc")
                s_pA = sem("a_pA"); s_atm = sem("a_atm")
                s_pO = sem("a_pO"); s_orms = sem("a_orms")
                s_pS = sem("a_pS"); s_Sup = sem("a_Sup")
                s_gl = sem("a_gl"); s_glx = sem("a_glx"); s_ep = sem("a_ep")
                s_eps = sem("a_eps"); s_epv = sem("a_epv")
                s_pot = sem("a_pot"); s_otc = sem("a_otc")
                s_pout = sem("a_pout"); s_xo = sem("a_xo")
                s_trc = sem("atrc")
                s_avw = sem("a_vw"); s_asw = sem("a_sw")
                ctx["ptr6_guards"] = [s_kdc, s_otc]

                ds_po.wait_all(P, "sync")
                ds_po.wait_all(P, "gpsimd")
                for gs, gv in gates:
                    for eng in ("sync", "tensor", "vector", "scalar"):
                        P.wait(eng, gs, gv)
                for eng in ("tensor", "vector", "scalar"):
                    P.wait(eng, s_c, 16 * NC_CONST)
                    P.wait(eng, sem("s_cg"), 16 * NC_CONSTG)

                c_pp = [0]
                c_pj = [0]
                a_tile = [0]
                n_cs = [0]
                e_tile = [0]
                piece_free = []

                def stream_piece(img, row):
                    i = c_pp[0]; c_pp[0] += 1
                    slot = wpb[i % 3]
                    if i >= 3:
                        fs, fv = piece_free[i - 3]
                        P.wait("gpsimd", fs, fv)
                    ds_wp.issue(P, "gpsimd", lambda h, slot=slot, img=img, row=row:
                                h.dma_start(out=slot[:],
                                            in_=A(img, row * 4096, [[4096, 128], [1, 4096]])))
                    return slot, i

                def piece_done():
                    piece_free.append((s_pjf, c_pj[0]))

                pjbanks = [ppj[0], ppj[1], pS[0], pS[1]]
                def next_pj():
                    u = c_pj[0]; c_pj[0] += 1
                    P.wait("tensor", s_pjf, u - 3)
                    return pjbanks[u % 4]

                def emit_attn_ln(p, w):
                    ctx["xlnT"] = xlnTs_a[w % 2]
                    for bi in range(2):
                        b = 2 * p + bi
                        P.wait("vector", s_pA, s_pA.n)
                        emit_ln_tile(P, ctx, at_src, b * T + w * WT, bi * 128,
                                     False, a_tile[0])
                        a_tile[0] += 1

                for p in range(DEV_NPASS):
                    P.wait("vector", s_Sup, s_Sup.n)
                    P.op("vector", lambda h: h.memzero(S_all[:]))
                    emit_attn_ln(p, 0)
                    for w in range(DEV_NWIN):
                        xlnT = xlnTs_a[w % 2]
                        P.wait("tensor", s_atm, s_atm.n)
                        P.wait("tensor", s_orms, s_orms.n)
                        P.wait("tensor", s_Sup, s_Sup.n)
                        P.wait("tensor", s_xo, s_xo.n)
                        P.wait("vector", s_eps, s_eps.n)
                        P.wait("scalar", s_ep, s_ep.n)
                        P.wait("tensor", s_trc, s_trc.n)
                        for (img, dstT) in ((wqi_d, qT), (wki_d, kT)):
                            for fp in range(2):
                                slot, wn = stream_piece(img, fp * 128)
                                ds_wp.wait_done(P, "tensor", wn)
                                for fl in range(4):
                                    ft = fp * 4 + fl
                                    pb = next_pj()
                                    for dt in range(8):
                                        P.op("tensor", lambda h, dt=dt, fl=fl, slot=slot, pb=pb, xlnT=xlnT:
                                             h.matmul(pb[:, 0:256],
                                                      slot[:, dt * 512 + fl * 128:
                                                           dt * 512 + (fl + 1) * 128],
                                                      xlnT[:, dt * 256:(dt + 1) * 256],
                                                      start=(dt == 0), stop=(dt == 7)),
                                             inc=(s_pj, 1) if dt == 7 else None)
                                    if ft % 2 == 0:
                                        P.wait("vector", s_pj, s_pj.n)
                                        P.op("vector", lambda h, ft=ft, dstT=dstT, pb=pb:
                                             h.tensor_copy(dstT[:, ft * 256:(ft + 1) * 256],
                                                           pb[:, 0:256]), inc=(s_pjf, 1))
                                    else:
                                        P.wait("scalar", s_pj, s_pj.n)
                                        P.op("scalar", lambda h, ft=ft, dstT=dstT, pb=pb:
                                             h.activation(dstT[:, ft * 256:(ft + 1) * 256],
                                                          pb[:, 0:256], AF.Copy),
                                             inc=(s_pjf, 1))
                                piece_done()
                        for (img, dsts, act) in ((wvi_d, v_sb, False), (wgi_d, wgs, True)):
                            for fp in range(4):
                                slot, wn = stream_piece(img, fp * 128)
                                ds_wp.wait_done(P, "tensor", wn)
                                for bi in range(2):
                                    pb = next_pj()
                                    for dt in range(8):
                                        P.op("tensor", lambda h, dt=dt, bi=bi, slot=slot, pb=pb, xlnT=xlnT:
                                             h.matmul(pb[:],
                                                      xlnT[:, dt * 256 + bi * 128:
                                                           dt * 256 + bi * 128 + 128],
                                                      slot[:, dt * 512:(dt + 1) * 512],
                                                      start=(dt == 0), stop=(dt == 7)),
                                             inc=(s_pj, 1) if dt == 7 else None)
                                    if act:
                                        P.wait("scalar", s_pj, s_pj.n)
                                        P.op("scalar", lambda h, bi=bi, fp=fp, dsts=dsts, pb=pb:
                                             h.activation(dsts[bi][:, fp * 512:(fp + 1) * 512],
                                                          pb[:], AF.Silu), inc=(s_pjf, 1))
                                    else:
                                        P.wait("vector", s_pj, s_pj.n)
                                        P.op("vector", lambda h, bi=bi, fp=fp, dsts=dsts, pb=pb:
                                             h.tensor_copy(dsts[bi][0:64, fp * 512:(fp + 1) * 512],
                                                           pb[0:64, :]))
                                        P.op("vector", lambda h, bi=bi, fp=fp, dsts=dsts, pb=pb:
                                             h.tensor_copy(dsts[bi][0:64, 2048 + fp * 512:
                                                                    2048 + (fp + 1) * 512],
                                                           pb[64:128, :]), inc=(s_pjf, 1))
                                piece_done()
                        pb = next_pj()
                        for dt in range(8):
                            P.op("tensor", lambda h, dt=dt, pb=pb, xlnT=xlnT: h.matmul(
                                pb[0:16, 0:256], gk1_sb[:, dt * 16:(dt + 1) * 16],
                                xlnT[:, dt * 256:(dt + 1) * 256],
                                start=(dt == 0), stop=(dt == 7)),
                                inc=(s_pj, 1) if dt == 7 else None)
                        P.wait("vector", s_pj, s_pj.n)
                        P.op("vector", lambda h, pb=pb: h.tensor_copy(tmpb[:], pb[0:16, 0:256]),
                             inc=(s_pjf, 1))
                        P.wait("tensor", s_pjf, s_pjf.n)
                        for ft in range(8):
                            pb = next_pj()
                            P.op("tensor", lambda h, ft=ft, pb=pb: h.matmul(
                                pb[:, 0:256], gk2_sb[:, ft * 128:(ft + 1) * 128], tmpb[:],
                                start=True, stop=True), inc=(s_pj, 1))
                            P.wait("scalar", s_pj, s_pj.n)
                            P.wait("scalar", s_gl, s_gl.n - 1)
                            P.op("scalar", lambda h, ft=ft, pb=pb: h.activation(
                                sglb[ft % 2][:], pb[:, 0:256], AF.Sigmoid,
                                bias=gkbT_sb[:, ft:ft + 1]), inc=(s_pjf, 1))
                            P.wait("scalar", s_glx, s_glx.n - 1)
                            P.op("scalar", lambda h, ft=ft: h.activation(
                                sglb2[ft % 2][:], sglb[ft % 2][:], AF.Ln), inc=(s_gl, 1))
                            P.wait("vector", s_gl, s_gl.n)
                            P.op("vector", lambda h, ft=ft: h.tensor_scalar(
                                gT[:, ft * 256:(ft + 1) * 256], sglb2[ft % 2][:], 1.0 / 16.0,
                                None, ALU.mult), inc=(s_glx, 1))
                        P.wait("tensor", s_pjf, s_pjf.n)
                        steps = [(cc, bi, hh) for cc in range(WT // C)
                                 for bi in range(2) for hh in range(H)]
                        base_n = n_cs[0]

                        def st_va(si):
                            cc, bi, hh = steps[si]
                            n = base_n + si
                            par = n % NPB
                            tok0 = bi * 128 + cc * C
                            Bp = B_sb[par]
                            P.wait("vector", s_s1, n - (NPB - 1))
                            for fl in range(2):
                                off = (2 * hh + fl) * 256 + tok0
                                P.op("vector", lambda h, fl=fl, off=off, Bp=Bp:
                                     h.tensor_tensor_scan(
                                         Bp[:, fl * 64:fl * 64 + 64], ones64[:],
                                         gT[:, off:off + C], 0.0, ALU.mult, ALU.add),
                                     inc=(s_v1, 1) if fl == 1 else None)

                        def st_sc(si):
                            n = base_n + si
                            par = n % NPB
                            Bp = B_sb[par]; sc = scl[par]
                            P.wait("scalar", s_v1, n + 1)
                            P.wait("scalar", s_Sup, n - (NPB - 1))
                            P.wait("scalar", s_v2, n - (NPB - 1))
                            P.op("scalar", lambda h, Bp=Bp, sc=sc: h.activation(
                                sc[:, 0:2], A(Bp, 31, [[128, 128], [64, 2], [1, 1]]),
                                AF.Copy, scale=-1.0, bias=-LN16))
                            P.op("scalar", lambda h, Bp=Bp, sc=sc: h.activation(
                                sc[:, 2:4], A(Bp, 31, [[128, 128], [64, 2], [1, 1]]),
                                AF.Copy, scale=-1.0), inc=(s_asw, 1))
                            P.wait("scalar", s_asw, s_asw.n)
                            P.op("scalar", lambda h, Bp=Bp, sc=sc: h.activation(
                                sc[:, 4:6], A(Bp, 63, [[128, 128], [64, 2], [1, 1]]),
                                AF.Exp))
                            P.op("scalar", lambda h, Bp=Bp, sc=sc: h.activation(
                                sc[:, 6:8], A(Bp, 31, [[128, 128], [64, 2], [1, 1]]),
                                AF.Exp))
                            for fl in range(2):
                                P.op("scalar", lambda h, fl=fl, Bp=Bp, sc=sc, par=par:
                                     h.activation(Eq_sb[par][:, fl * 64:fl * 64 + 64],
                                                  Bp[:, fl * 64:fl * 64 + 64], AF.Exp,
                                                  bias=sc[:, fl:fl + 1]))
                                P.op("scalar", lambda h, fl=fl, Bp=Bp, par=par:
                                     h.activation(Ek_sb[par][:, fl * 64:fl * 64 + 64],
                                                  Bp[:, fl * 64:fl * 64 + 64], AF.Exp,
                                                  bias=Bp[:, fl * 64 + 31:fl * 64 + 32],
                                                  scale=-1.0))
                            P.wait("scalar", s_asw, s_asw.n)
                            P.op("scalar", lambda h, Bp=Bp, sc=sc: h.activation(
                                sc[:, 0:2], A(Bp, 63, [[128, 128], [64, 2], [1, 1]]),
                                AF.Exp, bias=sc[:, 2:3]), inc=(s_asw, 1))
                            P.op("scalar", lambda h, Bp=Bp, sc=sc: h.activation(
                                sc[:, 1:2], Bp[:, 127:128],
                                AF.Exp, bias=sc[:, 3:4]), inc=(s_s1, 1))

                        def st_vb(si):
                            cc, bi, hh = steps[si]
                            n = base_n + si
                            par = n % NPB
                            parS = n % NPS
                            bh = bi * H + hh
                            tok0 = bi * 128 + cc * C
                            sc = scl[par]
                            P.wait("vector", s_s1, n + 1)
                            P.wait("vector", s_pO, n - (NPS - 1))
                            P.wait("vector", s_pS, n - (NPB - 1))
                            qoff = 2 * hh * 256 + tok0
                            P.op("vector", lambda h, qoff=qoff, par=par: h.tensor_mul(
                                A(qs_sb[par], 0, [[128, 128], [64, 2], [1, 64]]),
                                A(qT, qoff, [[2048, 128], [256, 2], [1, 64]]),
                                A(Eq_sb[par], 0, [[128, 128], [64, 2], [1, 64]])))
                            P.op("vector", lambda h, qoff=qoff, par=par: h.tensor_mul(
                                A(ks_sb[par], 0, [[128, 128], [64, 2], [1, 64]]),
                                A(kT, qoff, [[2048, 128], [256, 2], [1, 64]]),
                                A(Ek_sb[par], 0, [[128, 128], [64, 2], [1, 64]])),
                                inc=(s_avw, 1))
                            P.wait("vector", s_avw, s_avw.n)
                            for fl in range(2):
                                P.op("vector", lambda h, fl=fl, par=par, sc=sc:
                                     h.tensor_scalar(kdT_sb[par][:, fl * 64:fl * 64 + 64],
                                                     ks_sb[par][:, fl * 64:fl * 64 + 64],
                                                     sc[:, fl:fl + 1], None, ALU.mult))
                                P.op("vector", lambda h, fl=fl, parS=parS, bh=bh, sc=sc:
                                     h.tensor_scalar(
                                         Sbf[parS][:, fl * 512:(fl + 1) * 512],
                                         S_all[:, bh * 1024 + fl * 512:
                                               bh * 1024 + (fl + 1) * 512],
                                         sc[:, 6 + fl:7 + fl], None, ALU.mult),
                                     inc=(s_v2, 1) if fl == 1 else None)

                        def st_pe1(si):
                            n = base_n + si
                            par = n % NPB
                            kdo = (n % 2) * 256
                            P.wait("tensor", s_v2, n + 1)
                            P.wait("tensor", s_kdc, n)
                            P.wait("tensor", s_trc, s_trc.n)
                            P.wait("tensor", s_otc, s_otc.n)
                            for fl in range(2):
                                P.op("tensor", lambda h, fl=fl, par=par, kdo=kdo:
                                     h.transpose(
                                         ptr6[0:64, kdo + fl * 128:kdo + (fl + 1) * 128],
                                         kdT_sb[par][:, fl * 64:fl * 64 + 64],
                                         identb[:]), inc=(s_pkd, 1) if fl == 1 else None)
                            P.wait("scalar", s_pkd, s_pkd.n)
                            P.wait("scalar", s_pS, n - (NPB - 1))
                            P.op("scalar", lambda h, par=par, kdo=kdo: h.activation(
                                kd_sb[par][:], ptr6[0:64, kdo:kdo + 256], AF.Copy),
                                inc=(s_kdc, 1))
                            P.wait("tensor", s_atm, n)
                            for fl in range(2):
                                P.op("tensor", lambda h, fl=fl, par=par: h.matmul(
                                    pout[0:64, 0:64],
                                    ks_sb[par][:, fl * 64:fl * 64 + 64],
                                    qs_sb[par][:, fl * 64:fl * 64 + 64],
                                    start=(fl == 0), stop=(fl == 1)),
                                    inc=(s_pA, 1) if fl == 1 else None)
                            P.wait("vector", s_pA, s_pA.n)
                            P.wait("vector", s_pO, n - (NPB - 1))
                            P.op("vector", lambda h, par=par: h.tensor_mul(
                                ATm[par][:], pout[0:64, 0:64], maskU[:]),
                                inc=(s_atm, 1))

                        def st_pe2(si):
                            cc, bi, hh = steps[si]
                            n = base_n + si
                            par = n % NPB
                            parS = n % NPS
                            par2 = n % 2
                            bh = bi * H + hh
                            sc = scl[par]
                            pSp = pS if par2 == 0 else ppj
                            P.wait("tensor", s_atm, n + 1)
                            P.wait("tensor", s_orms, n - 1)
                            P.op("tensor", lambda h, par=par, par2=par2, bi=bi, cc=cc, hh=hh:
                                 h.matmul(pO[par2][:], ATm[par][:],
                                          v_sb[bi][0:64, cc * 2048 + hh * 512:
                                                   cc * 2048 + (hh + 1) * 512],
                                          start=True, stop=False))
                            for fl in range(2):
                                P.op("tensor", lambda h, fl=fl, par=par, parS=parS, par2=par2:
                                     h.matmul(pO[par2][:],
                                              qs_sb[par][:, fl * 64:fl * 64 + 64],
                                              Sbf[parS][:, fl * 512:(fl + 1) * 512],
                                              start=False, stop=(fl == 1)),
                                     inc=(s_pO, 1) if fl == 1 else None)
                            P.wait("tensor", s_kdc, n + 1)
                            P.wait("tensor", s_Sup, n - 1)
                            for fl in range(2):
                                P.op("tensor", lambda h, fl=fl, par=par, pSp=pSp, bi=bi, cc=cc, hh=hh:
                                     h.matmul(pSp[fl][:],
                                              kd_sb[par][:, fl * 128:(fl + 1) * 128],
                                              v_sb[bi][0:64, cc * 2048 + hh * 512:
                                                       cc * 2048 + (hh + 1) * 512],
                                              start=True, stop=True),
                                     inc=(s_pS, 1) if fl == 1 else None)
                            P.wait("scalar", s_pO, n + 1)
                            P.op("scalar", lambda h, par2=par2, bi=bi, cc=cc, hh=hh:
                                 h.activation(
                                     o_raw[bi][cc * C:(cc + 1) * C,
                                               hh * 512:(hh + 1) * 512],
                                     pO[par2][:], AF.Copy), inc=(s_orms, 1))
                            P.wait("vector", s_pS, n + 1)
                            for fl in range(2):
                                P.op("vector", lambda h, fl=fl, par=par, pSp=pSp, bh=bh, sc=sc:
                                     h.scalar_tensor_tensor(
                                         S_all[:, bh * 1024 + fl * 512:
                                               bh * 1024 + (fl + 1) * 512],
                                         S_all[:, bh * 1024 + fl * 512:
                                               bh * 1024 + (fl + 1) * 512],
                                         sc[:, 4 + fl:5 + fl],
                                         pSp[fl][:], ALU.mult, ALU.add),
                                     inc=(s_Sup, 1) if fl == 1 else None)

                        HB = 8
                        for hb0 in range(0, len(steps), HB):
                            batch = range(hb0, min(hb0 + HB, len(steps)))
                            for si in batch:
                                st_va(si)
                            for si in batch:
                                st_sc(si)
                            for si in batch:
                                st_vb(si)
                            for si in batch:
                                st_pe1(si)
                            for si in batch:
                                st_pe2(si)
                        n_cs[0] += len(steps)
                        P.wait("vector", s_orms, s_orms.n)
                        P.wait("scalar", s_orms, s_orms.n)
                        for bi in range(2):
                            for half in range(2):
                                P.wait("scalar", s_epv, s_epv.n)
                                P.op("scalar", lambda h, bi=bi, half=half: h.activation(
                                    sq_scr[:], o_raw[bi][:, half * 1024:(half + 1) * 1024],
                                    AF.Square), inc=(s_eps, 1))
                                P.wait("vector", s_eps, s_eps.n)
                                for j in range(2):
                                    hh2 = half * 2 + j
                                    P.op("vector", lambda h, hh2=hh2, j=j: h.reduce_sum(
                                        mh[:, hh2:hh2 + 1],
                                        sq_scr[:, j * 512:(j + 1) * 512], axis=AX.X),
                                        inc=(s_avw, 1) if j == 0 else (s_epv, 1))
                            P.wait("vector", s_epv, s_epv.n)
                            P.op("vector", lambda h: h.tensor_scalar(
                                mh[:, 4:8], mh[:, 0:4], 1.0 / 512.0, EPS,
                                ALU.mult, ALU.add), inc=(s_epv, 1))
                            P.wait("scalar", s_epv, s_epv.n)
                            P.op("scalar", lambda h: h.activation(mh[:, 8:12], mh[:, 4:8],
                                                                  AF.Sqrt), inc=(s_eps, 1))
                            P.wait("vector", s_eps, s_eps.n)
                            P.op("vector", lambda h: h.reciprocal(out=mh[:, 12:16],
                                                                  in_=mh[:, 8:12]))
                            for hh2 in range(4):
                                P.op("vector", lambda h, hh2=hh2, bi=bi: h.tensor_scalar(
                                    ofg[bi][:, hh2 * 512:(hh2 + 1) * 512],
                                    o_raw[bi][:, hh2 * 512:(hh2 + 1) * 512],
                                    mh[:, 12 + hh2:13 + hh2], None, ALU.mult))
                            P.op("vector", lambda h, bi=bi: h.tensor_mul(
                                A(ofg[bi], 0, [[2048, 128], [512, 4], [1, 512]]),
                                A(ofg[bi], 0, [[2048, 128], [512, 4], [1, 512]]),
                                A(rmsw_sb, 0, [[512, 128], [0, 4], [1, 512]])))
                            P.op("vector", lambda h, bi=bi: h.tensor_mul(
                                ofg[bi][:], ofg[bi][:], wgs[bi][:]), inc=(s_ep, 1))
                        for bi in range(2):
                            b = 2 * p + bi
                            k = e_tile[0]; e_tile[0] += 1
                            for rnd in range(2):
                                P.wait("tensor", s_ep, s_ep.n)
                                P.wait("tensor", s_otc, s_otc.n)
                                P.wait("tensor", s_kdc, s_kdc.n)
                                P.wait("tensor", s_trc, s_trc.n)
                                for j in range(8):
                                    fvt = rnd * 8 + j
                                    P.op("tensor", lambda h, fvt=fvt, j=j, bi=bi: h.transpose(
                                        ptr6[:, j * 128:(j + 1) * 128],
                                        ofg[bi][:, fvt * 128:(fvt + 1) * 128], identb[:]),
                                        inc=(s_pot, 1) if j == 7 else None)
                                P.wait("vector", s_pot, s_pot.n)
                                P.wait("vector", s_pout, s_pout.n)
                                P.op("vector", lambda h, rnd=rnd: h.tensor_copy(
                                    oT_sb[:, rnd * 1024:(rnd + 1) * 1024], ptr6[:]),
                                    inc=(s_otc, 1))
                            P.wait("sync", s_xo, s_xo.n)
                            iax = ds_ax.issue(P, "sync", lambda h, b=b, w=w, bi=bi:
                                              h.dma_start(out=x1b[bi][:],
                                                          in_=A(at_src, (b * T + w * WT) * D,
                                                                [[D, 128], [1, D]])))
                            P.wait("tensor", s_otc, s_otc.n)
                            P.wait("tensor", s_xo, s_xo.n)
                            P.wait("tensor", s_atm, s_atm.n)
                            P.wait("tensor", s_Sup, s_Sup.n)
                            pout2 = pS[0]
                            for wp in range(4):
                                slot, wn = stream_piece(woi_d, wp * 128)
                                ds_wp.wait_done(P, "tensor", wn)
                                for fl in range(4):
                                    fvt = wp * 4 + fl
                                    for fch in range(2):
                                        P.op("tensor", lambda h, fvt=fvt, fl=fl, fch=fch, slot=slot:
                                             h.matmul((pout if fch == 0 else pout2)[:],
                                                      oT_sb[:, fvt * 128:(fvt + 1) * 128],
                                                      slot[:, fl * 1024 + fch * 512:
                                                           fl * 1024 + fch * 512 + 512],
                                                      start=(fvt == 0), stop=(fvt == 15)),
                                             inc=(s_pout, 1) if (fl == 3 and fch == 1) else None)
                                piece_free.append((s_pout, s_pout.n))
                            for fch in range(2):
                                P.wait("vector", s_pout, s_pout.n)
                                ds_ax.wait_done(P, "vector", iax)
                                ds_ast.wait_done(P, "vector", k - 2)
                                P.op("vector", lambda h, bi=bi, fch=fch: h.tensor_add(
                                    xob[bi][:, fch * 512:(fch + 1) * 512],
                                    (pout if fch == 0 else pout2)[:],
                                    x1b[bi][:, fch * 512:(fch + 1) * 512]),
                                    inc=(s_xo, 1))
                            P.wait("sync", s_xo, s_xo.n)
                            ds_ast.issue(P, "sync", lambda h, b=b, w=w, bi=bi:
                                         h.dma_start(out=A(at_dst, (b * T + w * WT) * D,
                                                           [[D, 128], [1, D]]),
                                                     in_=xob[bi][:]))
                        if w + 1 < DEV_NWIN:
                            emit_attn_ln(p, w + 1)
                P.flush(nc)
            gates = stream_gates(ds_ast)

        if do_f2:
            f2_st = ffn_stage("f", "f2", f2_src, y_d, gates)
            gates = stream_gates(f2_st)

        P = Prog()
        for gs, gv in gates:
            P.wait("sync", gs, gv)
        P.flush(nc)

    return nc


def make_inputs(L0_attn_Wq, L0_attn_Wk, L0_attn_Wv, L0_attn_Wg,
                L0_attn_gk_w1, L0_attn_gk_w2, L0_attn_gk_b, L0_attn_Wo,
                L0_attn_rms_w, L0_ffn_router_w, L0_ffn_w13, L0_ffn_w2):
    f32 = np.float32
    ac = np.ascontiguousarray
    consts = dict(
        rwT=ac(np.asarray(L0_ffn_router_w, f32).T),
        Wq=ac(np.asarray(L0_attn_Wq, f32)),
        Wk=ac(np.asarray(L0_attn_Wk, f32)),
        Wv=ac(np.asarray(L0_attn_Wv, f32)),
        Wg=ac(np.asarray(L0_attn_Wg, f32)),
        Wo=ac(np.asarray(L0_attn_Wo, f32)),
        gk1=ac(np.asarray(L0_attn_gk_w1, f32)),
        gk2=ac(np.asarray(L0_attn_gk_w2, f32)),
        gkbT=ac(np.asarray(L0_attn_gk_b, f32).reshape(8, 128).T),
        rmswB=ac(np.broadcast_to(np.asarray(L0_attn_rms_w, f32), (128, 512))),
        w13=ac(np.asarray(L0_ffn_w13, f32)),
        w2=ac(np.asarray(L0_ffn_w2, f32)),
        identb=np.eye(128, dtype=ml_dtypes.bfloat16),
        identf=np.eye(128, dtype=f32),
        maskU=ac(np.triu(np.ones((64, 64), f32))),
        ones64=np.ones((128, 64), f32),
    )
    oh = np.zeros((8, 1024), ml_dtypes.bfloat16)
    for e in range(8):
        oh[e, e * 128:(e + 1) * 128] = 1
    consts["oneh"] = oh
    return consts


def run_full(x_full, consts, do_f1=True, do_attn=True, do_f2=True):
    x = np.ascontiguousarray(np.asarray(x_full, np.float32)).reshape(32 * T, D)
    nc = build_program(do_f1=do_f1, do_attn=do_attn, do_f2=do_f2)
    in_maps = []
    for c in range(NCORES):
        m = dict(consts)
        m["x"] = np.ascontiguousarray(x[c * NTOK:(c + 1) * NTOK])
        in_maps.append(m)
    res = run_bass_kernel_spmd(nc, in_maps[:DEV_NCORES], core_ids=list(range(DEV_NCORES)))
    out = np.concatenate([r["y"] for r in res.results], axis=0)
    if DEV_NCORES < NCORES:
        pad = np.zeros((NTOK * (NCORES - DEV_NCORES), D), np.float32)
        out = np.concatenate([out, pad], axis=0)
    return out.reshape(32, T, D)


def kernel(L0_attn_x, L0_attn_Wq, L0_attn_Wk, L0_attn_Wv, L0_attn_Wg,
           L0_attn_gk_w1, L0_attn_gk_w2, L0_attn_gk_b, L0_attn_Wo, L0_attn_rms_w,
           L0_ffn_router_w, L0_ffn_w13, L0_ffn_w2):
    consts = make_inputs(L0_attn_Wq, L0_attn_Wk, L0_attn_Wv, L0_attn_Wg,
                         L0_attn_gk_w1, L0_attn_gk_w2, L0_attn_gk_b, L0_attn_Wo,
                         L0_attn_rms_w, L0_ffn_router_w, L0_ffn_w13, L0_ffn_w2)
    return run_full(L0_attn_x, consts)


# version 17
# speedup vs baseline: 1.1527x; 1.1527x over previous
import numpy as np
import ml_dtypes
import concourse.bass as bass
import concourse.mybir as mybir
from concourse.bass_utils import run_bass_kernel_spmd
from contextlib import ExitStack

F32 = mybir.dt.float32
BF16 = mybir.dt.bfloat16
AF = mybir.ActivationFunctionType
ALU = mybir.AluOpType
AX = mybir.AxisListType

NCORES = 8
NB = 4
T = 2048
D = 1024
NTOK = NB * T
G = 1024
NGRP = NTOK // G
E = 8
I2 = 4096
II = 2048
H = 4
C = 128
WT = 128
NWIN = T // WT
LN16 = float(np.log(16.0))
DEV_NGRP = NGRP
DEV_NE = E
DEV_NCORES = NCORES
DEV_PHASE = 5
DEV_DUMP = False
DEV_NPASS = 2
DEV_NWIN = NWIN
EPS = 1e-5


class Sem:
    __slots__ = ("h", "n")
    def __init__(self, h):
        self.h = h
        self.n = 0


class Prog:
    def __init__(self):
        self.items = []
        self.last_wait = {}

    def op(self, eng, fn, inc=None, inc2=None):
        if inc is not None:
            inc[0].n += inc[1]
        if inc2 is not None:
            inc2[0].n += inc2[1]
        def run(h, fn=fn, inc=inc, inc2=inc2):
            ins = fn(h)
            if inc is not None:
                ins = ins.then_inc(inc[0].h, inc[1])
            if inc2 is not None:
                ins.then_inc(inc2[0].h, inc2[1])
        self.items.append((eng, run))

    def wait(self, eng, sem, val):
        if val <= 0:
            return
        key = (eng, id(sem))
        if self.last_wait.get(key, -1) >= val:
            return
        self.last_wait[key] = val
        self.items.append((eng, lambda h, s=sem, v=val: h.wait_ge(s.h, v)))

    def run(self, eng, handle):
        for e, fn in self.items:
            if e == eng:
                fn(handle)

    def flush(self, nc):
        with nc.Block() as blk:
            @blk.sync
            def _(h):
                self.run("sync", h)
            @blk.scalar
            def _(h):
                self.run("scalar", h)
            @blk.vector
            def _(h):
                self.run("vector", h)
            @blk.tensor
            def _(h):
                self.run("tensor", h)
            @blk.gpsimd
            def _(h):
                self.run("gpsimd", h)


def A(t, off, dims):
    return bass.AP(t, off, dims)


class DStream:
    def __init__(self, sem_fn, name, nbuf):
        self.sems = [sem_fn(name + str(j)) for j in range(nbuf)]
        self.nbuf = nbuf
        self.count = 0

    def issue(self, P, q, fn):
        i = self.count
        self.count += 1
        P.op(q, fn, inc=(self.sems[i % self.nbuf], 16))
        return i

    def wait_done(self, P, eng, i):
        if i < 0:
            return
        P.wait(eng, self.sems[i % self.nbuf], 16 * (i // self.nbuf + 1))

    def wait_all(self, P, eng):
        for j, s in enumerate(self.sems):
            P.wait(eng, s, s.n)


def build_program(do_f1=True, do_attn=True, do_f2=True):
    nc = bass.Bass("TRN2", target_bir_lowering=False, detect_race_conditions=False)

    x_d = nc.dram_tensor("x", [NTOK, D], F32, kind="ExternalInput")
    w13_d = nc.dram_tensor("w13", [E, D, I2], F32, kind="ExternalInput")
    w2_d = nc.dram_tensor("w2", [E, II, D], F32, kind="ExternalInput")
    rwT_d = nc.dram_tensor("rwT", [D, E], F32, kind="ExternalInput")
    wq_d = nc.dram_tensor("Wq", [D, D], F32, kind="ExternalInput")
    wk_d = nc.dram_tensor("Wk", [D, D], F32, kind="ExternalInput")
    wv_d = nc.dram_tensor("Wv", [D, 2048], F32, kind="ExternalInput")
    wg_d = nc.dram_tensor("Wg", [D, 2048], F32, kind="ExternalInput")
    wo_d = nc.dram_tensor("Wo", [2048, D], F32, kind="ExternalInput")
    gk1_d = nc.dram_tensor("gk1", [D, 16], F32, kind="ExternalInput")
    gk2_d = nc.dram_tensor("gk2", [16, D], F32, kind="ExternalInput")
    gkbT_d = nc.dram_tensor("gkbT", [128, 8], F32, kind="ExternalInput")
    rmswB_d = nc.dram_tensor("rmswB", [128, 512], F32, kind="ExternalInput")
    identb_d = nc.dram_tensor("identb", [128, 128], BF16, kind="ExternalInput")
    identf_d = nc.dram_tensor("identf", [128, 128], F32, kind="ExternalInput")
    mask_d = nc.dram_tensor("maskU", [128, 128], F32, kind="ExternalInput")
    oneh_d = nc.dram_tensor("oneh", [8, 1024], BF16, kind="ExternalInput")
    ones_d = nc.dram_tensor("ones64", [128, 128], F32, kind="ExternalInput")

    x1_d = nc.dram_tensor("x1s", [NTOK, D], F32, kind="Internal")
    x2_d = nc.dram_tensor("x2s", [NTOK, D], F32, kind="Internal")
    y_d = nc.dram_tensor("y", [NTOK, D], F32, kind="ExternalOutput")
    w13i_d = nc.dram_tensor("w13i", [E * 4 * 128, 8192], BF16, kind="Internal")
    w2i_d = nc.dram_tensor("w2i", [E * 4 * 128, 4096], BF16, kind="Internal")
    wqi_d = nc.dram_tensor("wqi", [2 * 128, 4096], BF16, kind="Internal")
    wki_d = nc.dram_tensor("wki", [2 * 128, 4096], BF16, kind="Internal")
    wvi_d = nc.dram_tensor("wvi", [4 * 128, 4096], BF16, kind="Internal")
    wgi_d = nc.dram_tensor("wgi", [4 * 128, 4096], BF16, kind="Internal")
    woi_d = nc.dram_tensor("woi", [4 * 128, 4096], BF16, kind="Internal")

    f1_dst = x1_d if (do_attn or do_f2) else y_d
    at_dst = x2_d if do_f2 else y_d
    at_src = x1_d if do_f1 else x_d
    f2_src = x2_d if do_attn else (x1_d if do_f1 else x_d)

    st = ExitStack()
    sems = {}
    def sem(name):
        if name not in sems:
            sems[name] = Sem(st.enter_context(nc.semaphore(name)))
        return sems[name]

    with st:
        identb = st.enter_context(nc.sbuf_tensor("c_identb", [128, 128], BF16))
        identf = st.enter_context(nc.sbuf_tensor("c_identf", [128, 128], F32))
        maskU = st.enter_context(nc.sbuf_tensor("c_maskU", [128, 128], F32))
        oneh = st.enter_context(nc.sbuf_tensor("c_oneh", [8, 1024], BF16))
        ones64 = st.enter_context(nc.sbuf_tensor("c_ones64", [128, 128], F32))
        rwT_sb = st.enter_context(nc.sbuf_tensor("rwT_sb", [128, 64], BF16))
        gk1_sb = st.enter_context(nc.sbuf_tensor("gk1_sb", [128, 128], BF16))
        gk2_sb = st.enter_context(nc.sbuf_tensor("gk2_sb", [16, 1024], BF16))
        gkbT_sb = st.enter_context(nc.sbuf_tensor("gkbT_sb", [128, 8], F32))
        rmsw_sb = st.enter_context(nc.sbuf_tensor("rmsw_sb", [128, 512], BF16))

        s_c = sem("s_c")

        P = Prog()
        for dst, src in ((identb, identb_d), (identf, identf_d), (maskU, mask_d),
                         (oneh, oneh_d), (ones64, ones_d), (gkbT_sb, gkbT_d)):
            P.op("sync", lambda h, dst=dst, src=src: h.dma_start(out=dst[:], in_=src.ap()),
                 inc=(s_c, 16))
        s_cg = sem("s_cg")
        P.op("gpsimd", lambda h: h.dma_start(
            out=rwT_sb[:], in_=A(rwT_d, 0, [[8, 128], [1024, 8], [1, 8]])), inc=(s_cg, 16))
        P.op("gpsimd", lambda h: h.dma_start(
            out=gk1_sb[:], in_=A(gk1_d, 0, [[16, 128], [2048, 8], [1, 16]])), inc=(s_cg, 16))
        P.op("gpsimd", lambda h: h.dma_start(out=gk2_sb[:], in_=gk2_d.ap()), inc=(s_cg, 16))
        P.op("gpsimd", lambda h: h.dma_start(out=rmsw_sb[:], in_=rmswB_d.ap()), inc=(s_cg, 16))
        NC_CONST = 6
        NC_CONSTG = 4

        ds_pi = DStream(sem, "s_pi", 2)
        ds_po = DStream(sem, "s_po", 2)
        with nc.sbuf_tensor("stg0", [128, 8192], BF16) as stg0, \
             nc.sbuf_tensor("stg1", [128, 8192], BF16) as stg1:
            stgs = (stg0, stg1)
            pieces = []
            for e in range(E):
                for ib in range(4):
                    for sect in range(2):
                        src = A(w13_d, e * D * I2 + sect * 2048 + ib * 512,
                                [[I2, 128], [128 * I2, 8], [1, 512]])
                        pieces.append((src, w13i_d, (e * 4 + ib) * 128, sect * 4096, 4096))
            for e in range(E):
                for q in range(4):
                    src = A(w2_d, e * II * D + q * 256,
                            [[D, 128], [128 * D, 16], [1, 256]])
                    pieces.append((src, w2i_d, (e * 4 + q) * 128, 0, 4096))
            for fp in range(2):
                pieces.append((A(wq_d, fp * 512, [[D, 128], [128 * D, 8], [1, 512]]),
                               wqi_d, fp * 128, 0, 4096))
            for fp in range(2):
                pieces.append((A(wk_d, fp * 512, [[D, 128], [128 * D, 8], [1, 512]]),
                               wki_d, fp * 128, 0, 4096))
            for fp in range(4):
                pieces.append((A(wv_d, fp * 512, [[2048, 128], [128 * 2048, 8], [1, 512]]),
                               wvi_d, fp * 128, 0, 4096))
            for fp in range(4):
                pieces.append((A(wg_d, fp * 512, [[2048, 128], [128 * 2048, 8], [1, 512]]),
                               wgi_d, fp * 128, 0, 4096))
            for fp in range(4):
                pieces.append((A(wo_d, fp * 4 * 128 * D, [[D, 128], [128 * D, 4], [1, 1024]]),
                               woi_d, fp * 128, 0, 4096))

            for i, (src, img, row, coloff, ncols) in enumerate(pieces):
                b = stgs[i % 2]
                ds_po.wait_done(P, "gpsimd", i - 2)
                counts = [d[1] for d in src.ap[1:]]
                strides = []
                acc = 1
                for cnt in reversed(counts):
                    strides.append(acc)
                    acc *= cnt
                strides = list(reversed(strides))
                dstap = A(b, 0, [[8192, 128]] + [[s, cn] for s, cn in zip(strides, counts)])
                ii = ds_pi.issue(P, "gpsimd", lambda h, dstap=dstap, src=src:
                                 h.dma_start(out=dstap, in_=src))
                ds_pi.wait_done(P, "sync", ii)
                rowlen = img.shape[1]
                dst = A(img, row * rowlen + coloff, [[rowlen, 128], [1, ncols]])
                ds_po.issue(P, "sync", lambda h, b=b, dst=dst, ncols=ncols:
                            h.dma_start(out=dst, in_=b[:, 0:ncols]))
            n_pieces = len(pieces)
            P.flush(nc)
        P = None

        def emit_ln_tile(P, ctx, src_d, row0, col0, do_router, tile_no):
            pfx = ctx["pfx"]
            VE = ctx.get("ln_eng", "vector")
            CE = ctx.get("lncp_eng", "vector")
            xb = ctx["xb"]; sqb = ctx["sqb"]; stats = ctx["stats"]
            xlnb = ctx["xlnb"]; xlnT = ctx["xlnT"]; ptr6 = ctx["ptr6"]
            ds_x = ctx["ds_x"]; s_sq = sem(pfx + "sq")
            s_va = sem(pfx + "va"); s_rs = sem(pfx + "rs")
            s_ln = sem(pfx + "ln"); s_ptr = sem(pfx + "ptr")
            s_trc = sem(pfx + "trc"); s_red = sem(pfx + "red")
            s_sw = sem(pfx + "sw")
            par = tile_no % 2
            xbp = xb[par]; xlp = xlnb[par]
            P.wait("sync", s_ln, sem(pfx + "ln").n - 1)
            ix = ds_x.issue(P, "sync", lambda h: h.dma_start(
                out=xbp[:], in_=A(src_d, row0 * D, [[D, 128], [1, D]])))
            s_vw = sem(pfx + VE + "w")
            def vw(fn):
                P.op(VE, fn, inc=(s_vw, 1))
                P.wait(VE, s_vw, s_vw.n)
            ds_x.wait_done(P, VE, ix)
            P.wait(VE, s_red, s_red.n)
            P.op(VE, lambda h: h.tensor_mul(sqb[:], xbp[:], xbp[:]), inc=(s_sq, 1))
            ds_x.wait_done(P, "vector", ix)
            P.wait("vector", s_sq, s_sq.n)
            P.op("vector", lambda h: h.reduce_sum(stats[:, 0:1], xbp[:], axis=AX.X))
            P.op("vector", lambda h: h.reduce_sum(stats[:, 1:2], sqb[:], axis=AX.X),
                 inc=(s_red, 1))
            P.wait(VE, s_red, s_red.n)
            vw(lambda h: h.tensor_scalar(stats[:, 2:3], stats[:, 0:1],
                                         1.0 / D, None, ALU.mult))
            vw(lambda h: h.tensor_mul(stats[:, 3:4], stats[:, 2:3], stats[:, 2:3]))
            vw(lambda h: h.tensor_scalar(stats[:, 13:14], stats[:, 1:2],
                                         1.0 / D, None, ALU.mult))
            vw(lambda h: h.tensor_sub(stats[:, 4:5], stats[:, 13:14],
                                      stats[:, 3:4]))
            P.op(VE, lambda h: h.tensor_scalar(stats[:, 5:6], stats[:, 4:5],
                                               EPS, None, ALU.add), inc=(s_va, 1))
            P.wait("scalar", s_va, s_va.n)
            P.op("scalar", lambda h: h.activation(stats[:, 6:7], stats[:, 5:6],
                                                  AF.Sqrt), inc=(s_rs, 1))
            P.wait(VE, s_rs, s_rs.n)
            P.op(VE, lambda h: h.reciprocal(out=stats[:, 7:8], in_=stats[:, 6:7]))
            P.op(VE, lambda h: h.tensor_scalar(
                xlp[:], xbp[:], stats[:, 2:3], stats[:, 7:8],
                ALU.subtract, ALU.mult), inc=(s_ln, 1))
            P.wait("tensor", s_ln, s_ln.n)
            P.wait("tensor", s_trc, s_trc.n)
            for g in ctx["ptr6_guards"]:
                P.wait("tensor", g, g.n)
            for dt in range(8):
                P.op("tensor", lambda h, dt=dt: h.transpose(
                    ptr6[:, dt * 128:(dt + 1) * 128], xlp[:, dt * 128:(dt + 1) * 128],
                    identb[:]), inc=(s_ptr, 1) if dt == 7 else None)
            P.wait(CE, s_ptr, s_ptr.n)
            xlw = ctx["xlw"]
            xw = xlnT.shape[1]
            if CE == "scalar":
                P.op(CE, lambda h: h.activation(
                    A(xlnT, col0, [[xw, 128], [xlw, 8], [1, 128]]),
                    A(ptr6, 0, [[1024, 128], [128, 8], [1, 128]]), AF.Copy),
                    inc=(s_trc, 1))
            else:
                P.op(CE, lambda h: h.tensor_copy(
                    A(xlnT, col0, [[xw, 128], [xlw, 8], [1, 128]]),
                    A(ptr6, 0, [[1024, 128], [128, 8], [1, 128]])), inc=(s_trc, 1))
            if do_router:
                emit_router_tile(P, ctx, col0, tile_no)

        def emit_router_tile(P, ctx, col0, tile_no):
            pfx = ctx["pfx"]
            VE = ctx.get("ln_eng", "vector")
            CE = ctx.get("lncp_eng", "vector")
            xlnT = ctx["xlnT"]; p_rt = ctx["p_rt"]; ptr6 = ctx["ptr6"]
            stats = ctx["stats"]; rpb = ctx["rpb"]; rT = ctx["rT"]
            qv = ctx["qv"]; qv2 = ctx["qv2"]; qv3 = ctx["qv3"]
            s_trc = sem(pfx + "trc"); s_prt = sem(pfx + "prt")
            s_t2 = sem(pfx + "t2"); s_rtc = sem(pfx + "rtc")
            s_ptr = sem(pfx + "ptr"); s_nm1 = sem(pfx + "nm1")
            s_qx = sem(pfx + "qx"); s_sw = sem(pfx + "sw")
            s_red = sem(pfx + "red"); s_gx = sem(pfx + "gx")
            s_vw = sem(pfx + VE + "w")
            def vw(fn):
                P.op(VE, fn, inc=(s_vw, 1))
                P.wait(VE, s_vw, s_vw.n)
            rto = (tile_no % 2) * 8
            P.wait("tensor", s_trc, s_trc.n)
            P.wait("tensor", s_t2, s_t2.n - 1)
            for dt in range(8):
                P.op("tensor", lambda h, dt=dt, col0=col0, rto=rto: h.matmul(
                    p_rt[:, rto:rto + 8],
                    A(xlnT, dt * 1024 + col0, [[xlnT.shape[1], 128], [1, 128]]),
                    rwT_sb[:, dt * 8:(dt + 1) * 8],
                    start=(dt == 0), stop=(dt == 7)),
                    inc=(s_prt, 1) if dt == 7 else None)
            P.wait("vector", s_prt, s_prt.n)
            P.op("vector", lambda h, rto=rto: h.reduce_max(
                stats[:, 8:9], p_rt[:, rto:rto + 8], axis=AX.X,
                negate=True), inc=(s_nm1, 1))
            P.wait("scalar", s_nm1, s_nm1.n)
            P.op("scalar", lambda h, rto=rto: h.activation(
                qv[:], p_rt[:, rto:rto + 8], AF.Exp,
                bias=stats[:, 8:9]), inc=(s_qx, 1))
            P.wait("vector", s_qx, s_qx.n)
            P.op("vector", lambda h: h.reduce_sum(stats[:, 9:10], qv[:], axis=AX.X),
                 inc=(s_red, 1))
            P.wait(VE, s_qx, s_qx.n)
            vw(lambda h: h.tensor_single_scalar(qv2[:], qv[:], 1.0, ALU.is_ge))
            P.op(VE, lambda h: h.tensor_sub(qv3[:], qv[:], qv2[:]), inc=(s_gx, 1))
            P.wait("vector", s_gx, s_gx.n)
            P.op("vector", lambda h: h.reduce_max(stats[:, 10:11], qv3[:], axis=AX.X),
                 inc=(s_red, 1))
            P.wait(VE, s_red, s_red.n)
            vw(lambda h: h.tensor_scalar(stats[:, 14:15], stats[:, 9:10],
                                         1e-6, None, ALU.mult))
            vw(lambda h: h.tensor_add(stats[:, 11:12], stats[:, 14:15],
                                      stats[:, 10:11]))
            vw(lambda h: h.tensor_scalar(stats[:, 11:12], stats[:, 11:12],
                                         2.0, 2.0, ALU.mult, ALU.add))
            vw(lambda h: h.reciprocal(out=stats[:, 12:13], in_=stats[:, 11:12]))
            vw(lambda h: h.tensor_scalar(qv2[:], qv[:], stats[:, 10:11],
                                         None, ALU.is_ge))
            vw(lambda h: h.tensor_mul(qv3[:], qv[:], qv2[:]))
            P.op(VE, lambda h: h.tensor_scalar(rpb[:], qv3[:], stats[:, 12:13],
                                               None, ALU.mult), inc=(s_t2, 1))
            P.wait("tensor", s_t2, s_t2.n)
            P.wait("tensor", s_rtc, s_rtc.n)
            P.op("tensor", lambda h: h.transpose(ptr6[0:8, 0:128], rpb[:], identb[:]),
                 inc=(s_ptr, 1))
            P.wait(CE, s_ptr, s_ptr.n)
            if CE == "scalar":
                P.op(CE, lambda h, col0=col0: h.activation(
                    rT[:, col0:col0 + 128], ptr6[0:8, 0:128], AF.Copy), inc=(s_rtc, 1))
            else:
                P.op(CE, lambda h, col0=col0: h.tensor_copy(
                    rT[:, col0:col0 + 128], ptr6[0:8, 0:128]), inc=(s_rtc, 1))

        fcnt = {"c_tile": 0, "c_w13": 0, "c_w2": 0, "c_h": 0, "c_y": 0, "c_res": 0}
        fds = {}

        def ffn_stage(name, tname, src_d, dst_d, gate_sems):
            P = Prog()
            with ExitStack() as ar:
                sb = lambda n, shp, dt: ar.enter_context(nc.sbuf_tensor(tname + n, shp, dt))
                ps = lambda n, shp, dt: ar.enter_context(nc.psum_tensor(tname + n, shp, dt))
                ctx = {"pfx": name}
                ctx["xb"] = [sb("xb0", [128, 1024], F32), sb("xb1", [128, 1024], F32)]
                ctx["sqb"] = sb("sqb", [128, 1024], F32)
                ctx["stats"] = sb("stats", [128, 16], F32)
                ctx["xlnb"] = [sb("xlnb0", [128, 1024], BF16), sb("xlnb1", [128, 1024], BF16)]
                xlnTs = [sb("xlnTa", [128, 8192], BF16), sb("xlnTb", [128, 8192], BF16)]
                ctx["xlw"] = 1024
                ctx["rpb"] = sb("rpb", [128, 8], BF16)
                rTs = [sb("rTa", [8, 1024], BF16), sb("rTb", [8, 1024], BF16)]
                rb = sb("rb", [128, 1024], BF16)
                ctx["ln_eng"] = "vector"
                ctx["lncp_eng"] = "vector"
                w13b = [sb("w13b0", [128, 8192], BF16), sb("w13b1", [128, 8192], BF16)]
                w2b = [sb("w2b0", [128, 4096], BF16), sb("w2b1", [128, 4096], BF16),
                       sb("w2b2", [128, 4096], BF16)]
                sgb = sb("sgb", [128, 1024], BF16)
                tbb = sb("tbb", [128, 1024], BF16)
                h_all = sb("h_all", [128, 16384], BF16)
                yac = sb("yac", [128, 8192], F32)
                xrb = [sb("xrb0", [128, 1024], F32), sb("xrb1", [128, 1024], F32)]
                xob = [sb("xob0", [128, 1024], F32), sb("xob1", [128, 1024], F32)]
                ctx["qv"] = sb("qv", [128, 8], F32)
                ctx["qv2"] = sb("qv2", [128, 8], F32)
                ctx["qv3"] = sb("qv3", [128, 8], F32)
                pgP = ps("pgP", [128, 1024], F32)
                puP = ps("puP", [128, 1024], F32)
                py = [ps("py0", [128, 512], F32), ps("py1", [128, 512], F32)]
                ctx["ptr6"] = ps("ptr6", [128, 1024], BF16)
                ctx["p_rt"] = ps("p_rt", [128, 16], F32)

                if not fds:
                    fds["x"] = DStream(sem, name + "x", 2)
                    fds["w13"] = DStream(sem, name + "w13", 2)
                    fds["w2"] = DStream(sem, name + "w2", 3)
                    fds["xr"] = DStream(sem, name + "xr", 2)
                    fds["st"] = DStream(sem, name + "st", 2)
                ds_x = fds["x"]; ds_w13 = fds["w13"]; ds_w2 = fds["w2"]
                ds_xr = fds["xr"]; ds_st = fds["st"]
                ctx["ds_x"] = ds_x
                s_pg = sem(name + "pg"); s_pu = sem(name + "pu")
                s_sg = sem(name + "sg"); s_puf = sem(name + "puf")
                s_h = sem(name + "h"); s_rbs = sem(name + "rbs"); s_prb = sem(name + "prb")
                s_py = sem(name + "py"); s_yev = sem(name + "yev")
                s_ptres = sem(name + "ptres")
                s_resc = sem(name + "resc")
                s_rtc = sem(name + "rtc"); s_t2 = sem(name + "t2")
                s_trc = sem(name + "trc")
                ctx["ptr6_guards"] = [s_rtc]

                ds_po.wait_all(P, "sync")
                for gs, gv in gate_sems:
                    for eng in ("sync", "tensor", "vector", "scalar"):
                        P.wait(eng, gs, gv)
                for eng in ("tensor", "vector", "scalar"):
                    P.wait(eng, s_c, 16 * NC_CONST)
                    P.wait(eng, sem("s_cg"), 16 * NC_CONSTG)

                h_marks = [0]
                rb_marks = [0]

                def emit_ln_group(g):
                    gm = max(0, g - 1)
                    P.wait("scalar", sem(name + "pu"), h_marks[min(gm, len(h_marks) - 1)])
                    P.wait("scalar", sem(name + "prb"), rb_marks[min(gm, len(rb_marks) - 1)])
                    ctx["xlnT"] = xlnTs[g % 2]
                    ctx["rT"] = rTs[g % 2]
                    for t in range(8):
                        emit_ln_tile(P, ctx, src_d, g * G + t * 128, t * 128,
                                     DEV_PHASE >= 2, fcnt["c_tile"])
                        fcnt["c_tile"] += 1

                emit_ln_group(0)
                for g in range(DEV_NGRP):
                    xlnT = xlnTs[g % 2]
                    rT = rTs[g % 2]
                    for e in (range(DEV_NE) if DEV_PHASE >= 3 else []):
                        P.wait("tensor", s_rtc, s_rtc.n)
                        P.wait("tensor", s_yev, s_yev.n)
                        P.wait("tensor", s_resc, fcnt["c_res"])
                        for cch in range(2):
                            P.op("tensor", lambda h, cch=cch, e=e, rT=rT: h.matmul(
                                py[cch][:, 0:512], oneh[:, e * 128:(e + 1) * 128],
                                rT[:, cch * 512:(cch + 1) * 512],
                                start=True, stop=True), inc=(s_prb, 1) if cch == 1 else None)
                        P.wait("vector", s_prb, s_prb.n)
                        P.op("vector", lambda h: h.tensor_copy(rb[:, 0:512], py[0][:, 0:512]))
                        P.op("vector", lambda h: h.tensor_copy(rb[:, 512:1024], py[1][:, 0:512]),
                             inc=(s_rbs, 1))
                        P.wait("vector", s_py, s_py.n)
                        for ib in range(4):
                            wb = fcnt['c_w13']; fcnt['c_w13'] += 1
                            wslot = w13b[wb % 2]
                            P.wait("sync", s_pu, 4 * (wb - 1))
                            ds_w13.issue(P, "sync", lambda h, wslot=wslot, e=e, ib=ib:
                                         h.dma_start(out=wslot[:],
                                                     in_=A(w13i_d, (e * 4 + ib) * 128 * 8192,
                                                           [[8192, 128], [1, 8192]])))
                            for ch in range(2):
                                for ipair in range(2):
                                    nh = fcnt['c_h']; fcnt['c_h'] += 1
                                    ita, itb = 2 * ipair, 2 * ipair + 1
                                    ds_w13.wait_done(P, "tensor", wb)
                                    P.wait("tensor", s_sg, nh - 1)
                                    P.wait("tensor", s_puf, nh - 1)
                                    if e == 0:
                                        P.wait("tensor", s_t2, s_t2.n)
                                    for half, it in ((0, ita), (1, itb)):
                                        for dt in range(8):
                                            P.op("tensor", lambda h, dt=dt, it=it, half=half, ch=ch, wslot=wslot, xlnT=xlnT:
                                                 h.matmul(pgP[:, half * 512:(half + 1) * 512],
                                                          wslot[:, dt * 512 + it * 128:
                                                                dt * 512 + (it + 1) * 128],
                                                          xlnT[:, dt * 1024 + ch * 512:
                                                               dt * 1024 + (ch + 1) * 512],
                                                          start=(dt == 0), stop=(dt == 7)),
                                                 inc=(s_pg, 1) if (dt == 7 and half == 1) else None)
                                    for half, it in ((0, ita), (1, itb)):
                                        for dt in range(8):
                                            P.op("tensor", lambda h, dt=dt, it=it, half=half, ch=ch, wslot=wslot, xlnT=xlnT:
                                                 h.matmul(puP[:, half * 512:(half + 1) * 512],
                                                          wslot[:, 4096 + dt * 512 + it * 128:
                                                                4096 + dt * 512 + (it + 1) * 128],
                                                          xlnT[:, dt * 1024 + ch * 512:
                                                               dt * 1024 + (ch + 1) * 512],
                                                          start=(dt == 0), stop=(dt == 7)),
                                                 inc=(s_pu, 1) if (dt == 7 and half == 1) else None)
                                    P.wait("scalar", s_pg, nh + 1)
                                    P.wait("scalar", s_puf, nh - 1)
                                    P.op("scalar", lambda h: h.activation(
                                        sgb[:], pgP[:], AF.Silu), inc=(s_sg, 1))
                                    P.wait("vector", s_pu, nh + 1)
                                    P.wait("vector", s_sg, nh + 1)
                                    P.op("vector", lambda h: h.tensor_mul(
                                        tbb[:], puP[:], sgb[:]), inc=(s_puf, 1))
                                    P.wait("vector", s_rbs, s_rbs.n)
                                    iga = (ib * 4 + ita) * 1024 + ch * 512
                                    igb = (ib * 4 + itb) * 1024 + ch * 512
                                    P.op("vector", lambda h, iga=iga, ch=ch: h.tensor_mul(
                                        A(h_all, iga, [[16384, 128], [1024, 2], [1, 512]]),
                                        tbb[:],
                                        A(rb, ch * 512, [[1024, 128], [0, 2], [1, 512]])),
                                        inc=(s_h, 1))
                            wq_i = fcnt['c_w2']; fcnt['c_w2'] += 1
                            if wq_i >= 3:
                                be = (wq_i - 3) // 4
                                qq = (wq_i - 3) % 4
                                P.wait("sync", s_py, be * 16 + 4 * qq + 4)
                            ds_w2.issue(P, "sync", lambda h, wq_i=wq_i, e=e, ib=ib:
                                        h.dma_start(out=w2b[wq_i % 3][:],
                                                    in_=A(w2i_d, (e * 4 + ib) * 128 * 4096,
                                                          [[4096, 128], [1, 4096]])))
                        if DEV_PHASE < 4:
                            continue
                        P.wait("tensor", s_h, s_h.n)
                        wq_first = fcnt["c_w2"] - 4
                        for ft in range(8):
                            for ch in range(2):
                                ny = fcnt['c_y']; fcnt['c_y'] += 1
                                bank = ny % 2
                                q = ft // 2
                                ds_w2.wait_done(P, "tensor", wq_first + q)
                                P.wait("tensor", s_yev, ny - 1)
                                for it in range(16):
                                    P.op("tensor", lambda h, it=it, ft=ft, ch=ch, bank=bank, q=q, wq_first=wq_first:
                                         h.matmul(py[bank][:],
                                                  w2b[(wq_first + q) % 3][:, it * 256 + (ft % 2) * 128:
                                                                          it * 256 + (ft % 2) * 128 + 128],
                                                  A(h_all, it * 1024 + ch * 512,
                                                    [[16384, 128], [1, 512]]),
                                                  start=(it == 0), stop=(it == 15)),
                                         inc=(s_py, 1) if it == 15 else None)
                                P.wait("vector", s_py, ny + 1)
                                dsl = A(yac, ft * 1024 + ch * 512, [[8192, 128], [1, 512]])
                                if e == 0:
                                    P.op("vector", lambda h, dsl=dsl, bank=bank:
                                         h.tensor_copy(dsl, py[bank][:]), inc=(s_yev, 1))
                                else:
                                    P.op("vector", lambda h, dsl=dsl, bank=bank:
                                         h.tensor_add(dsl, dsl, py[bank][:]), inc=(s_yev, 1))
                        if e == 5 and g + 1 < DEV_NGRP:
                            emit_ln_group(g + 1)
                    if DEV_PHASE < 3 and g + 1 < DEV_NGRP:
                        emit_ln_group(g + 1)
                    h_marks.append(sem(name + "pu").n)
                    rb_marks.append(sem(name + "prb").n)
                    P.wait("tensor", s_yev, s_yev.n)
                    for t in (range(8) if DEV_PHASE >= 5 else []):
                        nr = fcnt['c_res']; fcnt['c_res'] += 1
                        par = nr % 2
                        P.wait("sync", s_resc, nr - 1)
                        ixr = ds_xr.issue(P, "sync", lambda h, g=g, t=t, par=par:
                                          h.dma_start(out=xrb[par][:],
                                                      in_=A(src_d, (g * G + t * 128) * D,
                                                            [[D, 128], [1, D]])))
                        P.wait("tensor", s_resc, nr)
                        for ft in range(8):
                            P.op("tensor", lambda h, ft=ft, t=t: h.transpose(
                                py[ft // 4][:, (ft % 4) * 128:(ft % 4 + 1) * 128],
                                A(yac, ft * 1024 + t * 128, [[8192, 128], [1, 128]]),
                                identf[:]), inc=(s_ptres, 1) if ft == 7 else None)
                        P.wait("vector", s_ptres, s_ptres.n)
                        ds_xr.wait_done(P, "vector", ixr)
                        ds_st.wait_done(P, "vector", nr - 2)
                        P.op("vector", lambda h, par=par: h.tensor_add(
                            xob[par][:, 0:512], py[0][:, 0:512], xrb[par][:, 0:512]))
                        P.op("vector", lambda h, par=par: h.tensor_add(
                            xob[par][:, 512:1024], py[1][:, 0:512], xrb[par][:, 512:1024]),
                            inc=(s_resc, 1))
                        P.wait("sync", s_resc, nr + 1)
                        ds_st.issue(P, "sync", lambda h, g=g, t=t, par=par:
                                    h.dma_start(out=A(dst_d, (g * G + t * 128) * D,
                                                      [[D, 128], [1, D]]),
                                                in_=xob[par][:]))
                    P.wait("vector", s_ptres, s_ptres.n)
                if DEV_DUMP:
                    ds_dbg = DStream(sem, name + "dbg", 4)
                    P.wait("gpsimd", sem(name + "yev"), sem(name + "yev").n)
                    P.wait("gpsimd", sem(name + "trc"), sem(name + "trc").n)
                    P.wait("gpsimd", sem(name + "rtc"), sem(name + "rtc").n)
                    P.wait("gpsimd", sem(name + "h"), sem(name + "h").n)
                    ds_dbg.issue(P, "gpsimd", lambda h: h.dma_start(
                        out=A(dst_d, 0, [[D, 128], [1, 1024]]), in_=ctx["xlnT"][:, 0:1024]))
                    ds_dbg.issue(P, "gpsimd", lambda h: h.dma_start(
                        out=A(dst_d, 128 * D, [[D, 8], [1, 1024]]), in_=ctx["rT"][:, 0:1024]))
                    ds_dbg.issue(P, "gpsimd", lambda h: h.dma_start(
                        out=A(dst_d, 200 * D, [[D, 128], [1, 1024]]), in_=yac[:, 0:1024]))
                    ds_dbg.issue(P, "gpsimd", lambda h: h.dma_start(
                        out=A(dst_d, 400 * D, [[D, 128], [1, 1024]]), in_=h_all[:, 0:1024]))
                    ds_st.sems.extend(ds_dbg.sems)
                P.flush(nc)
            return ds_st

        def stream_gates(ds):
            return [(s, s.n) for s in ds.sems]

        gates = []
        if do_f1:
            f1_st = ffn_stage("f", "f1", x_d, f1_dst, gates)
            gates = stream_gates(f1_st)

        if do_attn:
            P = Prog()
            with ExitStack() as ar:
                sb = lambda n, shp, dt: ar.enter_context(nc.sbuf_tensor("a" + n, shp, dt))
                ps = lambda n, shp, dt: ar.enter_context(nc.psum_tensor("a" + n, shp, dt))
                ctx = {"pfx": "a"}
                ctx["xb"] = [sb("xb0", [128, 1024], F32), sb("xb1", [128, 1024], F32)]
                ctx["sqb"] = sb("sqb", [128, 1024], F32)
                ctx["stats"] = sb("stats", [128, 16], F32)
                ctx["xlnb"] = [sb("xlnb0", [128, 1024], BF16), sb("xlnb1", [128, 1024], BF16)]
                xlnTs_a = [sb("xlnTa", [128, 2048], BF16), sb("xlnTb", [128, 2048], BF16)]
                ctx["xlw"] = 256
                x1b = [sb("x1b0", [128, 1024], F32), sb("x1b1", [128, 1024], F32)]
                qT = sb("qT", [128, 2048], BF16)
                kT = sb("kT", [128, 2048], BF16)
                gT = sb("gT", [128, 2048], F32)
                v_sb = [sb("v0", [128, 2048], BF16), sb("v1", [128, 2048], BF16)]
                wgs = [sb("wg0", [128, 2048], BF16), sb("wg1", [128, 2048], BF16)]
                o_raw = [sb("or0", [128, 2048], BF16), sb("or1", [128, 2048], BF16)]
                wpb = [sb("wpb0", [128, 4096], BF16), sb("wpb1", [128, 4096], BF16),
                       sb("wpb2", [128, 4096], BF16)]
                S_all = sb("S", [128, 8192], F32)
                NPB = 8
                NPS = 8
                B_sb = [sb(f"B{i}", [128, 256], F32) for i in range(NPB)]
                scl = [sb(f"scl{i}", [128, 8], F32) for i in range(NPB)]
                qs_sb = [sb(f"qs{i}", [128, 256], BF16) for i in range(NPB)]
                ks_sb = [sb(f"ks{i}", [128, 256], BF16) for i in range(NPB)]
                kdT_sb = [sb(f"kdT{i}", [128, 256], BF16) for i in range(NPB)]
                kd_sb = [sb(f"kd{i}", [128, 256], BF16) for i in range(NPB)]
                ATm = [sb(f"ATm{i}", [128, 128], BF16) for i in range(NPB)]
                Eq_sb = kdT_sb
                Ek_sb = kd_sb
                Sbf = [sb(f"Sbf{i}", [128, 1024], BF16) for i in range(NPS)]
                sq_scr = sb("sqscr", [128, 1024], F32)
                mh = sb("mh", [128, 16], F32)
                tmpb = sb("tmpb", [16, 256], BF16)
                sglb = [sb("sglb0", [128, 256], F32), sb("sglb1", [128, 256], F32)]
                sglb2 = [sb("sglb2a", [128, 256], F32), sb("sglb2b", [128, 256], F32)]
                oT_sb = sb("oT", [128, 2048], BF16)
                ofg = [sb("ofg0", [128, 2048], BF16), sb("ofg1", [128, 2048], BF16)]
                xob = [sb("xob0", [128, 1024], F32), sb("xob1", [128, 1024], F32)]
                ppj = [ps("ppj0", [128, 512], F32), ps("ppj1", [128, 512], F32)]
                pS = [ps("pS0", [128, 512], F32), ps("pS1", [128, 512], F32)]
                pO = [ps("pO0", [128, 512], F32), ps("pO1", [128, 512], F32)]
                ptr6 = ps("ptr6", [128, 1024], BF16)
                pout = ps("pout", [128, 512], F32)
                ctx["ptr6"] = ptr6
                ctx["pg0"] = None

                ds_x = DStream(sem, "a_x", 2)
                ctx["ds_x"] = ds_x
                ds_wp = DStream(sem, "a_wp", 3)
                ds_ax = DStream(sem, "a_axd", 2)
                ds_ast = DStream(sem, "a_std", 2)
                s_awo = sem("a_wo")
                s_pj = sem("a_pj"); s_pjf = sem("a_pjf")
                s_v1 = sem("a_v1"); s_s1 = sem("a_s1"); s_v2 = sem("a_v2")
                s_pkd = sem("a_pkd"); s_kdc = sem("a_kdc")
                s_pA = sem("a_pA"); s_atm = sem("a_atm")
                s_pO = sem("a_pO"); s_orms = sem("a_orms")
                s_pS = sem("a_pS"); s_Sup = sem("a_Sup")
                s_gl = sem("a_gl"); s_glx = sem("a_glx"); s_ep = sem("a_ep")
                s_eps = sem("a_eps"); s_epv = sem("a_epv")
                s_pot = sem("a_pot"); s_otc = sem("a_otc")
                s_pout = sem("a_pout"); s_xo = sem("a_xo")
                s_trc = sem("atrc")
                s_avw = sem("a_vw"); s_asw = sem("a_sw")
                ctx["ptr6_guards"] = [s_kdc, s_otc]

                ds_po.wait_all(P, "sync")
                ds_po.wait_all(P, "gpsimd")
                for gs, gv in gates:
                    for eng in ("sync", "tensor", "vector", "scalar"):
                        P.wait(eng, gs, gv)
                for eng in ("tensor", "vector", "scalar"):
                    P.wait(eng, s_c, 16 * NC_CONST)
                    P.wait(eng, sem("s_cg"), 16 * NC_CONSTG)

                c_pp = [0]
                c_pj = [0]
                a_tile = [0]
                n_cs = [0]
                e_tile = [0]
                piece_free = []

                def stream_piece(img, row):
                    i = c_pp[0]; c_pp[0] += 1
                    slot = wpb[i % 3]
                    if i >= 3:
                        fs, fv = piece_free[i - 3]
                        P.wait("gpsimd", fs, fv)
                    ds_wp.issue(P, "gpsimd", lambda h, slot=slot, img=img, row=row:
                                h.dma_start(out=slot[:],
                                            in_=A(img, row * 4096, [[4096, 128], [1, 4096]])))
                    return slot, i

                def piece_done():
                    piece_free.append((s_pjf, c_pj[0]))

                pjbanks = [ppj[0], ppj[1], pS[0], pS[1]]
                def next_pj():
                    u = c_pj[0]; c_pj[0] += 1
                    P.wait("tensor", s_pjf, u - 3)
                    return pjbanks[u % 4]

                def emit_attn_ln(p, w):
                    ctx["xlnT"] = xlnTs_a[w % 2]
                    for bi in range(2):
                        b = 2 * p + bi
                        P.wait("vector", s_pA, s_pA.n)
                        emit_ln_tile(P, ctx, at_src, b * T + w * WT, bi * 128,
                                     False, a_tile[0])
                        a_tile[0] += 1

                for p in range(DEV_NPASS):
                    P.wait("vector", s_Sup, s_Sup.n)
                    P.op("vector", lambda h: h.memzero(S_all[:]))
                    emit_attn_ln(p, 0)
                    for w in range(DEV_NWIN):
                        xlnT = xlnTs_a[w % 2]
                        P.wait("tensor", s_atm, s_atm.n)
                        P.wait("tensor", s_orms, s_orms.n)
                        P.wait("tensor", s_Sup, s_Sup.n)
                        P.wait("tensor", s_xo, s_xo.n)
                        P.wait("vector", s_eps, s_eps.n)
                        P.wait("scalar", s_ep, s_ep.n)
                        P.wait("tensor", s_trc, s_trc.n)
                        for (img, dstT) in ((wqi_d, qT), (wki_d, kT)):
                            for fp in range(2):
                                slot, wn = stream_piece(img, fp * 128)
                                ds_wp.wait_done(P, "tensor", wn)
                                for fl in range(4):
                                    ft = fp * 4 + fl
                                    pb = next_pj()
                                    for dt in range(8):
                                        P.op("tensor", lambda h, dt=dt, fl=fl, slot=slot, pb=pb, xlnT=xlnT:
                                             h.matmul(pb[:, 0:256],
                                                      slot[:, dt * 512 + fl * 128:
                                                           dt * 512 + (fl + 1) * 128],
                                                      xlnT[:, dt * 256:(dt + 1) * 256],
                                                      start=(dt == 0), stop=(dt == 7)),
                                             inc=(s_pj, 1) if dt == 7 else None)
                                    if ft % 2 == 0:
                                        P.wait("vector", s_pj, s_pj.n)
                                        P.op("vector", lambda h, ft=ft, dstT=dstT, pb=pb:
                                             h.tensor_copy(dstT[:, ft * 256:(ft + 1) * 256],
                                                           pb[:, 0:256]), inc=(s_pjf, 1))
                                    else:
                                        P.wait("scalar", s_pj, s_pj.n)
                                        P.op("scalar", lambda h, ft=ft, dstT=dstT, pb=pb:
                                             h.activation(dstT[:, ft * 256:(ft + 1) * 256],
                                                          pb[:, 0:256], AF.Copy),
                                             inc=(s_pjf, 1))
                                piece_done()
                        for (img, dsts, act) in ((wvi_d, v_sb, False), (wgi_d, wgs, True)):
                            for fp in range(4):
                                slot, wn = stream_piece(img, fp * 128)
                                ds_wp.wait_done(P, "tensor", wn)
                                for bi in range(2):
                                    pb = next_pj()
                                    for dt in range(8):
                                        P.op("tensor", lambda h, dt=dt, bi=bi, slot=slot, pb=pb, xlnT=xlnT:
                                             h.matmul(pb[:],
                                                      xlnT[:, dt * 256 + bi * 128:
                                                           dt * 256 + bi * 128 + 128],
                                                      slot[:, dt * 512:(dt + 1) * 512],
                                                      start=(dt == 0), stop=(dt == 7)),
                                             inc=(s_pj, 1) if dt == 7 else None)
                                    if act:
                                        P.wait("scalar", s_pj, s_pj.n)
                                        P.op("scalar", lambda h, bi=bi, fp=fp, dsts=dsts, pb=pb:
                                             h.activation(dsts[bi][:, fp * 512:(fp + 1) * 512],
                                                          pb[:], AF.Silu), inc=(s_pjf, 1))
                                    else:
                                        P.wait("vector", s_pj, s_pj.n)
                                        P.op("vector", lambda h, bi=bi, fp=fp, dsts=dsts, pb=pb:
                                             h.tensor_copy(dsts[bi][:, fp * 512:(fp + 1) * 512],
                                                           pb[:]), inc=(s_pjf, 1))
                                piece_done()
                        pb = next_pj()
                        for dt in range(8):
                            P.op("tensor", lambda h, dt=dt, pb=pb, xlnT=xlnT: h.matmul(
                                pb[0:16, 0:256], gk1_sb[:, dt * 16:(dt + 1) * 16],
                                xlnT[:, dt * 256:(dt + 1) * 256],
                                start=(dt == 0), stop=(dt == 7)),
                                inc=(s_pj, 1) if dt == 7 else None)
                        P.wait("vector", s_pj, s_pj.n)
                        P.op("vector", lambda h, pb=pb: h.tensor_copy(tmpb[:], pb[0:16, 0:256]),
                             inc=(s_pjf, 1))
                        P.wait("tensor", s_pjf, s_pjf.n)
                        for ft in range(8):
                            pb = next_pj()
                            P.op("tensor", lambda h, ft=ft, pb=pb: h.matmul(
                                pb[:, 0:256], gk2_sb[:, ft * 128:(ft + 1) * 128], tmpb[:],
                                start=True, stop=True), inc=(s_pj, 1))
                            P.wait("scalar", s_pj, s_pj.n)
                            P.wait("scalar", s_gl, s_gl.n - 1)
                            P.op("scalar", lambda h, ft=ft, pb=pb: h.activation(
                                sglb[ft % 2][:], pb[:, 0:256], AF.Sigmoid,
                                bias=gkbT_sb[:, ft:ft + 1]), inc=(s_pjf, 1))
                            P.wait("scalar", s_glx, s_glx.n - 1)
                            P.op("scalar", lambda h, ft=ft: h.activation(
                                sglb2[ft % 2][:], sglb[ft % 2][:], AF.Ln), inc=(s_gl, 1))
                            P.wait("vector", s_gl, s_gl.n)
                            P.op("vector", lambda h, ft=ft: h.tensor_scalar(
                                gT[:, ft * 256:(ft + 1) * 256], sglb2[ft % 2][:], 1.0 / 16.0,
                                None, ALU.mult), inc=(s_glx, 1))
                        P.wait("tensor", s_pjf, s_pjf.n)
                        steps = [(bi, hh) for bi in range(2) for hh in range(H)]
                        base_n = n_cs[0]

                        def st_va(si):
                            bi, hh = steps[si]
                            n = base_n + si
                            par = n % NPB
                            tok0 = bi * 128
                            Bp = B_sb[par]
                            P.wait("vector", s_s1, n - (NPB - 1))
                            for fl in range(2):
                                off = (2 * hh + fl) * 256 + tok0
                                P.op("vector", lambda h, fl=fl, off=off, Bp=Bp:
                                     h.tensor_tensor_scan(
                                         Bp[:, fl * 128:fl * 128 + 128], ones64[:],
                                         gT[:, off:off + C], 0.0, ALU.mult, ALU.add),
                                     inc=(s_v1, 1) if fl == 1 else None)

                        def st_sc(si):
                            n = base_n + si
                            par = n % NPB
                            Bp = B_sb[par]; sc = scl[par]
                            P.wait("scalar", s_v1, n + 1)
                            P.wait("scalar", s_Sup, n - (NPB - 1))
                            P.wait("scalar", s_v2, n - (NPB - 1))
                            P.op("scalar", lambda h, Bp=Bp, sc=sc: h.activation(
                                sc[:, 0:2], A(Bp, 63, [[256, 128], [128, 2], [1, 1]]),
                                AF.Copy, scale=-1.0, bias=-LN16))
                            P.op("scalar", lambda h, Bp=Bp, sc=sc: h.activation(
                                sc[:, 2:4], A(Bp, 63, [[256, 128], [128, 2], [1, 1]]),
                                AF.Copy, scale=-1.0), inc=(s_asw, 1))
                            P.wait("scalar", s_asw, s_asw.n)
                            P.op("scalar", lambda h, Bp=Bp, sc=sc: h.activation(
                                sc[:, 4:6], A(Bp, 127, [[256, 128], [128, 2], [1, 1]]),
                                AF.Exp))
                            P.op("scalar", lambda h, Bp=Bp, sc=sc: h.activation(
                                sc[:, 6:8], A(Bp, 63, [[256, 128], [128, 2], [1, 1]]),
                                AF.Exp))
                            for fl in range(2):
                                P.op("scalar", lambda h, fl=fl, Bp=Bp, sc=sc, par=par:
                                     h.activation(Eq_sb[par][:, fl * 128:fl * 128 + 128],
                                                  Bp[:, fl * 128:fl * 128 + 128], AF.Exp,
                                                  bias=sc[:, fl:fl + 1]))
                                P.op("scalar", lambda h, fl=fl, Bp=Bp, par=par:
                                     h.activation(Ek_sb[par][:, fl * 128:fl * 128 + 128],
                                                  Bp[:, fl * 128:fl * 128 + 128], AF.Exp,
                                                  bias=Bp[:, fl * 128 + 63:fl * 128 + 64],
                                                  scale=-1.0))
                            P.wait("scalar", s_asw, s_asw.n)
                            P.op("scalar", lambda h, Bp=Bp, sc=sc: h.activation(
                                sc[:, 0:2], A(Bp, 127, [[256, 128], [128, 2], [1, 1]]),
                                AF.Exp, bias=sc[:, 2:3]), inc=(s_asw, 1))
                            P.op("scalar", lambda h, Bp=Bp, sc=sc: h.activation(
                                sc[:, 1:2], Bp[:, 255:256],
                                AF.Exp, bias=sc[:, 3:4]), inc=(s_s1, 1))

                        def st_vb(si):
                            bi, hh = steps[si]
                            n = base_n + si
                            par = n % NPB
                            parS = n % NPS
                            bh = bi * H + hh
                            tok0 = bi * 128
                            sc = scl[par]
                            P.wait("vector", s_s1, n + 1)
                            P.wait("vector", s_pO, n - (NPS - 1))
                            P.wait("vector", s_pS, n - (NPB - 1))
                            qoff = 2 * hh * 256 + tok0
                            P.op("vector", lambda h, qoff=qoff, par=par: h.tensor_mul(
                                A(qs_sb[par], 0, [[256, 128], [128, 2], [1, 128]]),
                                A(qT, qoff, [[2048, 128], [256, 2], [1, 128]]),
                                A(Eq_sb[par], 0, [[256, 128], [128, 2], [1, 128]])))
                            P.op("vector", lambda h, qoff=qoff, par=par: h.tensor_mul(
                                A(ks_sb[par], 0, [[256, 128], [128, 2], [1, 128]]),
                                A(kT, qoff, [[2048, 128], [256, 2], [1, 128]]),
                                A(Ek_sb[par], 0, [[256, 128], [128, 2], [1, 128]])),
                                inc=(s_avw, 1))
                            P.wait("vector", s_avw, s_avw.n)
                            for fl in range(2):
                                P.op("vector", lambda h, fl=fl, par=par, sc=sc:
                                     h.tensor_scalar(kdT_sb[par][:, fl * 128:fl * 128 + 128],
                                                     ks_sb[par][:, fl * 128:fl * 128 + 128],
                                                     sc[:, fl:fl + 1], None, ALU.mult))
                                P.op("vector", lambda h, fl=fl, parS=parS, bh=bh, sc=sc:
                                     h.tensor_scalar(
                                         Sbf[parS][:, fl * 512:(fl + 1) * 512],
                                         S_all[:, bh * 1024 + fl * 512:
                                               bh * 1024 + (fl + 1) * 512],
                                         sc[:, 6 + fl:7 + fl], None, ALU.mult),
                                     inc=(s_v2, 1) if fl == 1 else None)

                        def st_pe1(si):
                            n = base_n + si
                            par = n % NPB
                            kdo = (n % 2) * 256
                            P.wait("tensor", s_v2, n + 1)
                            P.wait("tensor", s_kdc, n)
                            P.wait("tensor", s_trc, s_trc.n)
                            P.wait("tensor", s_otc, s_otc.n)
                            for fl in range(2):
                                P.op("tensor", lambda h, fl=fl, par=par, kdo=kdo:
                                     h.transpose(
                                         ptr6[:, kdo + fl * 128:kdo + (fl + 1) * 128],
                                         kdT_sb[par][:, fl * 128:fl * 128 + 128],
                                         identb[:]), inc=(s_pkd, 1) if fl == 1 else None)
                            P.wait("scalar", s_pkd, s_pkd.n)
                            P.wait("scalar", s_pS, n - (NPB - 1))
                            P.op("scalar", lambda h, par=par, kdo=kdo: h.activation(
                                kd_sb[par][:], ptr6[:, kdo:kdo + 256], AF.Copy),
                                inc=(s_kdc, 1))
                            P.wait("tensor", s_atm, n)
                            for fl in range(2):
                                P.op("tensor", lambda h, fl=fl, par=par: h.matmul(
                                    pout[:, 0:128],
                                    ks_sb[par][:, fl * 128:fl * 128 + 128],
                                    qs_sb[par][:, fl * 128:fl * 128 + 128],
                                    start=(fl == 0), stop=(fl == 1)),
                                    inc=(s_pA, 1) if fl == 1 else None)
                            P.wait("vector", s_pA, s_pA.n)
                            P.wait("vector", s_pO, n - (NPB - 1))
                            P.op("vector", lambda h, par=par: h.tensor_mul(
                                ATm[par][:], pout[:, 0:128], maskU[:]),
                                inc=(s_atm, 1))

                        def st_pe2(si):
                            bi, hh = steps[si]
                            n = base_n + si
                            par = n % NPB
                            parS = n % NPS
                            par2 = n % 2
                            bh = bi * H + hh
                            sc = scl[par]
                            pSp = pS if par2 == 0 else ppj
                            P.wait("tensor", s_atm, n + 1)
                            P.wait("tensor", s_orms, n - 1)
                            P.op("tensor", lambda h, par=par, par2=par2, bi=bi, hh=hh:
                                 h.matmul(pO[par2][:], ATm[par][:],
                                          v_sb[bi][:, hh * 512:(hh + 1) * 512],
                                          start=True, stop=False))
                            for fl in range(2):
                                P.op("tensor", lambda h, fl=fl, par=par, parS=parS, par2=par2:
                                     h.matmul(pO[par2][:],
                                              qs_sb[par][:, fl * 128:fl * 128 + 128],
                                              Sbf[parS][:, fl * 512:(fl + 1) * 512],
                                              start=False, stop=(fl == 1)),
                                     inc=(s_pO, 1) if fl == 1 else None)
                            P.wait("tensor", s_kdc, n + 1)
                            P.wait("tensor", s_Sup, n - 1)
                            for fl in range(2):
                                P.op("tensor", lambda h, fl=fl, par=par, pSp=pSp, bi=bi, hh=hh:
                                     h.matmul(pSp[fl][:],
                                              kd_sb[par][:, fl * 128:(fl + 1) * 128],
                                              v_sb[bi][:, hh * 512:(hh + 1) * 512],
                                              start=True, stop=True),
                                     inc=(s_pS, 1) if fl == 1 else None)
                            P.wait("scalar", s_pO, n + 1)
                            P.op("scalar", lambda h, par2=par2, bi=bi, hh=hh:
                                 h.activation(
                                     o_raw[bi][:, hh * 512:(hh + 1) * 512],
                                     pO[par2][:], AF.Copy), inc=(s_orms, 1))
                            P.wait("vector", s_pS, n + 1)
                            for fl in range(2):
                                P.op("vector", lambda h, fl=fl, par=par, pSp=pSp, bh=bh, sc=sc:
                                     h.scalar_tensor_tensor(
                                         S_all[:, bh * 1024 + fl * 512:
                                               bh * 1024 + (fl + 1) * 512],
                                         S_all[:, bh * 1024 + fl * 512:
                                               bh * 1024 + (fl + 1) * 512],
                                         sc[:, 4 + fl:5 + fl],
                                         pSp[fl][:], ALU.mult, ALU.add),
                                     inc=(s_Sup, 1) if fl == 1 else None)

                        HB = 8
                        for hb0 in range(0, len(steps), HB):
                            batch = range(hb0, min(hb0 + HB, len(steps)))
                            for si in batch:
                                st_va(si)
                            for si in batch:
                                st_sc(si)
                            for si in batch:
                                st_vb(si)
                            for si in batch:
                                st_pe1(si)
                            for si in batch:
                                st_pe2(si)
                        n_cs[0] += len(steps)
                        P.wait("vector", s_orms, s_orms.n)
                        P.wait("scalar", s_orms, s_orms.n)
                        for bi in range(2):
                            for half in range(2):
                                P.wait("scalar", s_epv, s_epv.n)
                                P.op("scalar", lambda h, bi=bi, half=half: h.activation(
                                    sq_scr[:], o_raw[bi][:, half * 1024:(half + 1) * 1024],
                                    AF.Square), inc=(s_eps, 1))
                                P.wait("vector", s_eps, s_eps.n)
                                for j in range(2):
                                    hh2 = half * 2 + j
                                    P.op("vector", lambda h, hh2=hh2, j=j: h.reduce_sum(
                                        mh[:, hh2:hh2 + 1],
                                        sq_scr[:, j * 512:(j + 1) * 512], axis=AX.X),
                                        inc=(s_avw, 1) if j == 0 else (s_epv, 1))
                            P.wait("vector", s_epv, s_epv.n)
                            P.op("vector", lambda h: h.tensor_scalar(
                                mh[:, 4:8], mh[:, 0:4], 1.0 / 512.0, EPS,
                                ALU.mult, ALU.add), inc=(s_epv, 1))
                            P.wait("scalar", s_epv, s_epv.n)
                            P.op("scalar", lambda h: h.activation(mh[:, 8:12], mh[:, 4:8],
                                                                  AF.Sqrt), inc=(s_eps, 1))
                            P.wait("vector", s_eps, s_eps.n)
                            P.op("vector", lambda h: h.reciprocal(out=mh[:, 12:16],
                                                                  in_=mh[:, 8:12]))
                            for hh2 in range(4):
                                P.op("vector", lambda h, hh2=hh2, bi=bi: h.tensor_scalar(
                                    ofg[bi][:, hh2 * 512:(hh2 + 1) * 512],
                                    o_raw[bi][:, hh2 * 512:(hh2 + 1) * 512],
                                    mh[:, 12 + hh2:13 + hh2], None, ALU.mult))
                            P.op("vector", lambda h, bi=bi: h.tensor_mul(
                                A(ofg[bi], 0, [[2048, 128], [512, 4], [1, 512]]),
                                A(ofg[bi], 0, [[2048, 128], [512, 4], [1, 512]]),
                                A(rmsw_sb, 0, [[512, 128], [0, 4], [1, 512]])))
                            P.op("vector", lambda h, bi=bi: h.tensor_mul(
                                ofg[bi][:], ofg[bi][:], wgs[bi][:]), inc=(s_ep, 1))
                        for bi in range(2):
                            b = 2 * p + bi
                            k = e_tile[0]; e_tile[0] += 1
                            for rnd in range(2):
                                P.wait("tensor", s_ep, s_ep.n)
                                P.wait("tensor", s_otc, s_otc.n)
                                P.wait("tensor", s_kdc, s_kdc.n)
                                P.wait("tensor", s_trc, s_trc.n)
                                for j in range(8):
                                    fvt = rnd * 8 + j
                                    P.op("tensor", lambda h, fvt=fvt, j=j, bi=bi: h.transpose(
                                        ptr6[:, j * 128:(j + 1) * 128],
                                        ofg[bi][:, fvt * 128:(fvt + 1) * 128], identb[:]),
                                        inc=(s_pot, 1) if j == 7 else None)
                                P.wait("vector", s_pot, s_pot.n)
                                P.wait("vector", s_pout, s_pout.n)
                                P.op("vector", lambda h, rnd=rnd: h.tensor_copy(
                                    oT_sb[:, rnd * 1024:(rnd + 1) * 1024], ptr6[:]),
                                    inc=(s_otc, 1))
                            P.wait("sync", s_xo, s_xo.n)
                            iax = ds_ax.issue(P, "sync", lambda h, b=b, w=w, bi=bi:
                                              h.dma_start(out=x1b[bi][:],
                                                          in_=A(at_src, (b * T + w * WT) * D,
                                                                [[D, 128], [1, D]])))
                            P.wait("tensor", s_otc, s_otc.n)
                            P.wait("tensor", s_xo, s_xo.n)
                            P.wait("tensor", s_atm, s_atm.n)
                            P.wait("tensor", s_Sup, s_Sup.n)
                            pout2 = pS[0]
                            for wp in range(4):
                                slot, wn = stream_piece(woi_d, wp * 128)
                                ds_wp.wait_done(P, "tensor", wn)
                                for fl in range(4):
                                    fvt = wp * 4 + fl
                                    for fch in range(2):
                                        P.op("tensor", lambda h, fvt=fvt, fl=fl, fch=fch, slot=slot:
                                             h.matmul((pout if fch == 0 else pout2)[:],
                                                      oT_sb[:, fvt * 128:(fvt + 1) * 128],
                                                      slot[:, fl * 1024 + fch * 512:
                                                           fl * 1024 + fch * 512 + 512],
                                                      start=(fvt == 0), stop=(fvt == 15)),
                                             inc=(s_pout, 1) if (fl == 3 and fch == 1) else None)
                                piece_free.append((s_pout, s_pout.n))
                            for fch in range(2):
                                P.wait("vector", s_pout, s_pout.n)
                                ds_ax.wait_done(P, "vector", iax)
                                ds_ast.wait_done(P, "vector", k - 2)
                                P.op("vector", lambda h, bi=bi, fch=fch: h.tensor_add(
                                    xob[bi][:, fch * 512:(fch + 1) * 512],
                                    (pout if fch == 0 else pout2)[:],
                                    x1b[bi][:, fch * 512:(fch + 1) * 512]),
                                    inc=(s_xo, 1))
                            P.wait("sync", s_xo, s_xo.n)
                            ds_ast.issue(P, "sync", lambda h, b=b, w=w, bi=bi:
                                         h.dma_start(out=A(at_dst, (b * T + w * WT) * D,
                                                           [[D, 128], [1, D]]),
                                                     in_=xob[bi][:]))
                        if w + 1 < DEV_NWIN:
                            emit_attn_ln(p, w + 1)
                P.flush(nc)
            gates = stream_gates(ds_ast)

        if do_f2:
            f2_st = ffn_stage("f", "f2", f2_src, y_d, gates)
            gates = stream_gates(f2_st)

        P = Prog()
        for gs, gv in gates:
            P.wait("sync", gs, gv)
        P.flush(nc)

    return nc


def make_inputs(L0_attn_Wq, L0_attn_Wk, L0_attn_Wv, L0_attn_Wg,
                L0_attn_gk_w1, L0_attn_gk_w2, L0_attn_gk_b, L0_attn_Wo,
                L0_attn_rms_w, L0_ffn_router_w, L0_ffn_w13, L0_ffn_w2):
    f32 = np.float32
    ac = np.ascontiguousarray
    consts = dict(
        rwT=ac(np.asarray(L0_ffn_router_w, f32).T),
        Wq=ac(np.asarray(L0_attn_Wq, f32)),
        Wk=ac(np.asarray(L0_attn_Wk, f32)),
        Wv=ac(np.asarray(L0_attn_Wv, f32)),
        Wg=ac(np.asarray(L0_attn_Wg, f32)),
        Wo=ac(np.asarray(L0_attn_Wo, f32)),
        gk1=ac(np.asarray(L0_attn_gk_w1, f32)),
        gk2=ac(np.asarray(L0_attn_gk_w2, f32)),
        gkbT=ac(np.asarray(L0_attn_gk_b, f32).reshape(8, 128).T),
        rmswB=ac(np.broadcast_to(np.asarray(L0_attn_rms_w, f32), (128, 512))),
        w13=ac(np.asarray(L0_ffn_w13, f32)),
        w2=ac(np.asarray(L0_ffn_w2, f32)),
        identb=np.eye(128, dtype=ml_dtypes.bfloat16),
        identf=np.eye(128, dtype=f32),
        maskU=ac(np.triu(np.ones((128, 128), f32))),
        ones64=np.ones((128, 128), f32),
    )
    oh = np.zeros((8, 1024), ml_dtypes.bfloat16)
    for e in range(8):
        oh[e, e * 128:(e + 1) * 128] = 1
    consts["oneh"] = oh
    return consts


def run_full(x_full, consts, do_f1=True, do_attn=True, do_f2=True):
    x = np.ascontiguousarray(np.asarray(x_full, np.float32)).reshape(32 * T, D)
    nc = build_program(do_f1=do_f1, do_attn=do_attn, do_f2=do_f2)
    in_maps = []
    for c in range(NCORES):
        m = dict(consts)
        m["x"] = np.ascontiguousarray(x[c * NTOK:(c + 1) * NTOK])
        in_maps.append(m)
    res = run_bass_kernel_spmd(nc, in_maps[:DEV_NCORES], core_ids=list(range(DEV_NCORES)))
    out = np.concatenate([r["y"] for r in res.results], axis=0)
    if DEV_NCORES < NCORES:
        pad = np.zeros((NTOK * (NCORES - DEV_NCORES), D), np.float32)
        out = np.concatenate([out, pad], axis=0)
    return out.reshape(32, T, D)


def kernel(L0_attn_x, L0_attn_Wq, L0_attn_Wk, L0_attn_Wv, L0_attn_Wg,
           L0_attn_gk_w1, L0_attn_gk_w2, L0_attn_gk_b, L0_attn_Wo, L0_attn_rms_w,
           L0_ffn_router_w, L0_ffn_w13, L0_ffn_w2):
    consts = make_inputs(L0_attn_Wq, L0_attn_Wk, L0_attn_Wv, L0_attn_Wg,
                         L0_attn_gk_w1, L0_attn_gk_w2, L0_attn_gk_b, L0_attn_Wo,
                         L0_attn_rms_w, L0_ffn_router_w, L0_ffn_w13, L0_ffn_w2)
    return run_full(L0_attn_x, consts)


# version 18
# speedup vs baseline: 1.2579x; 1.0913x over previous
import numpy as np
import ml_dtypes
import concourse.bass as bass
import concourse.mybir as mybir
from concourse.bass_utils import run_bass_kernel_spmd
from contextlib import ExitStack

F32 = mybir.dt.float32
BF16 = mybir.dt.bfloat16
AF = mybir.ActivationFunctionType
ALU = mybir.AluOpType
AX = mybir.AxisListType

NCORES = 8
NB = 4
T = 2048
D = 1024
NTOK = NB * T
G = 1024
NGRP = NTOK // G
E = 8
I2 = 4096
II = 2048
H = 4
C = 128
WT = 128
NWIN = T // WT
LN16 = float(np.log(16.0))
DEV_NGRP = NGRP
DEV_NE = E
DEV_NCORES = NCORES
DEV_PHASE = 5
DEV_DUMP = False
DEV_NPASS = 2
DEV_NWIN = NWIN
EPS = 1e-5


class Sem:
    __slots__ = ("h", "n")
    def __init__(self, h):
        self.h = h
        self.n = 0


class Prog:
    def __init__(self):
        self.items = []
        self.last_wait = {}

    def op(self, eng, fn, inc=None, inc2=None):
        if inc is not None:
            inc[0].n += inc[1]
        if inc2 is not None:
            inc2[0].n += inc2[1]
        def run(h, fn=fn, inc=inc, inc2=inc2):
            ins = fn(h)
            if inc is not None:
                ins = ins.then_inc(inc[0].h, inc[1])
            if inc2 is not None:
                ins.then_inc(inc2[0].h, inc2[1])
        self.items.append((eng, run))

    def wait(self, eng, sem, val):
        if val <= 0:
            return
        key = (eng, id(sem))
        if self.last_wait.get(key, -1) >= val:
            return
        self.last_wait[key] = val
        self.items.append((eng, lambda h, s=sem, v=val: h.wait_ge(s.h, v)))

    def run(self, eng, handle):
        for e, fn in self.items:
            if e == eng:
                fn(handle)

    def flush(self, nc):
        with nc.Block() as blk:
            @blk.sync
            def _(h):
                self.run("sync", h)
            @blk.scalar
            def _(h):
                self.run("scalar", h)
            @blk.vector
            def _(h):
                self.run("vector", h)
            @blk.tensor
            def _(h):
                self.run("tensor", h)
            @blk.gpsimd
            def _(h):
                self.run("gpsimd", h)


def A(t, off, dims):
    return bass.AP(t, off, dims)


class DStream:
    def __init__(self, sem_fn, name, nbuf):
        self.sems = [sem_fn(name + str(j)) for j in range(nbuf)]
        self.nbuf = nbuf
        self.count = 0

    def issue(self, P, q, fn):
        i = self.count
        self.count += 1
        P.op(q, fn, inc=(self.sems[i % self.nbuf], 16))
        return i

    def wait_done(self, P, eng, i):
        if i < 0:
            return
        P.wait(eng, self.sems[i % self.nbuf], 16 * (i // self.nbuf + 1))

    def wait_all(self, P, eng):
        for j, s in enumerate(self.sems):
            P.wait(eng, s, s.n)


def build_program(do_f1=True, do_attn=True, do_f2=True):
    nc = bass.Bass("TRN2", target_bir_lowering=False, detect_race_conditions=False)

    x_d = nc.dram_tensor("x", [NTOK, D], F32, kind="ExternalInput")
    w13_d = nc.dram_tensor("w13", [E, D, I2], F32, kind="ExternalInput")
    w2_d = nc.dram_tensor("w2", [E, II, D], F32, kind="ExternalInput")
    rwT_d = nc.dram_tensor("rwT", [D, E], F32, kind="ExternalInput")
    wq_d = nc.dram_tensor("Wq", [D, D], F32, kind="ExternalInput")
    wk_d = nc.dram_tensor("Wk", [D, D], F32, kind="ExternalInput")
    wv_d = nc.dram_tensor("Wv", [D, 2048], F32, kind="ExternalInput")
    wg_d = nc.dram_tensor("Wg", [D, 2048], F32, kind="ExternalInput")
    wo_d = nc.dram_tensor("Wo", [2048, D], F32, kind="ExternalInput")
    gk1_d = nc.dram_tensor("gk1", [D, 16], F32, kind="ExternalInput")
    gk2_d = nc.dram_tensor("gk2", [16, D], F32, kind="ExternalInput")
    gkbT_d = nc.dram_tensor("gkbT", [128, 8], F32, kind="ExternalInput")
    rmswB_d = nc.dram_tensor("rmswB", [128, 512], F32, kind="ExternalInput")
    identb_d = nc.dram_tensor("identb", [128, 128], BF16, kind="ExternalInput")
    identf_d = nc.dram_tensor("identf", [128, 128], F32, kind="ExternalInput")
    mask_d = nc.dram_tensor("maskU", [128, 128], F32, kind="ExternalInput")
    oneh_d = nc.dram_tensor("oneh", [8, 1024], BF16, kind="ExternalInput")
    ones_d = nc.dram_tensor("ones64", [128, 128], F32, kind="ExternalInput")

    x1_d = nc.dram_tensor("x1s", [NTOK, D], F32, kind="Internal")
    x2_d = nc.dram_tensor("x2s", [NTOK, D], F32, kind="Internal")
    y_d = nc.dram_tensor("y", [NTOK, D], F32, kind="ExternalOutput")
    w13i_d = nc.dram_tensor("w13i", [E * 4 * 128, 8192], BF16, kind="Internal")
    w2i_d = nc.dram_tensor("w2i", [E * 4 * 128, 4096], BF16, kind="Internal")
    wqi_d = nc.dram_tensor("wqi", [2 * 128, 4096], BF16, kind="Internal")
    wki_d = nc.dram_tensor("wki", [2 * 128, 4096], BF16, kind="Internal")
    wvi_d = nc.dram_tensor("wvi", [4 * 128, 4096], BF16, kind="Internal")
    wgi_d = nc.dram_tensor("wgi", [4 * 128, 4096], BF16, kind="Internal")
    woi_d = nc.dram_tensor("woi", [4 * 128, 4096], BF16, kind="Internal")

    f1_dst = x1_d if (do_attn or do_f2) else y_d
    at_dst = x2_d if do_f2 else y_d
    at_src = x1_d if do_f1 else x_d
    f2_src = x2_d if do_attn else (x1_d if do_f1 else x_d)

    st = ExitStack()
    sems = {}
    def sem(name):
        if name not in sems:
            sems[name] = Sem(st.enter_context(nc.semaphore(name)))
        return sems[name]

    with st:
        identb = st.enter_context(nc.sbuf_tensor("c_identb", [128, 128], BF16))
        identf = st.enter_context(nc.sbuf_tensor("c_identf", [128, 128], F32))
        maskU = st.enter_context(nc.sbuf_tensor("c_maskU", [128, 128], F32))
        oneh = st.enter_context(nc.sbuf_tensor("c_oneh", [8, 1024], BF16))
        ones64 = st.enter_context(nc.sbuf_tensor("c_ones64", [128, 128], F32))
        rwT_sb = st.enter_context(nc.sbuf_tensor("rwT_sb", [128, 64], BF16))
        gk1_sb = st.enter_context(nc.sbuf_tensor("gk1_sb", [128, 128], BF16))
        gk2_sb = st.enter_context(nc.sbuf_tensor("gk2_sb", [16, 1024], BF16))
        gkbT_sb = st.enter_context(nc.sbuf_tensor("gkbT_sb", [128, 8], F32))
        rmsw_sb = st.enter_context(nc.sbuf_tensor("rmsw_sb", [128, 512], BF16))

        s_c = sem("s_c")

        P = Prog()
        for dst, src in ((identb, identb_d), (identf, identf_d), (maskU, mask_d),
                         (oneh, oneh_d), (ones64, ones_d), (gkbT_sb, gkbT_d)):
            P.op("sync", lambda h, dst=dst, src=src: h.dma_start(out=dst[:], in_=src.ap()),
                 inc=(s_c, 16))
        s_cg = sem("s_cg")
        P.op("gpsimd", lambda h: h.dma_start(
            out=rwT_sb[:], in_=A(rwT_d, 0, [[8, 128], [1024, 8], [1, 8]])), inc=(s_cg, 16))
        P.op("gpsimd", lambda h: h.dma_start(
            out=gk1_sb[:], in_=A(gk1_d, 0, [[16, 128], [2048, 8], [1, 16]])), inc=(s_cg, 16))
        P.op("gpsimd", lambda h: h.dma_start(out=gk2_sb[:], in_=gk2_d.ap()), inc=(s_cg, 16))
        P.op("gpsimd", lambda h: h.dma_start(out=rmsw_sb[:], in_=rmswB_d.ap()), inc=(s_cg, 16))
        NC_CONST = 6
        NC_CONSTG = 4

        ds_pi = DStream(sem, "s_pi", 2)
        ds_po = DStream(sem, "s_po", 2)
        with nc.sbuf_tensor("stg0", [128, 8192], BF16) as stg0, \
             nc.sbuf_tensor("stg1", [128, 8192], BF16) as stg1:
            stgs = (stg0, stg1)
            pieces = []
            for e in range(E):
                for ib in range(4):
                    for sect in range(2):
                        src = A(w13_d, e * D * I2 + sect * 2048 + ib * 512,
                                [[I2, 128], [128 * I2, 8], [1, 512]])
                        pieces.append((src, w13i_d, (e * 4 + ib) * 128, sect * 4096, 4096))
            for e in range(E):
                for q in range(4):
                    src = A(w2_d, e * II * D + q * 256,
                            [[D, 128], [128 * D, 16], [1, 256]])
                    pieces.append((src, w2i_d, (e * 4 + q) * 128, 0, 4096))
            for fp in range(2):
                pieces.append((A(wq_d, fp * 512, [[D, 128], [128 * D, 8], [1, 512]]),
                               wqi_d, fp * 128, 0, 4096))
            for fp in range(2):
                pieces.append((A(wk_d, fp * 512, [[D, 128], [128 * D, 8], [1, 512]]),
                               wki_d, fp * 128, 0, 4096))
            for fp in range(4):
                pieces.append((A(wv_d, fp * 512, [[2048, 128], [128 * 2048, 8], [1, 512]]),
                               wvi_d, fp * 128, 0, 4096))
            for fp in range(4):
                pieces.append((A(wg_d, fp * 512, [[2048, 128], [128 * 2048, 8], [1, 512]]),
                               wgi_d, fp * 128, 0, 4096))
            for fp in range(4):
                pieces.append((A(wo_d, fp * 4 * 128 * D, [[D, 128], [128 * D, 4], [1, 1024]]),
                               woi_d, fp * 128, 0, 4096))

            for i, (src, img, row, coloff, ncols) in enumerate(pieces):
                b = stgs[i % 2]
                ds_po.wait_done(P, "gpsimd", i - 2)
                counts = [d[1] for d in src.ap[1:]]
                strides = []
                acc = 1
                for cnt in reversed(counts):
                    strides.append(acc)
                    acc *= cnt
                strides = list(reversed(strides))
                dstap = A(b, 0, [[8192, 128]] + [[s, cn] for s, cn in zip(strides, counts)])
                ii = ds_pi.issue(P, "gpsimd", lambda h, dstap=dstap, src=src:
                                 h.dma_start(out=dstap, in_=src))
                ds_pi.wait_done(P, "sync", ii)
                rowlen = img.shape[1]
                dst = A(img, row * rowlen + coloff, [[rowlen, 128], [1, ncols]])
                ds_po.issue(P, "sync", lambda h, b=b, dst=dst, ncols=ncols:
                            h.dma_start(out=dst, in_=b[:, 0:ncols]))
            n_pieces = len(pieces)
            P.flush(nc)
        P = None

        def emit_ln_tile(P, ctx, src_d, row0, col0, do_router, tile_no, part=None):
            pfx = ctx["pfx"]
            VE = ctx.get("ln_eng", "vector")
            CE = ctx.get("lncp_eng", "vector")
            xb = ctx["xb"]; sqb = ctx["sqb"]; stats = ctx["stats"]
            xlnb = ctx["xlnb"]; xlnT = ctx["xlnT"]; ptr6 = ctx["ptr6"]
            ds_x = ctx["ds_x"]; s_sq = sem(pfx + "sq")
            s_va = sem(pfx + "va"); s_rs = sem(pfx + "rs")
            s_ln = sem(pfx + "ln"); s_ptr = sem(pfx + "ptr")
            s_trc = sem(pfx + "trc"); s_red = sem(pfx + "red")
            par = tile_no % 2
            xbp = xb[par]; xlp = xlnb[par]
            if part in (None, "math"):
                xeng = ctx.get("x_eng", "sync")
                P.wait(xeng, s_ln, s_ln.n - 1)
                ix = ds_x.issue(P, xeng, lambda h: h.dma_start(
                    out=xbp[:], in_=A(src_d, row0 * D, [[D, 128], [1, D]])))
                s_vw = sem(pfx + VE + "w")
                def vw(fn):
                    P.op(VE, fn, inc=(s_vw, 1))
                    P.wait(VE, s_vw, s_vw.n)
                ds_x.wait_done(P, VE, ix)
                P.wait(VE, s_red, s_red.n)
                P.op(VE, lambda h: h.tensor_mul(sqb[:], xbp[:], xbp[:]), inc=(s_sq, 1))
                ds_x.wait_done(P, "vector", ix)
                P.wait("vector", s_sq, s_sq.n)
                P.op("vector", lambda h: h.reduce_sum(stats[:, 0:1], xbp[:], axis=AX.X))
                P.op("vector", lambda h: h.reduce_sum(stats[:, 1:2], sqb[:], axis=AX.X),
                     inc=(s_red, 1))
                P.wait(VE, s_red, s_red.n)
                vw(lambda h: h.tensor_scalar(stats[:, 2:3], stats[:, 0:1],
                                             1.0 / D, None, ALU.mult))
                vw(lambda h: h.tensor_mul(stats[:, 3:4], stats[:, 2:3], stats[:, 2:3]))
                vw(lambda h: h.tensor_scalar(stats[:, 13:14], stats[:, 1:2],
                                             1.0 / D, None, ALU.mult))
                vw(lambda h: h.tensor_sub(stats[:, 4:5], stats[:, 13:14],
                                          stats[:, 3:4]))
                P.op(VE, lambda h: h.tensor_scalar(stats[:, 5:6], stats[:, 4:5],
                                                   EPS, None, ALU.add), inc=(s_va, 1))
                P.wait("scalar", s_va, s_va.n)
                P.op("scalar", lambda h: h.activation(stats[:, 6:7], stats[:, 5:6],
                                                      AF.Sqrt), inc=(s_rs, 1))
                P.wait(VE, s_rs, s_rs.n)
                P.op(VE, lambda h: h.reciprocal(out=stats[:, 7:8], in_=stats[:, 6:7]))
                lnfree = ctx.get(("lnfree", par))
                if lnfree is not None:
                    P.wait(VE, lnfree[0], lnfree[1])
                P.op(VE, lambda h: h.tensor_scalar(
                    xlp[:], xbp[:], stats[:, 2:3], stats[:, 7:8],
                    ALU.subtract, ALU.mult), inc=(s_ln, 1))
            if part in (None, "tr"):
                P.wait("tensor", s_ln, s_ln.n)
                P.wait("tensor", s_trc, s_trc.n)
                for g in ctx["ptr6_guards"]:
                    P.wait("tensor", g, g.n)
                for dt in range(8):
                    P.op("tensor", lambda h, dt=dt: h.transpose(
                        ptr6[:, dt * 128:(dt + 1) * 128], xlp[:, dt * 128:(dt + 1) * 128],
                        identb[:]), inc=(s_ptr, 1) if dt == 7 else None)
                ctx[("lnfree", par)] = (s_ptr, s_ptr.n)
                P.wait(CE, s_ptr, s_ptr.n)
                xlw = ctx["xlw"]
                xw = xlnT.shape[1]
                if CE == "scalar":
                    P.op(CE, lambda h: h.activation(
                        A(xlnT, col0, [[xw, 128], [xlw, 8], [1, 128]]),
                        A(ptr6, 0, [[1024, 128], [128, 8], [1, 128]]), AF.Copy),
                        inc=(s_trc, 1))
                else:
                    P.op(CE, lambda h: h.tensor_copy(
                        A(xlnT, col0, [[xw, 128], [xlw, 8], [1, 128]]),
                        A(ptr6, 0, [[1024, 128], [128, 8], [1, 128]])), inc=(s_trc, 1))
                if do_router:
                    emit_router_tile(P, ctx, col0, tile_no)

        def emit_router_tile(P, ctx, col0, tile_no):
            pfx = ctx["pfx"]
            VE = ctx.get("ln_eng", "vector")
            CE = ctx.get("lncp_eng", "vector")
            xlnT = ctx["xlnT"]; p_rt = ctx["p_rt"]; ptr6 = ctx["ptr6"]
            stats = ctx["stats"]; rpb = ctx["rpb"]; rT = ctx["rT"]
            qv = ctx["qv"]; qv2 = ctx["qv2"]; qv3 = ctx["qv3"]
            s_trc = sem(pfx + "trc"); s_prt = sem(pfx + "prt")
            s_t2 = sem(pfx + "t2"); s_rtc = sem(pfx + "rtc")
            s_ptr = sem(pfx + "ptr"); s_nm1 = sem(pfx + "nm1")
            s_qx = sem(pfx + "qx"); s_sw = sem(pfx + "sw")
            s_red = sem(pfx + "red"); s_gx = sem(pfx + "gx")
            s_vw = sem(pfx + VE + "w")
            def vw(fn):
                P.op(VE, fn, inc=(s_vw, 1))
                P.wait(VE, s_vw, s_vw.n)
            rto = (tile_no % 2) * 8
            P.wait("tensor", s_trc, s_trc.n)
            P.wait("tensor", s_t2, s_t2.n - 1)
            for dt in range(8):
                P.op("tensor", lambda h, dt=dt, col0=col0, rto=rto: h.matmul(
                    p_rt[:, rto:rto + 8],
                    A(xlnT, dt * 1024 + col0, [[xlnT.shape[1], 128], [1, 128]]),
                    rwT_sb[:, dt * 8:(dt + 1) * 8],
                    start=(dt == 0), stop=(dt == 7)),
                    inc=(s_prt, 1) if dt == 7 else None)
            P.wait("vector", s_prt, s_prt.n)
            P.op("vector", lambda h, rto=rto: h.reduce_max(
                stats[:, 8:9], p_rt[:, rto:rto + 8], axis=AX.X,
                negate=True), inc=(s_nm1, 1))
            P.wait("scalar", s_nm1, s_nm1.n)
            P.op("scalar", lambda h, rto=rto: h.activation(
                qv[:], p_rt[:, rto:rto + 8], AF.Exp,
                bias=stats[:, 8:9]), inc=(s_qx, 1))
            P.wait("vector", s_qx, s_qx.n)
            P.op("vector", lambda h: h.reduce_sum(stats[:, 9:10], qv[:], axis=AX.X),
                 inc=(s_red, 1))
            P.wait(VE, s_qx, s_qx.n)
            vw(lambda h: h.tensor_single_scalar(qv2[:], qv[:], 1.0, ALU.is_ge))
            P.op(VE, lambda h: h.tensor_sub(qv3[:], qv[:], qv2[:]), inc=(s_gx, 1))
            P.wait("vector", s_gx, s_gx.n)
            P.op("vector", lambda h: h.reduce_max(stats[:, 10:11], qv3[:], axis=AX.X),
                 inc=(s_red, 1))
            P.wait(VE, s_red, s_red.n)
            vw(lambda h: h.tensor_scalar(stats[:, 14:15], stats[:, 9:10],
                                         1e-6, None, ALU.mult))
            vw(lambda h: h.tensor_add(stats[:, 11:12], stats[:, 14:15],
                                      stats[:, 10:11]))
            vw(lambda h: h.tensor_scalar(stats[:, 11:12], stats[:, 11:12],
                                         2.0, 2.0, ALU.mult, ALU.add))
            vw(lambda h: h.reciprocal(out=stats[:, 12:13], in_=stats[:, 11:12]))
            vw(lambda h: h.tensor_scalar(qv2[:], qv[:], stats[:, 10:11],
                                         None, ALU.is_ge))
            vw(lambda h: h.tensor_mul(qv3[:], qv[:], qv2[:]))
            P.op(VE, lambda h: h.tensor_scalar(rpb[:], qv3[:], stats[:, 12:13],
                                               None, ALU.mult), inc=(s_t2, 1))
            P.wait("tensor", s_t2, s_t2.n)
            P.wait("tensor", s_rtc, s_rtc.n)
            P.op("tensor", lambda h: h.transpose(ptr6[0:8, 0:128], rpb[:], identb[:]),
                 inc=(s_ptr, 1))
            P.wait(CE, s_ptr, s_ptr.n)
            if CE == "scalar":
                P.op(CE, lambda h, col0=col0: h.activation(
                    rT[:, col0:col0 + 128], ptr6[0:8, 0:128], AF.Copy), inc=(s_rtc, 1))
            else:
                P.op(CE, lambda h, col0=col0: h.tensor_copy(
                    rT[:, col0:col0 + 128], ptr6[0:8, 0:128]), inc=(s_rtc, 1))

        fcnt = {"c_tile": 0, "c_w13": 0, "c_w2": 0, "c_h": 0, "c_y": 0, "c_res": 0}
        fds = {}

        def ffn_stage(name, tname, src_d, dst_d, gate_sems):
            P = Prog()
            with ExitStack() as ar:
                sb = lambda n, shp, dt: ar.enter_context(nc.sbuf_tensor(tname + n, shp, dt))
                ps = lambda n, shp, dt: ar.enter_context(nc.psum_tensor(tname + n, shp, dt))
                ctx = {"pfx": name}
                ctx["xb"] = [sb("xb0", [128, 1024], F32), sb("xb1", [128, 1024], F32)]
                ctx["sqb"] = sb("sqb", [128, 1024], F32)
                ctx["stats"] = sb("stats", [128, 16], F32)
                ctx["xlnb"] = [sb("xlnb0", [128, 1024], BF16), sb("xlnb1", [128, 1024], BF16)]
                xlnTs = [sb("xlnTa", [128, 8192], BF16), sb("xlnTb", [128, 8192], BF16)]
                ctx["xlw"] = 1024
                ctx["rpb"] = sb("rpb", [128, 8], BF16)
                rTs = [sb("rTa", [8, 1024], BF16), sb("rTb", [8, 1024], BF16)]
                rb = sb("rb", [128, 1024], BF16)
                ctx["ln_eng"] = "vector"
                ctx["lncp_eng"] = "vector"
                ctx["x_eng"] = "gpsimd"
                w13b = [sb("w13b0", [128, 8192], BF16), sb("w13b1", [128, 8192], BF16)]
                w2b = [sb("w2b0", [128, 4096], BF16), sb("w2b1", [128, 4096], BF16),
                       sb("w2b2", [128, 4096], BF16)]
                sgb = sb("sgb", [128, 1024], BF16)
                tbb = sb("tbb", [128, 1024], BF16)
                h_all = sb("h_all", [128, 16384], BF16)
                yac = sb("yac", [128, 8192], F32)
                xrb = [sb("xrb0", [128, 1024], F32), sb("xrb1", [128, 1024], F32)]
                xob = [sb("xob0", [128, 1024], F32), sb("xob1", [128, 1024], F32)]
                ctx["qv"] = sb("qv", [128, 8], F32)
                ctx["qv2"] = sb("qv2", [128, 8], F32)
                ctx["qv3"] = sb("qv3", [128, 8], F32)
                pgP = ps("pgP", [128, 1024], F32)
                puP = ps("puP", [128, 1024], F32)
                py = [ps("py0", [128, 512], F32), ps("py1", [128, 512], F32)]
                ctx["ptr6"] = ps("ptr6", [128, 1024], BF16)
                ctx["p_rt"] = ps("p_rt", [128, 16], F32)

                if not fds:
                    fds["x"] = DStream(sem, name + "x", 2)
                    fds["w13"] = DStream(sem, name + "w13", 2)
                    fds["w2"] = DStream(sem, name + "w2", 3)
                    fds["xr"] = DStream(sem, name + "xr", 2)
                    fds["st"] = DStream(sem, name + "st", 2)
                ds_x = fds["x"]; ds_w13 = fds["w13"]; ds_w2 = fds["w2"]
                ds_xr = fds["xr"]; ds_st = fds["st"]
                ctx["ds_x"] = ds_x
                s_pg = sem(name + "pg"); s_pu = sem(name + "pu")
                s_sg = sem(name + "sg"); s_puf = sem(name + "puf")
                s_h = sem(name + "h"); s_rbs = sem(name + "rbs"); s_prb = sem(name + "prb")
                s_py = sem(name + "py"); s_yev = sem(name + "yev")
                s_ptres = sem(name + "ptres")
                s_resc = sem(name + "resc")
                s_rtc = sem(name + "rtc"); s_t2 = sem(name + "t2")
                s_trc = sem(name + "trc")
                ctx["ptr6_guards"] = [s_rtc]

                ds_po.wait_all(P, "sync")
                for gs, gv in gate_sems:
                    for eng in ("sync", "gpsimd", "tensor", "vector", "scalar"):
                        P.wait(eng, gs, gv)
                for eng in ("tensor", "vector", "scalar"):
                    P.wait(eng, s_c, 16 * NC_CONST)
                    P.wait(eng, sem("s_cg"), 16 * NC_CONSTG)

                h_marks = [0]
                rb_marks = [0]

                def emit_ln_group(g):
                    gm = max(0, g - 1)
                    P.wait("scalar", sem(name + "pu"), h_marks[min(gm, len(h_marks) - 1)])
                    P.wait("scalar", sem(name + "prb"), rb_marks[min(gm, len(rb_marks) - 1)])
                    ctx["xlnT"] = xlnTs[g % 2]
                    ctx["rT"] = rTs[g % 2]
                    for t in range(8):
                        emit_ln_tile(P, ctx, src_d, g * G + t * 128, t * 128,
                                     DEV_PHASE >= 2, fcnt["c_tile"])
                        fcnt["c_tile"] += 1

                emit_ln_group(0)
                for g in range(DEV_NGRP):
                    xlnT = xlnTs[g % 2]
                    rT = rTs[g % 2]
                    for e in (range(DEV_NE) if DEV_PHASE >= 3 else []):
                        P.wait("tensor", s_rtc, s_rtc.n)
                        P.wait("tensor", s_yev, s_yev.n)
                        P.wait("tensor", s_resc, fcnt["c_res"])
                        for cch in range(2):
                            P.op("tensor", lambda h, cch=cch, e=e, rT=rT: h.matmul(
                                py[cch][:, 0:512], oneh[:, e * 128:(e + 1) * 128],
                                rT[:, cch * 512:(cch + 1) * 512],
                                start=True, stop=True), inc=(s_prb, 1) if cch == 1 else None)
                        P.wait("vector", s_prb, s_prb.n)
                        P.op("vector", lambda h: h.tensor_copy(rb[:, 0:512], py[0][:, 0:512]))
                        P.op("vector", lambda h: h.tensor_copy(rb[:, 512:1024], py[1][:, 0:512]),
                             inc=(s_rbs, 1))
                        P.wait("vector", s_py, s_py.n)
                        for ib in range(4):
                            wb = fcnt['c_w13']; fcnt['c_w13'] += 1
                            wslot = w13b[wb % 2]
                            P.wait("sync", s_pu, 4 * (wb - 1))
                            ds_w13.issue(P, "sync", lambda h, wslot=wslot, e=e, ib=ib:
                                         h.dma_start(out=wslot[:],
                                                     in_=A(w13i_d, (e * 4 + ib) * 128 * 8192,
                                                           [[8192, 128], [1, 8192]])))
                            for ch in range(2):
                                for ipair in range(2):
                                    nh = fcnt['c_h']; fcnt['c_h'] += 1
                                    ita, itb = 2 * ipair, 2 * ipair + 1
                                    ds_w13.wait_done(P, "tensor", wb)
                                    P.wait("tensor", s_sg, nh - 1)
                                    P.wait("tensor", s_puf, nh - 1)
                                    if e == 0:
                                        P.wait("tensor", s_t2, s_t2.n)
                                    for half, it in ((0, ita), (1, itb)):
                                        for dt in range(8):
                                            P.op("tensor", lambda h, dt=dt, it=it, half=half, ch=ch, wslot=wslot, xlnT=xlnT:
                                                 h.matmul(pgP[:, half * 512:(half + 1) * 512],
                                                          wslot[:, dt * 512 + it * 128:
                                                                dt * 512 + (it + 1) * 128],
                                                          xlnT[:, dt * 1024 + ch * 512:
                                                               dt * 1024 + (ch + 1) * 512],
                                                          start=(dt == 0), stop=(dt == 7)),
                                                 inc=(s_pg, 1) if (dt == 7 and half == 1) else None)
                                    for half, it in ((0, ita), (1, itb)):
                                        for dt in range(8):
                                            P.op("tensor", lambda h, dt=dt, it=it, half=half, ch=ch, wslot=wslot, xlnT=xlnT:
                                                 h.matmul(puP[:, half * 512:(half + 1) * 512],
                                                          wslot[:, 4096 + dt * 512 + it * 128:
                                                                4096 + dt * 512 + (it + 1) * 128],
                                                          xlnT[:, dt * 1024 + ch * 512:
                                                               dt * 1024 + (ch + 1) * 512],
                                                          start=(dt == 0), stop=(dt == 7)),
                                                 inc=(s_pu, 1) if (dt == 7 and half == 1) else None)
                                    P.wait("scalar", s_pg, nh + 1)
                                    P.wait("scalar", s_puf, nh - 1)
                                    P.op("scalar", lambda h: h.activation(
                                        sgb[:], pgP[:], AF.Silu), inc=(s_sg, 1))
                                    P.wait("vector", s_pu, nh + 1)
                                    P.wait("vector", s_sg, nh + 1)
                                    P.op("vector", lambda h: h.tensor_mul(
                                        tbb[:], puP[:], sgb[:]), inc=(s_puf, 1))
                                    P.wait("vector", s_rbs, s_rbs.n)
                                    iga = (ib * 4 + ita) * 1024 + ch * 512
                                    igb = (ib * 4 + itb) * 1024 + ch * 512
                                    P.op("vector", lambda h, iga=iga, ch=ch: h.tensor_mul(
                                        A(h_all, iga, [[16384, 128], [1024, 2], [1, 512]]),
                                        tbb[:],
                                        A(rb, ch * 512, [[1024, 128], [0, 2], [1, 512]])),
                                        inc=(s_h, 1))
                            wq_i = fcnt['c_w2']; fcnt['c_w2'] += 1
                            if wq_i >= 3:
                                be = (wq_i - 3) // 4
                                qq = (wq_i - 3) % 4
                                P.wait("sync", s_py, be * 16 + 4 * qq + 4)
                            ds_w2.issue(P, "sync", lambda h, wq_i=wq_i, e=e, ib=ib:
                                        h.dma_start(out=w2b[wq_i % 3][:],
                                                    in_=A(w2i_d, (e * 4 + ib) * 128 * 4096,
                                                          [[4096, 128], [1, 4096]])))
                        if DEV_PHASE < 4:
                            continue
                        P.wait("tensor", s_h, s_h.n)
                        wq_first = fcnt["c_w2"] - 4
                        for ft in range(8):
                            for ch in range(2):
                                ny = fcnt['c_y']; fcnt['c_y'] += 1
                                bank = ny % 2
                                q = ft // 2
                                ds_w2.wait_done(P, "tensor", wq_first + q)
                                P.wait("tensor", s_yev, ny - 1)
                                for it in range(16):
                                    P.op("tensor", lambda h, it=it, ft=ft, ch=ch, bank=bank, q=q, wq_first=wq_first:
                                         h.matmul(py[bank][:],
                                                  w2b[(wq_first + q) % 3][:, it * 256 + (ft % 2) * 128:
                                                                          it * 256 + (ft % 2) * 128 + 128],
                                                  A(h_all, it * 1024 + ch * 512,
                                                    [[16384, 128], [1, 512]]),
                                                  start=(it == 0), stop=(it == 15)),
                                         inc=(s_py, 1) if it == 15 else None)
                                P.wait("vector", s_py, ny + 1)
                                dsl = A(yac, ft * 1024 + ch * 512, [[8192, 128], [1, 512]])
                                if e == 0:
                                    P.op("vector", lambda h, dsl=dsl, bank=bank:
                                         h.tensor_copy(dsl, py[bank][:]), inc=(s_yev, 1))
                                else:
                                    P.op("vector", lambda h, dsl=dsl, bank=bank:
                                         h.tensor_add(dsl, dsl, py[bank][:]), inc=(s_yev, 1))
                        if e == 5 and g + 1 < DEV_NGRP:
                            emit_ln_group(g + 1)
                    if DEV_PHASE < 3 and g + 1 < DEV_NGRP:
                        emit_ln_group(g + 1)
                    h_marks.append(sem(name + "pu").n)
                    rb_marks.append(sem(name + "prb").n)
                    P.wait("tensor", s_yev, s_yev.n)
                    for t in (range(8) if DEV_PHASE >= 5 else []):
                        nr = fcnt['c_res']; fcnt['c_res'] += 1
                        par = nr % 2
                        P.wait("gpsimd", s_resc, nr - 1)
                        ixr = ds_xr.issue(P, "gpsimd", lambda h, g=g, t=t, par=par:
                                          h.dma_start(out=xrb[par][:],
                                                      in_=A(src_d, (g * G + t * 128) * D,
                                                            [[D, 128], [1, D]])))
                        P.wait("tensor", s_resc, nr)
                        for ft in range(8):
                            P.op("tensor", lambda h, ft=ft, t=t: h.transpose(
                                py[ft // 4][:, (ft % 4) * 128:(ft % 4 + 1) * 128],
                                A(yac, ft * 1024 + t * 128, [[8192, 128], [1, 128]]),
                                identf[:]), inc=(s_ptres, 1) if ft == 7 else None)
                        P.wait("vector", s_ptres, s_ptres.n)
                        ds_xr.wait_done(P, "vector", ixr)
                        ds_st.wait_done(P, "vector", nr - 2)
                        P.op("vector", lambda h, par=par: h.tensor_add(
                            xob[par][:, 0:512], py[0][:, 0:512], xrb[par][:, 0:512]))
                        P.op("vector", lambda h, par=par: h.tensor_add(
                            xob[par][:, 512:1024], py[1][:, 0:512], xrb[par][:, 512:1024]),
                            inc=(s_resc, 1))
                        P.wait("sync", s_resc, nr + 1)
                        ds_st.issue(P, "sync", lambda h, g=g, t=t, par=par:
                                    h.dma_start(out=A(dst_d, (g * G + t * 128) * D,
                                                      [[D, 128], [1, D]]),
                                                in_=xob[par][:]))
                    P.wait("vector", s_ptres, s_ptres.n)
                if DEV_DUMP:
                    ds_dbg = DStream(sem, name + "dbg", 4)
                    P.wait("gpsimd", sem(name + "yev"), sem(name + "yev").n)
                    P.wait("gpsimd", sem(name + "trc"), sem(name + "trc").n)
                    P.wait("gpsimd", sem(name + "rtc"), sem(name + "rtc").n)
                    P.wait("gpsimd", sem(name + "h"), sem(name + "h").n)
                    ds_dbg.issue(P, "gpsimd", lambda h: h.dma_start(
                        out=A(dst_d, 0, [[D, 128], [1, 1024]]), in_=ctx["xlnT"][:, 0:1024]))
                    ds_dbg.issue(P, "gpsimd", lambda h: h.dma_start(
                        out=A(dst_d, 128 * D, [[D, 8], [1, 1024]]), in_=ctx["rT"][:, 0:1024]))
                    ds_dbg.issue(P, "gpsimd", lambda h: h.dma_start(
                        out=A(dst_d, 200 * D, [[D, 128], [1, 1024]]), in_=yac[:, 0:1024]))
                    ds_dbg.issue(P, "gpsimd", lambda h: h.dma_start(
                        out=A(dst_d, 400 * D, [[D, 128], [1, 1024]]), in_=h_all[:, 0:1024]))
                    ds_st.sems.extend(ds_dbg.sems)
                P.flush(nc)
            return ds_st

        def stream_gates(ds):
            return [(s, s.n) for s in ds.sems]

        gates = []
        if do_f1:
            f1_st = ffn_stage("f", "f1", x_d, f1_dst, gates)
            gates = stream_gates(f1_st)

        if do_attn:
            P = Prog()
            with ExitStack() as ar:
                sb = lambda n, shp, dt: ar.enter_context(nc.sbuf_tensor("a" + n, shp, dt))
                ps = lambda n, shp, dt: ar.enter_context(nc.psum_tensor("a" + n, shp, dt))
                ctx = {"pfx": "a"}
                ctx["xb"] = [sb("xb0", [128, 1024], F32), sb("xb1", [128, 1024], F32)]
                ctx["sqb"] = sb("sqb", [128, 1024], F32)
                ctx["stats"] = sb("stats", [128, 16], F32)
                ctx["xlnb"] = [sb("xlnb0", [128, 1024], BF16), sb("xlnb1", [128, 1024], BF16)]
                xlnTs_a = [sb("xlnTa", [128, 2048], BF16), sb("xlnTb", [128, 2048], BF16)]
                ctx["xlw"] = 256
                x1b = [sb("x1b0", [128, 1024], F32), sb("x1b1", [128, 1024], F32)]
                qT = sb("qT", [128, 2048], BF16)
                kT = sb("kT", [128, 2048], BF16)
                gT = sb("gT", [128, 2048], F32)
                v_sb = [sb("v0", [128, 2048], BF16), sb("v1", [128, 2048], BF16)]
                wgs = [sb("wg0", [128, 2048], BF16), sb("wg1", [128, 2048], BF16)]
                o_raw = [sb("or0", [128, 2048], BF16), sb("or1", [128, 2048], BF16)]
                wpb = [sb("wpb0", [128, 4096], BF16), sb("wpb1", [128, 4096], BF16),
                       sb("wpb2", [128, 4096], BF16)]
                S_all = sb("S", [128, 8192], F32)
                NPB = 8
                NPS = 8
                B_sb = [sb(f"B{i}", [128, 256], F32) for i in range(NPB)]
                scl = [sb(f"scl{i}", [128, 8], F32) for i in range(NPB)]
                qs_sb = [sb(f"qs{i}", [128, 256], BF16) for i in range(NPB)]
                ks_sb = [sb(f"ks{i}", [128, 256], BF16) for i in range(NPB)]
                kdT_sb = [sb(f"kdT{i}", [128, 256], BF16) for i in range(NPB)]
                kd_sb = [sb(f"kd{i}", [128, 256], BF16) for i in range(NPB)]
                ATm = [sb(f"ATm{i}", [128, 128], BF16) for i in range(NPB)]
                Eq_sb = kdT_sb
                Ek_sb = kd_sb
                Sbf = [sb(f"Sbf{i}", [128, 1024], BF16) for i in range(NPS)]
                sq_scr = sb("sqscr", [128, 1024], F32)
                mh = sb("mh", [128, 16], F32)
                tmpb = sb("tmpb", [16, 256], BF16)
                sglb = [sb("sglb0", [128, 256], F32), sb("sglb1", [128, 256], F32)]
                sglb2 = [sb("sglb2a", [128, 256], F32), sb("sglb2b", [128, 256], F32)]
                oT_sb = sb("oT", [128, 2048], BF16)
                ofg = [sb("ofg0", [128, 2048], BF16), sb("ofg1", [128, 2048], BF16)]
                xob = [sb("xob0", [128, 1024], F32), sb("xob1", [128, 1024], F32)]
                ppj = [ps("ppj0", [128, 512], F32), ps("ppj1", [128, 512], F32)]
                pS = [ps("pS0", [128, 512], F32), ps("pS1", [128, 512], F32)]
                pO = [ps("pO0", [128, 512], F32), ps("pO1", [128, 512], F32)]
                ptr6 = ps("ptr6", [128, 1024], BF16)
                pout = ps("pout", [128, 512], F32)
                ctx["ptr6"] = ptr6
                ctx["pg0"] = None

                ds_x = DStream(sem, "a_x", 2)
                ctx["ds_x"] = ds_x
                ds_wp = DStream(sem, "a_wp", 3)
                ds_ax = DStream(sem, "a_axd", 2)
                ds_ast = DStream(sem, "a_std", 2)
                s_awo = sem("a_wo")
                s_pj = sem("a_pj"); s_pjf = sem("a_pjf")
                s_v1 = sem("a_v1"); s_s1 = sem("a_s1"); s_v2 = sem("a_v2")
                s_pkd = sem("a_pkd"); s_kdc = sem("a_kdc")
                s_pA = sem("a_pA"); s_atm = sem("a_atm")
                s_pO = sem("a_pO"); s_orms = sem("a_orms")
                s_pS = sem("a_pS"); s_Sup = sem("a_Sup")
                s_gl = sem("a_gl"); s_glx = sem("a_glx"); s_ep = sem("a_ep")
                s_eps = sem("a_eps"); s_epv = sem("a_epv")
                s_pot = sem("a_pot"); s_otc = sem("a_otc")
                s_pout = sem("a_pout"); s_xo = sem("a_xo")
                s_trc = sem("atrc")
                s_avw = sem("a_vw"); s_asw = sem("a_sw")
                ctx["ptr6_guards"] = [s_kdc, s_otc]

                ds_po.wait_all(P, "sync")
                ds_po.wait_all(P, "gpsimd")
                for gs, gv in gates:
                    for eng in ("sync", "tensor", "vector", "scalar"):
                        P.wait(eng, gs, gv)
                for eng in ("tensor", "vector", "scalar"):
                    P.wait(eng, s_c, 16 * NC_CONST)
                    P.wait(eng, sem("s_cg"), 16 * NC_CONSTG)

                c_pp = [0]
                c_pj = [0]
                a_tile = [0]
                n_cs = [0]
                e_tile = [0]
                piece_free = []

                def stream_piece(img, row):
                    i = c_pp[0]; c_pp[0] += 1
                    slot = wpb[i % 3]
                    if i >= 3:
                        fs, fv = piece_free[i - 3]
                        P.wait("gpsimd", fs, fv)
                    ds_wp.issue(P, "gpsimd", lambda h, slot=slot, img=img, row=row:
                                h.dma_start(out=slot[:],
                                            in_=A(img, row * 4096, [[4096, 128], [1, 4096]])))
                    return slot, i

                def piece_done():
                    piece_free.append((s_pjf, c_pj[0]))

                pjbanks = [ppj[0], ppj[1], pS[0], pS[1]]
                def next_pj():
                    u = c_pj[0]; c_pj[0] += 1
                    P.wait("tensor", s_pjf, u - 3)
                    return pjbanks[u % 4]

                def emit_attn_ln(p, w, part=None):
                    ctx["xlnT"] = xlnTs_a[w % 2]
                    for bi in range(2):
                        b = 2 * p + bi
                        tile_no = 2 * (p * DEV_NWIN + w) + bi
                        if part in (None, "math"):
                            P.wait("vector", s_pA, s_pA.n)
                        emit_ln_tile(P, ctx, at_src, b * T + w * WT, bi * 128,
                                     False, tile_no, part)

                for p in range(DEV_NPASS):
                    P.wait("vector", s_Sup, s_Sup.n)
                    P.op("vector", lambda h: h.memzero(S_all[:]))
                    emit_attn_ln(p, 0)
                    for w in range(DEV_NWIN):
                        xlnT = xlnTs_a[w % 2]
                        P.wait("tensor", s_atm, s_atm.n)
                        P.wait("tensor", s_orms, s_orms.n)
                        P.wait("tensor", s_Sup, s_Sup.n)
                        P.wait("tensor", s_xo, s_xo.n)
                        P.wait("vector", s_eps, s_eps.n)
                        P.wait("scalar", s_ep, s_ep.n)
                        P.wait("tensor", s_trc, s_trc.n)
                        for (img, dstT) in ((wqi_d, qT), (wki_d, kT)):
                            for fp in range(2):
                                slot, wn = stream_piece(img, fp * 128)
                                ds_wp.wait_done(P, "tensor", wn)
                                for fl in range(4):
                                    ft = fp * 4 + fl
                                    pb = next_pj()
                                    for dt in range(8):
                                        P.op("tensor", lambda h, dt=dt, fl=fl, slot=slot, pb=pb, xlnT=xlnT:
                                             h.matmul(pb[:, 0:256],
                                                      slot[:, dt * 512 + fl * 128:
                                                           dt * 512 + (fl + 1) * 128],
                                                      xlnT[:, dt * 256:(dt + 1) * 256],
                                                      start=(dt == 0), stop=(dt == 7)),
                                             inc=(s_pj, 1) if dt == 7 else None)
                                    if ft % 2 == 0:
                                        P.wait("vector", s_pj, s_pj.n)
                                        P.op("vector", lambda h, ft=ft, dstT=dstT, pb=pb:
                                             h.tensor_copy(dstT[:, ft * 256:(ft + 1) * 256],
                                                           pb[:, 0:256]), inc=(s_pjf, 1))
                                    else:
                                        P.wait("scalar", s_pj, s_pj.n)
                                        P.op("scalar", lambda h, ft=ft, dstT=dstT, pb=pb:
                                             h.activation(dstT[:, ft * 256:(ft + 1) * 256],
                                                          pb[:, 0:256], AF.Copy),
                                             inc=(s_pjf, 1))
                                piece_done()
                        for (img, dsts, act) in ((wvi_d, v_sb, False), (wgi_d, wgs, True)):
                            for fp in range(4):
                                slot, wn = stream_piece(img, fp * 128)
                                ds_wp.wait_done(P, "tensor", wn)
                                for bi in range(2):
                                    pb = next_pj()
                                    for dt in range(8):
                                        P.op("tensor", lambda h, dt=dt, bi=bi, slot=slot, pb=pb, xlnT=xlnT:
                                             h.matmul(pb[:],
                                                      xlnT[:, dt * 256 + bi * 128:
                                                           dt * 256 + bi * 128 + 128],
                                                      slot[:, dt * 512:(dt + 1) * 512],
                                                      start=(dt == 0), stop=(dt == 7)),
                                             inc=(s_pj, 1) if dt == 7 else None)
                                    if act:
                                        P.wait("scalar", s_pj, s_pj.n)
                                        P.op("scalar", lambda h, bi=bi, fp=fp, dsts=dsts, pb=pb:
                                             h.activation(dsts[bi][:, fp * 512:(fp + 1) * 512],
                                                          pb[:], AF.Silu), inc=(s_pjf, 1))
                                    else:
                                        P.wait("vector", s_pj, s_pj.n)
                                        P.op("vector", lambda h, bi=bi, fp=fp, dsts=dsts, pb=pb:
                                             h.tensor_copy(dsts[bi][:, fp * 512:(fp + 1) * 512],
                                                           pb[:]), inc=(s_pjf, 1))
                                piece_done()
                        pb = next_pj()
                        for dt in range(8):
                            P.op("tensor", lambda h, dt=dt, pb=pb, xlnT=xlnT: h.matmul(
                                pb[0:16, 0:256], gk1_sb[:, dt * 16:(dt + 1) * 16],
                                xlnT[:, dt * 256:(dt + 1) * 256],
                                start=(dt == 0), stop=(dt == 7)),
                                inc=(s_pj, 1) if dt == 7 else None)
                        P.wait("vector", s_pj, s_pj.n)
                        P.op("vector", lambda h, pb=pb: h.tensor_copy(tmpb[:], pb[0:16, 0:256]),
                             inc=(s_pjf, 1))
                        P.wait("tensor", s_pjf, s_pjf.n)
                        for ft in range(8):
                            pb = next_pj()
                            P.op("tensor", lambda h, ft=ft, pb=pb: h.matmul(
                                pb[:, 0:256], gk2_sb[:, ft * 128:(ft + 1) * 128], tmpb[:],
                                start=True, stop=True), inc=(s_pj, 1))
                            P.wait("scalar", s_pj, s_pj.n)
                            P.wait("scalar", s_gl, s_gl.n - 1)
                            P.op("scalar", lambda h, ft=ft, pb=pb: h.activation(
                                sglb[ft % 2][:], pb[:, 0:256], AF.Sigmoid,
                                bias=gkbT_sb[:, ft:ft + 1]), inc=(s_pjf, 1))
                            P.wait("scalar", s_glx, s_glx.n - 1)
                            P.op("scalar", lambda h, ft=ft: h.activation(
                                sglb2[ft % 2][:], sglb[ft % 2][:], AF.Ln), inc=(s_gl, 1))
                            P.wait("vector", s_gl, s_gl.n)
                            P.op("vector", lambda h, ft=ft: h.tensor_scalar(
                                gT[:, ft * 256:(ft + 1) * 256], sglb2[ft % 2][:], 1.0 / 16.0,
                                None, ALU.mult), inc=(s_glx, 1))
                        P.wait("tensor", s_pjf, s_pjf.n)
                        steps = [(bi, hh) for bi in range(2) for hh in range(H)]
                        base_n = n_cs[0]

                        def st_va(si):
                            bi, hh = steps[si]
                            n = base_n + si
                            par = n % NPB
                            tok0 = bi * 128
                            Bp = B_sb[par]
                            P.wait("vector", s_s1, n - (NPB - 1))
                            for fl in range(2):
                                off = (2 * hh + fl) * 256 + tok0
                                P.op("vector", lambda h, fl=fl, off=off, Bp=Bp:
                                     h.tensor_tensor_scan(
                                         Bp[:, fl * 128:fl * 128 + 128], ones64[:],
                                         gT[:, off:off + C], 0.0, ALU.mult, ALU.add),
                                     inc=(s_v1, 1) if fl == 1 else None)

                        def st_sc(si):
                            n = base_n + si
                            par = n % NPB
                            Bp = B_sb[par]; sc = scl[par]
                            P.wait("scalar", s_v1, n + 1)
                            P.wait("scalar", s_Sup, n - (NPB - 1))
                            P.wait("scalar", s_v2, n - (NPB - 1))
                            P.op("scalar", lambda h, Bp=Bp, sc=sc: h.activation(
                                sc[:, 0:2], A(Bp, 63, [[256, 128], [128, 2], [1, 1]]),
                                AF.Copy, scale=-1.0, bias=-LN16))
                            P.op("scalar", lambda h, Bp=Bp, sc=sc: h.activation(
                                sc[:, 2:4], A(Bp, 63, [[256, 128], [128, 2], [1, 1]]),
                                AF.Copy, scale=-1.0), inc=(s_asw, 1))
                            P.wait("scalar", s_asw, s_asw.n)
                            P.op("scalar", lambda h, Bp=Bp, sc=sc: h.activation(
                                sc[:, 4:6], A(Bp, 127, [[256, 128], [128, 2], [1, 1]]),
                                AF.Exp))
                            P.op("scalar", lambda h, Bp=Bp, sc=sc: h.activation(
                                sc[:, 6:8], A(Bp, 63, [[256, 128], [128, 2], [1, 1]]),
                                AF.Exp))
                            for fl in range(2):
                                P.op("scalar", lambda h, fl=fl, Bp=Bp, sc=sc, par=par:
                                     h.activation(Eq_sb[par][:, fl * 128:fl * 128 + 128],
                                                  Bp[:, fl * 128:fl * 128 + 128], AF.Exp,
                                                  bias=sc[:, fl:fl + 1]))
                                P.op("scalar", lambda h, fl=fl, Bp=Bp, par=par:
                                     h.activation(Ek_sb[par][:, fl * 128:fl * 128 + 128],
                                                  Bp[:, fl * 128:fl * 128 + 128], AF.Exp,
                                                  bias=Bp[:, fl * 128 + 63:fl * 128 + 64],
                                                  scale=-1.0))
                            P.wait("scalar", s_asw, s_asw.n)
                            P.op("scalar", lambda h, Bp=Bp, sc=sc: h.activation(
                                sc[:, 0:2], A(Bp, 127, [[256, 128], [128, 2], [1, 1]]),
                                AF.Exp, bias=sc[:, 2:3]), inc=(s_asw, 1))
                            P.op("scalar", lambda h, Bp=Bp, sc=sc: h.activation(
                                sc[:, 1:2], Bp[:, 255:256],
                                AF.Exp, bias=sc[:, 3:4]), inc=(s_s1, 1))

                        def st_vb(si):
                            bi, hh = steps[si]
                            n = base_n + si
                            par = n % NPB
                            parS = n % NPS
                            bh = bi * H + hh
                            tok0 = bi * 128
                            sc = scl[par]
                            P.wait("vector", s_s1, n + 1)
                            P.wait("vector", s_pO, n - (NPS - 1))
                            P.wait("vector", s_pS, n - (NPB - 1))
                            qoff = 2 * hh * 256 + tok0
                            P.op("vector", lambda h, qoff=qoff, par=par: h.tensor_mul(
                                A(qs_sb[par], 0, [[256, 128], [128, 2], [1, 128]]),
                                A(qT, qoff, [[2048, 128], [256, 2], [1, 128]]),
                                A(Eq_sb[par], 0, [[256, 128], [128, 2], [1, 128]])))
                            P.op("vector", lambda h, qoff=qoff, par=par: h.tensor_mul(
                                A(ks_sb[par], 0, [[256, 128], [128, 2], [1, 128]]),
                                A(kT, qoff, [[2048, 128], [256, 2], [1, 128]]),
                                A(Ek_sb[par], 0, [[256, 128], [128, 2], [1, 128]])),
                                inc=(s_avw, 1))
                            P.wait("vector", s_avw, s_avw.n)
                            for fl in range(2):
                                P.op("vector", lambda h, fl=fl, par=par, sc=sc:
                                     h.tensor_scalar(kdT_sb[par][:, fl * 128:fl * 128 + 128],
                                                     ks_sb[par][:, fl * 128:fl * 128 + 128],
                                                     sc[:, fl:fl + 1], None, ALU.mult))
                                P.op("vector", lambda h, fl=fl, parS=parS, bh=bh, sc=sc:
                                     h.tensor_scalar(
                                         Sbf[parS][:, fl * 512:(fl + 1) * 512],
                                         S_all[:, bh * 1024 + fl * 512:
                                               bh * 1024 + (fl + 1) * 512],
                                         sc[:, 6 + fl:7 + fl], None, ALU.mult),
                                     inc=(s_v2, 1) if fl == 1 else None)

                        def st_pe1(si):
                            n = base_n + si
                            par = n % NPB
                            kdo = (n % 2) * 256
                            P.wait("tensor", s_v2, n + 1)
                            P.wait("tensor", s_kdc, n)
                            P.wait("tensor", s_trc, s_trc.n)
                            P.wait("tensor", s_otc, s_otc.n)
                            for fl in range(2):
                                P.op("tensor", lambda h, fl=fl, par=par, kdo=kdo:
                                     h.transpose(
                                         ptr6[:, kdo + fl * 128:kdo + (fl + 1) * 128],
                                         kdT_sb[par][:, fl * 128:fl * 128 + 128],
                                         identb[:]), inc=(s_pkd, 1) if fl == 1 else None)
                            P.wait("scalar", s_pkd, s_pkd.n)
                            P.wait("scalar", s_pS, n - (NPB - 1))
                            P.op("scalar", lambda h, par=par, kdo=kdo: h.activation(
                                kd_sb[par][:], ptr6[:, kdo:kdo + 256], AF.Copy),
                                inc=(s_kdc, 1))
                            P.wait("tensor", s_atm, n)
                            for fl in range(2):
                                P.op("tensor", lambda h, fl=fl, par=par: h.matmul(
                                    pout[:, 0:128],
                                    ks_sb[par][:, fl * 128:fl * 128 + 128],
                                    qs_sb[par][:, fl * 128:fl * 128 + 128],
                                    start=(fl == 0), stop=(fl == 1)),
                                    inc=(s_pA, 1) if fl == 1 else None)
                            P.wait("vector", s_pA, s_pA.n)
                            P.wait("vector", s_pO, n - (NPB - 1))
                            P.op("vector", lambda h, par=par: h.tensor_mul(
                                ATm[par][:], pout[:, 0:128], maskU[:]),
                                inc=(s_atm, 1))

                        def st_pe2(si):
                            bi, hh = steps[si]
                            n = base_n + si
                            par = n % NPB
                            parS = n % NPS
                            par2 = n % 2
                            bh = bi * H + hh
                            sc = scl[par]
                            pSp = pS if par2 == 0 else ppj
                            P.wait("tensor", s_atm, n + 1)
                            P.wait("tensor", s_orms, n - 1)
                            P.op("tensor", lambda h, par=par, par2=par2, bi=bi, hh=hh:
                                 h.matmul(pO[par2][:], ATm[par][:],
                                          v_sb[bi][:, hh * 512:(hh + 1) * 512],
                                          start=True, stop=False))
                            for fl in range(2):
                                P.op("tensor", lambda h, fl=fl, par=par, parS=parS, par2=par2:
                                     h.matmul(pO[par2][:],
                                              qs_sb[par][:, fl * 128:fl * 128 + 128],
                                              Sbf[parS][:, fl * 512:(fl + 1) * 512],
                                              start=False, stop=(fl == 1)),
                                     inc=(s_pO, 1) if fl == 1 else None)
                            P.wait("tensor", s_kdc, n + 1)
                            P.wait("tensor", s_Sup, n - 1)
                            for fl in range(2):
                                P.op("tensor", lambda h, fl=fl, par=par, pSp=pSp, bi=bi, hh=hh:
                                     h.matmul(pSp[fl][:],
                                              kd_sb[par][:, fl * 128:(fl + 1) * 128],
                                              v_sb[bi][:, hh * 512:(hh + 1) * 512],
                                              start=True, stop=True),
                                     inc=(s_pS, 1) if fl == 1 else None)
                            P.wait("scalar", s_pO, n + 1)
                            P.op("scalar", lambda h, par2=par2, bi=bi, hh=hh:
                                 h.activation(
                                     o_raw[bi][:, hh * 512:(hh + 1) * 512],
                                     pO[par2][:], AF.Copy), inc=(s_orms, 1))
                            P.wait("vector", s_pS, n + 1)
                            for fl in range(2):
                                P.op("vector", lambda h, fl=fl, par=par, pSp=pSp, bh=bh, sc=sc:
                                     h.scalar_tensor_tensor(
                                         S_all[:, bh * 1024 + fl * 512:
                                               bh * 1024 + (fl + 1) * 512],
                                         S_all[:, bh * 1024 + fl * 512:
                                               bh * 1024 + (fl + 1) * 512],
                                         sc[:, 4 + fl:5 + fl],
                                         pSp[fl][:], ALU.mult, ALU.add),
                                     inc=(s_Sup, 1) if fl == 1 else None)

                        HB = 8
                        for hb0 in range(0, len(steps), HB):
                            batch = range(hb0, min(hb0 + HB, len(steps)))
                            for si in batch:
                                st_va(si)
                            for si in batch:
                                st_sc(si)
                            for si in batch:
                                st_vb(si)
                            for si in batch:
                                st_pe1(si)
                            for si in batch:
                                st_pe2(si)
                        n_cs[0] += len(steps)
                        if w + 1 < DEV_NWIN:
                            emit_attn_ln(p, w + 1, "math")
                        P.wait("vector", s_orms, s_orms.n)
                        P.wait("scalar", s_orms, s_orms.n)
                        for bi in range(2):
                            for half in range(2):
                                P.wait("scalar", s_epv, s_epv.n)
                                P.op("scalar", lambda h, bi=bi, half=half: h.activation(
                                    sq_scr[:], o_raw[bi][:, half * 1024:(half + 1) * 1024],
                                    AF.Square), inc=(s_eps, 1))
                                P.wait("vector", s_eps, s_eps.n)
                                for j in range(2):
                                    hh2 = half * 2 + j
                                    P.op("vector", lambda h, hh2=hh2, j=j: h.reduce_sum(
                                        mh[:, hh2:hh2 + 1],
                                        sq_scr[:, j * 512:(j + 1) * 512], axis=AX.X),
                                        inc=(s_avw, 1) if j == 0 else (s_epv, 1))
                            P.wait("vector", s_epv, s_epv.n)
                            P.op("vector", lambda h: h.tensor_scalar(
                                mh[:, 4:8], mh[:, 0:4], 1.0 / 512.0, EPS,
                                ALU.mult, ALU.add), inc=(s_epv, 1))
                            P.wait("scalar", s_epv, s_epv.n)
                            P.op("scalar", lambda h: h.activation(mh[:, 8:12], mh[:, 4:8],
                                                                  AF.Sqrt), inc=(s_eps, 1))
                            P.wait("vector", s_eps, s_eps.n)
                            P.op("vector", lambda h: h.reciprocal(out=mh[:, 12:16],
                                                                  in_=mh[:, 8:12]))
                            for hh2 in range(4):
                                P.op("vector", lambda h, hh2=hh2, bi=bi: h.tensor_scalar(
                                    ofg[bi][:, hh2 * 512:(hh2 + 1) * 512],
                                    o_raw[bi][:, hh2 * 512:(hh2 + 1) * 512],
                                    mh[:, 12 + hh2:13 + hh2], None, ALU.mult))
                            P.op("vector", lambda h, bi=bi: h.tensor_mul(
                                A(ofg[bi], 0, [[2048, 128], [512, 4], [1, 512]]),
                                A(ofg[bi], 0, [[2048, 128], [512, 4], [1, 512]]),
                                A(rmsw_sb, 0, [[512, 128], [0, 4], [1, 512]])))
                            P.op("vector", lambda h, bi=bi: h.tensor_mul(
                                ofg[bi][:], ofg[bi][:], wgs[bi][:]), inc=(s_ep, 1))
                        for bi in range(2):
                            b = 2 * p + bi
                            k = e_tile[0]; e_tile[0] += 1
                            for rnd in range(2):
                                P.wait("tensor", s_ep, s_ep.n)
                                P.wait("tensor", s_otc, s_otc.n)
                                P.wait("tensor", s_kdc, s_kdc.n)
                                P.wait("tensor", s_trc, s_trc.n)
                                for j in range(8):
                                    fvt = rnd * 8 + j
                                    P.op("tensor", lambda h, fvt=fvt, j=j, bi=bi: h.transpose(
                                        ptr6[:, j * 128:(j + 1) * 128],
                                        ofg[bi][:, fvt * 128:(fvt + 1) * 128], identb[:]),
                                        inc=(s_pot, 1) if j == 7 else None)
                                P.wait("vector", s_pot, s_pot.n)
                                P.wait("vector", s_pout, s_pout.n)
                                P.op("vector", lambda h, rnd=rnd: h.tensor_copy(
                                    oT_sb[:, rnd * 1024:(rnd + 1) * 1024], ptr6[:]),
                                    inc=(s_otc, 1))
                            P.wait("sync", s_xo, s_xo.n)
                            iax = ds_ax.issue(P, "sync", lambda h, b=b, w=w, bi=bi:
                                              h.dma_start(out=x1b[bi][:],
                                                          in_=A(at_src, (b * T + w * WT) * D,
                                                                [[D, 128], [1, D]])))
                            P.wait("tensor", s_otc, s_otc.n)
                            P.wait("tensor", s_xo, s_xo.n)
                            P.wait("tensor", s_atm, s_atm.n)
                            P.wait("tensor", s_Sup, s_Sup.n)
                            pout2 = pS[0]
                            for wp in range(4):
                                slot, wn = stream_piece(woi_d, wp * 128)
                                ds_wp.wait_done(P, "tensor", wn)
                                for fl in range(4):
                                    fvt = wp * 4 + fl
                                    for fch in range(2):
                                        P.op("tensor", lambda h, fvt=fvt, fl=fl, fch=fch, slot=slot:
                                             h.matmul((pout if fch == 0 else pout2)[:],
                                                      oT_sb[:, fvt * 128:(fvt + 1) * 128],
                                                      slot[:, fl * 1024 + fch * 512:
                                                           fl * 1024 + fch * 512 + 512],
                                                      start=(fvt == 0), stop=(fvt == 15)),
                                             inc=(s_pout, 1) if (fl == 3 and fch == 1) else None)
                                piece_free.append((s_pout, s_pout.n))
                            for fch in range(2):
                                P.wait("vector", s_pout, s_pout.n)
                                ds_ax.wait_done(P, "vector", iax)
                                ds_ast.wait_done(P, "vector", k - 2)
                                P.op("vector", lambda h, bi=bi, fch=fch: h.tensor_add(
                                    xob[bi][:, fch * 512:(fch + 1) * 512],
                                    (pout if fch == 0 else pout2)[:],
                                    x1b[bi][:, fch * 512:(fch + 1) * 512]),
                                    inc=(s_xo, 1))
                            P.wait("sync", s_xo, s_xo.n)
                            ds_ast.issue(P, "sync", lambda h, b=b, w=w, bi=bi:
                                         h.dma_start(out=A(at_dst, (b * T + w * WT) * D,
                                                           [[D, 128], [1, D]]),
                                                     in_=xob[bi][:]))
                        if w + 1 < DEV_NWIN:
                            emit_attn_ln(p, w + 1, "tr")
                P.flush(nc)
            gates = stream_gates(ds_ast)

        if do_f2:
            f2_st = ffn_stage("f", "f2", f2_src, y_d, gates)
            gates = stream_gates(f2_st)

        P = Prog()
        for gs, gv in gates:
            P.wait("sync", gs, gv)
        P.flush(nc)

    return nc


def make_inputs(L0_attn_Wq, L0_attn_Wk, L0_attn_Wv, L0_attn_Wg,
                L0_attn_gk_w1, L0_attn_gk_w2, L0_attn_gk_b, L0_attn_Wo,
                L0_attn_rms_w, L0_ffn_router_w, L0_ffn_w13, L0_ffn_w2):
    f32 = np.float32
    ac = np.ascontiguousarray
    consts = dict(
        rwT=ac(np.asarray(L0_ffn_router_w, f32).T),
        Wq=ac(np.asarray(L0_attn_Wq, f32)),
        Wk=ac(np.asarray(L0_attn_Wk, f32)),
        Wv=ac(np.asarray(L0_attn_Wv, f32)),
        Wg=ac(np.asarray(L0_attn_Wg, f32)),
        Wo=ac(np.asarray(L0_attn_Wo, f32)),
        gk1=ac(np.asarray(L0_attn_gk_w1, f32)),
        gk2=ac(np.asarray(L0_attn_gk_w2, f32)),
        gkbT=ac(np.asarray(L0_attn_gk_b, f32).reshape(8, 128).T),
        rmswB=ac(np.broadcast_to(np.asarray(L0_attn_rms_w, f32), (128, 512))),
        w13=ac(np.asarray(L0_ffn_w13, f32)),
        w2=ac(np.asarray(L0_ffn_w2, f32)),
        identb=np.eye(128, dtype=ml_dtypes.bfloat16),
        identf=np.eye(128, dtype=f32),
        maskU=ac(np.triu(np.ones((128, 128), f32))),
        ones64=np.ones((128, 128), f32),
    )
    oh = np.zeros((8, 1024), ml_dtypes.bfloat16)
    for e in range(8):
        oh[e, e * 128:(e + 1) * 128] = 1
    consts["oneh"] = oh
    return consts


def run_full(x_full, consts, do_f1=True, do_attn=True, do_f2=True):
    x = np.ascontiguousarray(np.asarray(x_full, np.float32)).reshape(32 * T, D)
    nc = build_program(do_f1=do_f1, do_attn=do_attn, do_f2=do_f2)
    in_maps = []
    for c in range(NCORES):
        m = dict(consts)
        m["x"] = np.ascontiguousarray(x[c * NTOK:(c + 1) * NTOK])
        in_maps.append(m)
    res = run_bass_kernel_spmd(nc, in_maps[:DEV_NCORES], core_ids=list(range(DEV_NCORES)))
    out = np.concatenate([r["y"] for r in res.results], axis=0)
    if DEV_NCORES < NCORES:
        pad = np.zeros((NTOK * (NCORES - DEV_NCORES), D), np.float32)
        out = np.concatenate([out, pad], axis=0)
    return out.reshape(32, T, D)


def kernel(L0_attn_x, L0_attn_Wq, L0_attn_Wk, L0_attn_Wv, L0_attn_Wg,
           L0_attn_gk_w1, L0_attn_gk_w2, L0_attn_gk_b, L0_attn_Wo, L0_attn_rms_w,
           L0_ffn_router_w, L0_ffn_w13, L0_ffn_w2):
    consts = make_inputs(L0_attn_Wq, L0_attn_Wk, L0_attn_Wv, L0_attn_Wg,
                         L0_attn_gk_w1, L0_attn_gk_w2, L0_attn_gk_b, L0_attn_Wo,
                         L0_attn_rms_w, L0_ffn_router_w, L0_ffn_w13, L0_ffn_w2)
    return run_full(L0_attn_x, consts)
